# version 25
# baseline (speedup 1.0000x reference)
import numpy as np
import ml_dtypes
import concourse.bass as bass
import concourse.mybir as mybir
from concourse.bass_utils import run_bass_kernel_spmd

F32 = mybir.dt.float32
BF16 = mybir.dt.bfloat16
U32 = mybir.dt.uint32
AF = mybir.ActivationFunctionType
ALU = mybir.AluOpType

P = 128
MAGIC = float(1.5 * 2 ** 23)
EPS_Q = 1e-05
EPS_LN = 1e-05
NCORES = 8
AX = mybir.AxisListType.X


class Buf:
    __slots__ = ("w", "readers", "key")

    def __init__(self, key=None):
        self.w = None
        self.readers = []
        self.key = key


class Builder:

    ENGS = ("t", "v", "s", "p", "y")
    SECT = {"t": "tensor", "v": "vector", "s": "scalar", "p": "gpsimd",
            "y": "sync"}

    def __init__(self, nc):
        self.nc = nc
        self.stack = []
        self.sem = {}
        self.semn = {}
        for name in ("cT", "cV", "cS", "cP", "dS", "dF", "dG", "cc"):
            self.sem[name] = self.enter(nc.semaphore(name))
            self.semn[name] = 0
        self.known = {e: {} for e in self.ENGS}
        self.phase_ops = {e: [] for e in self.ENGS}
        self.bufs = {}
        self.nblocks = 0
        self.pending_reads = {e: [] for e in self.ENGS}

    def enter(self, cm):
        v = cm.__enter__()
        self.stack.append(cm)
        return v

    def close(self):
        for cm in reversed(self.stack):
            cm.__exit__(None, None, None)
        self.stack = []

    def buf(self, *key):
        b = self.bufs.get(key)
        if b is None:
            b = self.bufs[key] = Buf(key)
        return b

    def dma_sem(self, queue, key):
        name = "D_" + queue + "_" + "_".join(str(k) for k in key)
        if name not in self.sem:
            self.sem[name] = self.enter(self.nc.semaphore(name))
            self.semn[name] = 0
        return name

    def op(self, e, emit, reads=(), writes=(), sem=None, inc=0):
        waits = {}

        def add(s, v):
            if s.startswith("D_"):
                v = self.semn[s]
            if waits.get(s, -1) < v:
                waits[s] = v

        for b in reads:
            if b.w is not None:
                add(*b.w)
        for b in writes:
            deps = list(b.readers)
            if b.w is not None:
                deps.append(b.w)
            for s, v in deps:
                add(s, v)
        known = self.known[e]
        fw = {}
        for s, v in waits.items():
            if known.get(s, 0) < v:
                fw[s] = v
                known[s] = v
        mile = None
        if sem is not None and inc:
            self.semn[sem] += inc
            mile = (sem, self.semn[sem])
        if mile is not None:
            for b in writes:
                b.w = mile
                b.readers = []
            for b in reads:
                b.readers.append(mile)
            for b in self.pending_reads[e]:
                b.readers.append(mile)
            self.pending_reads[e] = []
        else:
            self.pending_reads[e].extend(reads)
        self.phase_ops[e].append((fw, emit, sem if mile else None,
                                  inc if mile else 0))
        return mile

    def raw_wait(self, e, sem, val):
        known = self.known[e]
        if known.get(sem, 0) >= val:
            return
        known[sem] = val
        self.phase_ops[e].append(({sem: val}, None, None, 0))

    def flush(self):
        pass

    def finalize(self):
        nc = self.nc
        pending = {e: ops for e, ops in self.phase_ops.items() if ops}
        self.phase_ops = {e: [] for e in self.ENGS}
        if not pending:
            return
        block = nc.Block("main")
        block.__enter__()

        def runner(ops):
            def body(eng):
                for waits, emit, sem, inc in ops:
                    for s, v in waits.items():
                        eng.wait_ge(self.sem[s], v)
                    if emit is not None:
                        instr = emit(eng)
                        if sem is not None:
                            instr.then_inc(self.sem[sem], inc)
            return body

        for e, ops in pending.items():
            getattr(block, self.SECT[e])(runner(ops))
        block.__exit__(None, None, None)


def build_program(cfg):
    L, D, N, R, I = cfg["L"], cfg["D"], cfg["N"], cfg["R"], cfg["I"]
    NCH = 16
    LC = L // NCH
    DH = D // 2
    NDT = DH // P
    NDTF = D // P
    LTC = LC // P
    TOKT = (L // 2) // P
    NG = 4
    TPG = TOKT // NG
    IG = min(1024, I)
    NIG = I // IG
    IGD = min(1024, I)
    NIGD = I // IGD
    NIT = IGD // P
    HH = min(1024, D)
    NHH = D // HH
    RN = R + 2 * N
    BBLK = 0
    WS_G = D // NCORES
    WS_D = I // NCORES
    PW = min(2048, max(D, 512))
    BBLK = TPG * P
    NRC = BBLK // LC

    nc = bass.Bass("TRN2", target_bir_lowering=False)

    xT_d = nc.dram_tensor("xT", [D, L], F32, kind="ExternalInput")
    wxT_d = nc.dram_tensor("wxT", [D, RN], F32, kind="ExternalInput")
    wdtT_d = nc.dram_tensor("wdtT", [R, DH], F32, kind="ExternalInput")
    bdt_d = nc.dram_tensor("bdt", [P, NDT], F32, kind="ExternalInput")
    alog_d = nc.dram_tensor("alog", [P, NDT * N], F32, kind="ExternalInput")
    dvec_d = nc.dram_tensor("dvec", [P, NDT], F32, kind="ExternalInput")
    wgT_d = nc.dram_tensor("wgTs", [WS_G, I], F32, kind="ExternalInput")
    wuT_d = nc.dram_tensor("wuTs", [WS_G, I], F32, kind="ExternalInput")
    wdT_d = nc.dram_tensor("wdTs", [WS_D, D], F32, kind="ExternalInput")
    idf_d = nc.dram_tensor("idf", [P, P], F32, kind="ExternalInput")
    idb_d = nc.dram_tensor("idb", [P, P], BF16, kind="ExternalInput")
    off_d = nc.dram_tensor("off", [1, 1], U32, kind="ExternalInput")
    cval_d = nc.dram_tensor("cval", [P, 4], F32, kind="ExternalInput")
    out_d = nc.dram_tensor("out", [L // 2, D], F32, kind="ExternalOutput")

    xproj_dram = nc.dram_tensor("xproj_dram", [RN, L], F32)
    contrib = nc.dram_tensor("contrib", [L, DH], F32)
    gathered = nc.dram_tensor("gathered", [2 * L, DH], F32)
    wq_c = {"g": nc.dram_tensor("wqc_g", [WS_G, I], BF16),
            "u": nc.dram_tensor("wqc_u", [WS_G, I], BF16),
            "d": nc.dram_tensor("wqc_d", [WS_D, D], BF16)}
    wq_full = {"g": nc.dram_tensor("wqf_g", [D, I], BF16, addr_space="Shared"),
               "u": nc.dram_tensor("wqf_u", [D, I], BF16, addr_space="Shared"),
               "d": nc.dram_tensor("wqf_d", [I, D], BF16, addr_space="Shared")}
    red_in = nc.dram_tensor("red_in", [1, 4], F32)
    red_out = nc.dram_tensor("red_out", [1, 4], F32, addr_space="Shared")
    sc_dram = nc.dram_tensor("sc_dram", [1, 16], F32)
    myg = nc.dram_tensor("myg", [L, DH], F32)
    xln_spill = nc.dram_tensor("xln_spill", [L // 2, D], F32)
    t_spill = nc.dram_tensor("t_spill", [L // 2, I], F32)

    bd = Builder(nc)
    T = lambda *a: bd.enter(nc.sbuf_tensor(*a))
    PSUM = lambda *a: bd.enter(nc.psum_tensor(*a))

    wxT_sb = T("wxT_sb", [P, NDTF * RN], F32)
    wdtT_sb = T("wdtT_sb", [R, DH], F32)
    bdt_sb = T("bdt_sb", [P, NDT], F32)
    a_sb = T("a_sb", [P, NDT * N], F32)
    dp1_sb = T("dp1_sb", [P, NDT], F32)
    idf_sb = T("idf_sb", [P, P], F32)
    idb_sb = T("idb_sb", [P, P], BF16)
    off_sb = T("off_sb", [1, 1], U32)
    ws_sb = T("ws_sb", [P, 8], F32)
    cval_sb = T("cval_sb", [P, 4], F32)
    sc_sb = T("sc_sb", [1, 16], F32)
    cols_sb = T("cols_sb", [P, 88], F32)

    xt_sb = T("xt_sb", [P, 3 * LC], F32)
    dt_sb = T("dt_sb", [P, NDT * LC], F32)
    u_sb = T("u_sb", [P, NDT * LC], F32)
    acc_sb = T("acc_sb", [P, NDT * LC], F32)
    BD = min(4, NDT)
    NB = NDT // BD
    da_sb = T("da_sb", [P, 2 * BD * LC], F32)
    db_sb = T("db_sb", [P, BD * LC], F32)
    h_sb = T("h_sb", [P, BD * LC], F32)
    bb_sb = T("bb_sb", [P, 2 * LC], F32)
    cb_sb = T("cb_sb", [P, 2 * LC], F32)
    hst_sb = T("hst_sb", [P, NDT * N], F32)
    r1t_sb = T("r1t_sb", [P, DH], F32)
    xp_sb = T("xp_sb", [RN, 2 * LC], F32)
    spa_sb = T("spa_sb", [P, max(LC, 16)], F32)

    SLW = NDTF * IG
    slab_sb = T("slab_sb", [P, 2 * SLW], BF16)
    UD = 4 * D
    unionA = T("unionA", [P, UD], F32)
    xq_sb = T("xq_sb", [P, max(D, PW, IGD)], BF16)
    xqT_sb = T("xqT_sb", [P, max(TPG * NDTF * P, PW, 2 * NIT * P)], BF16)
    gu_sb = T("gu_sb", [P, max(4 * IG, D)], F32)
    tr_sb = T("tr_sb", [P, max(2 * IGD, D)], F32)
    tq_sb = xq_sb
    tqT_sb = xqT_sb

    ps_mx = PSUM("ps_mx", [P, max(LC, 512)], F32)
    ps_tpb = PSUM("ps_tpb", [P, 1024], BF16)
    ps_g = PSUM("ps_g", [P, IG], F32)
    ps_u = PSUM("ps_u", [P, IG], F32)
    ps_dn = PSUM("ps_dn", [P, HH], F32)
    ps_tp = ps_mx
    ps_tpF = ps_tpb[:, 0:1024].bitcast(F32)

    mkb = bd.buf
    CONST = mkb("const")
    regs = {}

    def dma(e, dst, src, reads, writes, skey=None):
        sem = bd.dma_sem(e, skey if skey is not None else writes[0].key)
        return bd.op(e, lambda eng: eng.dma_start(out=dst, in_=src),
                     reads=reads, writes=writes, sem=sem, inc=16)

    def vop(emit, reads, writes):
        return bd.op("v", emit, reads, writes, sem="cV", inc=1)

    def sop(emit, reads, writes, inc=1):
        return bd.op("s", emit, reads, writes, sem="cS", inc=inc)

    def top(emit, reads, writes, inc=1):
        return bd.op("t", emit, reads, writes, sem="cT", inc=inc)

    def pop_(emit, reads, writes):
        return bd.op("p", emit, reads, writes, sem="cP", inc=1)

    dma("y", wxT_sb[:, :].rearrange("p (t f) -> p t f", t=NDTF),
        wxT_d.ap().rearrange("(t p) f -> p t f", p=P), [], [CONST])
    dma("y", wdtT_sb[:, :], wdtT_d[:, :], [], [CONST])
    dma("y", bdt_sb[:, :], bdt_d[:, :], [], [CONST])
    dma("y", a_sb[:, :], alog_d[:, :], [], [mkb("A")])
    dma("y", dp1_sb[:, :], dvec_d[:, :], [], [mkb("dvec")])
    dma("y", idf_sb[:, :], idf_d[:, :], [], [CONST])
    dma("y", idb_sb[:, :], idb_d[:, :], [], [CONST])
    dma("y", off_sb[:, :], off_d[:, :], [], [mkb("off")])
    dma("y", cval_sb[:, :], cval_d[:, :], [], [CONST])
    for _cv, _ci in ((0.0, 0), (1.0, 1), (MAGIC, 2), (-MAGIC, 3)):
        nc.const_aps.register(F32, _cv, cval_sb[:, _ci:_ci + 1])
    _csem = bd.dma_sem("y", CONST.key)
    _cm = bd.semn[_csem]
    for _e in Builder.ENGS:
        bd.raw_wait(_e, _csem, _cm)
    sop(lambda e: e.activation(a_sb[:, :], a_sb[:, :], AF.Exp),
        [mkb("A")], [mkb("A")])
    vop(lambda e: e.tensor_scalar_mul(a_sb[:, :], a_sb[:, :], -1.0),
        [mkb("A")], [mkb("A")])
    vop(lambda e: e.tensor_scalar_add(dp1_sb[:, :], dp1_sb[:, :], 1.0),
        [mkb("dvec")], [mkb("dvec")])
    vop(lambda e: e.memset(hst_sb[:, :], 0.0), [],
        [mkb("hst", c) for c in range(NDT * N)])
    vop(lambda e: e.memset(cols_sb[:, 80:88], 0.0), [], [mkb("wps")])
    vop(lambda e: e.memset(sc_sb[0:1, :], 0.0), [], [mkb("sc")])
    vop(lambda e: e.memset(cols_sb[:, 80:81], 1.0), [mkb("wps")],
        [mkb("wps")])
    bd.flush()

    def weight_prep():
        shard_specs = [("g", wgT_d, WS_G, I), ("u", wuT_d, WS_G, I),
                       ("d", wdT_d, WS_D, D)]

        def prep_tiles(wd_, rows, cols):
            idx = 0
            for tr in range((rows + P - 1) // P):
                pr = min(P, rows - tr * P)
                for tc in range(0, cols, PW):
                    w = min(PW, cols - tc)
                    yield idx, tr, pr, tc, w
                    idx += 1

        for mi, (m, wd_, rows, cols) in enumerate(shard_specs):
            for idx, tr, pr, tc, w in prep_tiles(wd_, rows, cols):
                slot = idx % 2
                tile = unionA[:pr, slot * PW: slot * PW + w]
                tb = mkb("U", slot)
                dma("y", tile, wd_[tr * P: tr * P + pr, tc: tc + w], [], [tb])
                sop(lambda e, t=tile, sl=slot, pr_=pr: e.activation(
                    t, t, AF.Abs, accum_out=cols_sb[:pr_, 84 + sl: 85 + sl]),
                    [tb], [tb, mkb("wps")])
                vop(lambda e, sl=slot, pr_=pr, mi_=mi: e.tensor_tensor(
                    cols_sb[:pr_, 81 + mi_: 82 + mi_],
                    cols_sb[:pr_, 81 + mi_: 82 + mi_],
                    cols_sb[:pr_, 84 + sl: 85 + sl], ALU.add),
                    [mkb("wps")], [mkb("wps")])
            yield
            top(lambda e, mi_=mi: e.matmul(
                ps_mx[0:1, mi_: mi_ + 1], cols_sb[:, 80:81],
                cols_sb[:, 81 + mi_: 82 + mi_],
                start=True, stop=True),
                [mkb("wps")], [mkb("ps_mx")], inc=1)
            sop(lambda e, mi_=mi: e.copy(sc_sb[0:1, mi_: mi_ + 1],
                                         ps_mx[0:1, mi_: mi_ + 1]),
                [mkb("ps_mx")], [mkb("sc")])
            yield
        yield
        dma("y", red_in[0:1, 0:3], sc_sb[0:1, 0:3], [mkb("sc")], [mkb("red_in")])
        bd.op("p", lambda e: e.collective_compute(
            "AllReduce", ALU.add, replica_groups=[list(range(NCORES))],
            ins=[red_in[0:1, 0:3].opt()], outs=[red_out[0:1, 0:3].opt()]),
            reads=[mkb("red_in")], writes=[mkb("red_out")],
            sem=bd.dma_sem("cc", ("red",)), inc=1)
        dma("y", sc_sb[0:1, 4:7], red_out[0:1, 0:3], [mkb("red_out")],
            [mkb("sc")])
        for i, (m, _, rows_, cols_) in enumerate(shard_specs):
            numel = float(rows_ * NCORES * cols_)
            vop(lambda e, i_=i, s_=1.0 / numel: e.tensor_scalar(
                sc_sb[0:1, 8 + i_: 9 + i_], sc_sb[0:1, 4 + i_: 5 + i_],
                s_, EPS_Q, ALU.mult, ALU.add), [mkb("sc")], [mkb("sc")])
        vop(lambda e: e.reciprocal(sc_sb[0:1, 12:15], sc_sb[0:1, 8:11]),
            [mkb("sc")], [mkb("sc")])
        dma("y", sc_dram[0:1, :], sc_sb[0:1, :], [mkb("sc")], [mkb("scd")])
        dma("y", ws_sb[:, 0:3], sc_dram[0:1, 8:11].partition_broadcast(P),
            [mkb("scd")], [mkb("ws")])
        dma("y", ws_sb[:, 3:6], sc_dram[0:1, 12:15].partition_broadcast(P),
            [mkb("scd")], [mkb("ws")])
        yield
        bd.flush()

        for m, wd_, rows, cols in shard_specs:
            sidx = {"g": 3, "u": 4, "d": 5}[m]
            for idx, tr, pr, tc, w in prep_tiles(wd_, rows, cols):
                slot = idx % 2
                tile = unionA[:pr, slot * PW: slot * PW + w]
                tb = mkb("U", slot)
                dma("y", tile, wd_[tr * P: tr * P + pr, tc: tc + w], [], [tb])
                sop(lambda e, t=tile, pr_=pr, si=sidx: e.activation(
                    t, t, AF.Identity, scale=ws_sb[:pr_, si: si + 1], bias=MAGIC),
                    [tb, mkb("ws")], [tb])
                sop(lambda e, t=tile: e.activation(t, t, AF.Identity, bias=-MAGIC),
                    [tb], [tb])
                obf = (xq_sb if slot == 0 else xqT_sb)[:pr, 0:w]
                ob = mkb("wpo", slot)
                vop(lambda e, t=tile, o=obf: e.tensor_scalar(
                    o, t, 1.0, -1.0, ALU.min, ALU.max), [tb], [ob])
                dma("y", wq_c[m][tr * P: tr * P + pr, tc: tc + w], obf,
                    [ob], [mkb("wqc", m)])
        for m, _, _, _ in shard_specs:
            bd.op("p", lambda e, m_=m: e.collective_compute(
                "AllGather", ALU.bypass, replica_groups=[list(range(NCORES))],
                ins=[wq_c[m_].ap().opt()], outs=[wq_full[m_].ap().opt()]),
                reads=[mkb("wqc", m)], writes=[mkb("wqf", m)],
                sem=bd.dma_sem("cc", ("wq",)), inc=1)
            yield
        bd.flush()


    NMM_LC = (LC + 511) // 512
    for k in range(NCH):
        for dtf in range(NDTF):
            slot = dtf % 2
            tile = da_sb[:, slot * LC: (slot + 1) * LC]
            tb = mkb("da", slot)
            dma("y", tile, xT_d[dtf * P: (dtf + 1) * P,
                                k * LC: (k + 1) * LC], [], [tb])
            for q in range(NMM_LC):
                c0, c1 = q * 512, min(LC, (q + 1) * 512)
                top(lambda e, t=tile, dt_=dtf, c0_=c0, c1_=c1: e.matmul(
                    ps_mx[0:RN, c0_:c1_],
                    wxT_sb[:, dt_ * RN: (dt_ + 1) * RN], t[:, c0_:c1_],
                    start=(dt_ == 0), stop=(dt_ == NDTF - 1)),
                    [CONST, tb], [mkb("ps_mx")], inc=1)
        slot = k % 2
        xps = xp_sb[0:RN, slot * LC: (slot + 1) * LC]
        xb = mkb("xp", slot)
        sop(lambda e, d=xps: e.copy(d, ps_mx[0:RN, 0:LC]),
            [mkb("ps_mx")], [xb])
        dma("y", xproj_dram[0:RN, k * LC: (k + 1) * LC], xps, [xb],
            [mkb("xpd", k)], skey=("xpd",))
    bd.flush()


    pend_cc = []

    def emit_pend_cc():
        while pend_cc:
            pend_cc.pop(0)()

    def mixer_chunk(k, pull):
        slot = k % 2
        xps = xp_sb[0:RN, slot * LC: (slot + 1) * LC]
        xb = mkb("xp", slot)
        dma("p", xps, xproj_dram[0:RN, k * LC: (k + 1) * LC],
            [mkb("xpd", k)], [xb])
        for dt in range(NDT):
            dst = dt_sb[:, dt * LC: (dt + 1) * LC]
            dtb = mkb("dt", dt)
            for q in range(NMM_LC):
                c0, c1 = q * 512, min(LC, (q + 1) * 512)
                top(lambda e, dt_=dt, c0_=c0, c1_=c1, x_=xps: e.matmul(
                    ps_mx[:, c0_:c1_], wdtT_sb[:, dt_ * P: (dt_ + 1) * P],
                    x_[0:R, c0_:c1_], start=True, stop=True),
                    [CONST, xb], [mkb("ps_mx")],
                    inc=1 if q == NMM_LC - 1 else 0)
            sop(lambda e, d=dst, dt_=dt: e.activation(
                d, ps_mx[:, 0:LC], AF.Identity,
                bias=bdt_sb[:, dt_: dt_ + 1]),
                [mkb("ps_mx"), CONST], [dtb])
            sab = mkb("spl")
            sop(lambda e, d=dst: e.activation(
                spa_sb[:, 0:LC], d, AF.Abs), [dtb, mkb("spa")], [sab])
            sop(lambda e: e.activation(spa_sb[:, 0:LC], spa_sb[:, 0:LC],
                                       AF.Exp, scale=-1.0), [sab], [sab])
            sop(lambda e: e.activation(spa_sb[:, 0:LC], spa_sb[:, 0:LC],
                                       AF.Ln, bias=1.0), [sab], [sab])
            vop(lambda e, d=dst: e.tensor_scalar_max(d, d, 0.0),
                [dtb], [dtb])
            vop(lambda e, d=dst: e.tensor_tensor(
                d, d, spa_sb[:, 0:LC], ALU.add), [dtb, sab], [dtb])
            xts = dt % 3
            dma("p", xt_sb[:, xts * LC: (xts + 1) * LC],
                xT_d[dt * P: (dt + 1) * P, k * LC: (k + 1) * LC],
                [], [mkb("xt", xts)])
            vop(lambda e, dt_=dt, d=dst, s_=xts: e.tensor_tensor(
                u_sb[:, dt_ * LC: (dt_ + 1) * LC], d,
                xt_sb[:, s_ * LC: (s_ + 1) * LC], ALU.mult),
                [dtb, mkb("xt", xts)], [mkb("u", dt)])
        for n in range(N):
            if n == 2:
                emit_pend_cc()
            bslot = n % 2
            bbuf = bb_sb[:, bslot * LC: (bslot + 1) * LC]
            cbuf = cb_sb[:, bslot * LC: (bslot + 1) * LC]
            bB = mkb("bb", bslot)
            bC = mkb("cb", bslot)
            dma("p", bbuf,
                xproj_dram[R + n: R + n + 1,
                           k * LC: (k + 1) * LC].partition_broadcast(P),
                [mkb("xpd", k)], [bB])
            dma("p", cbuf,
                xproj_dram[R + N + n: R + N + n + 1,
                           k * LC: (k + 1) * LC].partition_broadcast(P),
                [mkb("xpd", k)], [bC])
            bbrep = bbuf.unsqueeze(1).broadcast_to([P, BD, LC])
            cbrep = cbuf.unsqueeze(1).broadcast_to([P, BD, LC])
            for hb in range(NB):
                s = (n * NB + hb) % 2
                dav = da_sb[:, s * BD * LC: (s + 1) * BD * LC]
                bda = mkb("da", s)
                dts = list(range(hb * BD, (hb + 1) * BD))
                cols = [dt * N + n for dt in dts]
                for j, dt in enumerate(dts):
                    sop(lambda e, j=j, dt=dt, n=n, dv=dav: e.activation(
                        dv[:, j * LC: (j + 1) * LC],
                        dt_sb[:, dt * LC: (dt + 1) * LC], AF.Exp,
                        scale=a_sb[:, dt * N + n: dt * N + n + 1]),
                        [mkb("dt", dt), mkb("A")], [bda],
                        inc=1 if j == BD - 1 else 0)
                dbb = mkb("db")
                uv = u_sb[:, hb * BD * LC: (hb + 1) * BD * LC]
                vop(lambda e, uv=uv, bp=bbrep: e.tensor_tensor(
                    db_sb[:, 0:BD * LC].rearrange(
                        "p (r f) -> p r f", r=BD),
                    uv.rearrange("p (r f) -> p r f", r=BD),
                    bp, ALU.mult),
                    [mkb("u", dt) for dt in dts] + [bB], [dbb])
                hcols = [mkb("hst", c) for c in cols]
                scr1 = cols_sb[:, 72:72 + BD]
                scr2 = cols_sb[:, 76:76 + BD]
                b1 = mkb("scr1")
                b2_ = mkb("scr2")
                vop(lambda e, dv=dav, s1=scr1: e.tensor_copy(
                    s1, dv[:, 0:BD * LC:LC]), [bda], [b1])
                vop(lambda e, dv=dav: e.tensor_scalar_mul(
                    dv[:, 0:BD * LC:LC], dv[:, 0:BD * LC:LC], 0.0),
                    [bda, b1], [bda])
                hstv = hst_sb[:, hb * BD * N + n:
                              hb * BD * N + n + (BD - 1) * N + 1: N]
                vop(lambda e, hv_=hstv, s1=scr1, s2=scr2: e.tensor_tensor(
                    s2, s1, hv_, ALU.mult), [b1] + hcols, [b2_])
                vop(lambda e, s2=scr2: e.tensor_tensor(
                    db_sb[:, 0:BD * LC:LC], db_sb[:, 0:BD * LC:LC],
                    s2, ALU.add), [b2_, dbb], [dbb])
                hbuf = mkb("h")
                vop(lambda e, dv=dav: e.tensor_tensor_scan(
                    h_sb[:, 0:BD * LC], dv, db_sb[:, 0:BD * LC],
                    cval_sb[:, 0:1], ALU.mult, ALU.add),
                    [bda, dbb, CONST], [hbuf])
                vop(lambda e, hv_=hstv: e.tensor_copy(
                    hv_, h_sb[:, LC - 1:BD * LC:LC]),
                    [hbuf] + hcols, hcols)
                accv = acc_sb[:, hb * BD * LC: (hb + 1) * BD * LC]
                atags = [mkb("acc", dt) for dt in dts]
                if n == 0:
                    vop(lambda e, av=accv, cp=cbrep: e.tensor_tensor(
                        av.rearrange("p (r f) -> p r f", r=BD),
                        h_sb[:, 0:BD * LC].rearrange(
                            "p (r f) -> p r f", r=BD),
                        cp, ALU.mult), [hbuf, bC], atags)
                else:
                    vop(lambda e, cp=cbrep: e.tensor_tensor(
                        db_sb[:, 0:BD * LC].rearrange(
                            "p (r f) -> p r f", r=BD),
                        h_sb[:, 0:BD * LC].rearrange(
                            "p (r f) -> p r f", r=BD),
                        cp, ALU.mult), [hbuf, bC, dbb], [dbb])
                    vop(lambda e, av=accv: e.tensor_tensor(
                        av, av, db_sb[:, 0:BD * LC], ALU.add),
                        [dbb] + atags, atags)
            if n >= 2:
                pull()
        for dt in range(NDT):
            xts = dt % 3
            dma("p", xt_sb[:, xts * LC: (xts + 1) * LC],
                xT_d[dt * P: (dt + 1) * P, k * LC: (k + 1) * LC],
                [], [mkb("xt", xts)])
            vop(lambda e, dt_=dt, s_=xts: e.scalar_tensor_tensor(
                u_sb[:, dt_ * LC: (dt_ + 1) * LC],
                xt_sb[:, s_ * LC: (s_ + 1) * LC],
                dp1_sb[:, dt_: dt_ + 1],
                acc_sb[:, dt_ * LC: (dt_ + 1) * LC], ALU.mult, ALU.add),
                [mkb("xt", xts), mkb("dvec"), mkb("acc", dt)],
                [mkb("u", dt)])
        for lt in range(LTC):
            rview = r1t_sb[:, 0:DH]
            rtb = mkb("r1t", 0)
            for grp in range((NDT + 3) // 4):
                d0, d1 = grp * 4, min(NDT, grp * 4 + 4)
                for dt in range(d0, d1):
                    top(lambda e, dt_=dt, lt_=lt, g0=d0: e.transpose(
                        ps_tpF[:, (dt_ - g0) * P: (dt_ - g0 + 1) * P],
                        u_sb[:, dt_ * LC + lt_ * P:
                             dt_ * LC + (lt_ + 1) * P], idf_sb[:, :]),
                        [mkb("u", dt), CONST], [mkb("ps_tpb")],
                        inc=1 if dt == d1 - 1 else 0)
                sop(lambda e, g0=d0, g1=d1, rv=rview: e.copy(
                    rv[:, g0 * P: g1 * P], ps_tpF[:, 0:(g1 - g0) * P]),
                    [mkb("ps_tpb")], [rtb])
            gate_reads = [rtb] + ([mkb("gath", k - 2)] if k >= 2 else [])
            dma("y", contrib[k * LC + lt * P: k * LC + (lt + 1) * P, :],
                rview, gate_reads, [mkb("contrib", k)],
                skey=("contrib", k % 2))
            pull()

        def emit_cc(k_=k):
            bd.op("p", lambda e: e.collective_compute(
                "AllGather", ALU.bypass,
                replica_groups=[[2 * i, 2 * i + 1]
                                for i in range(NCORES // 2)],
                ins=[contrib[k_ * LC: (k_ + 1) * LC, :].opt()],
                outs=[gathered[2 * k_ * LC: 2 * (k_ + 1) * LC, :].opt()]),
                reads=[mkb("contrib", k_)], writes=[mkb("gath", k_)],
                sem=bd.dma_sem("cc", ("g", k_ // (2 * NRC))), inc=1)
        pend_cc.append(emit_cc)

    def Ub(i):
        return mkb("U", i)

    r1tok = unionA[:, 0:D]
    xlnb = unionA[:, 2 * D:3 * D]
    s2c = lambda j: cols_sb[:, 40 + j: 41 + j]
    sxdc = lambda j: cols_sb[:, 56 + j: 57 + j]

    NMM_IG = (IG + 511) // 512
    NMM_HH = (HH + 511) // 512
    NH1 = max(NIT // 2, 1)

    def gslot(jj):
        return gu_sb[:, (jj % 2) * IG: (jj % 2 + 1) * IG]

    def uslot(jj):
        return gu_sb[:, (2 + jj % 2) * IG: (3 + jj % 2) * IG]

    def ffn_group(g):
        gv = gathered[g * 4 * BBLK: (g + 1) * 4 * BBLK, :].rearrange(
            "(a rc h rr) d -> a rc h rr d", a=2, rc=NRC, h=2)
        dstv = myg[g * 2 * BBLK: (g + 1) * 2 * BBLK, :].rearrange(
            "(rc h rr) d -> rc h rr d", rc=NRC, h=2)

        pieces_done = set()

        def ensure_piece(rc_):
            if rc_ in pieces_done:
                return
            pieces_done.add(rc_)
            base = 2 * g * NRC
            needed = [mkb("gath", base + rc_ + 2 * a_) for a_ in range(2)]
            while any(b.w is None for b in needed):
                yield

            def em(e, gv_=gv, dstv_=dstv, rc__=rc_):
                ro = regs.get("y")
                if ro is None:
                    ro = regs["y"] = e.alloc_register("off_reg")
                e.reg_load(ro, off_sb[0:1, 0:1])
                ofs = e.snap(ro)
                return e.dma_start(out=dstv_[rc__],
                                   in_=gv_[bass.ds(ofs, 1), rc__])
            mgb = mkb("myg", g, rc_)
            bd.op("y", em,
                  reads=[mkb("off")] + needed,
                  writes=[mgb], sem=bd.dma_sem("y", mgb.key), inc=16)
            for _ in range(8):
                yield
        for jj in range(TPG):
            j = g * TPG + jj
            r = jj * P
            rc, rr = r // LC, r % LC
            yield from ensure_piece(rc)
            rb = Ub(0)
            for hh in range(2):
                st = g * 2 * BBLK + rc * 2 * LC + hh * LC + rr
                dma("p", r1tok[:, hh * DH: (hh + 1) * DH],
                    myg[st: st + P, :], [mkb("myg", g, rc)], [rb])
            sumb = mkb("c_sum")
            vop(lambda e: e.reduce_sum(cols_sb[:, 0:1], r1tok, axis=AX),
                [rb], [sumb])
            vop(lambda e: e.tensor_scalar_mul(
                cols_sb[:, 0:1], cols_sb[:, 0:1], -1.0 / D),
                [sumb], [sumb])
            yield
            vb = mkb("c_var")
            sop(lambda e: e.activation(xlnb, r1tok, AF.Square,
                                       accum_out=cols_sb[:, 1:2]),
                [rb], [Ub(2), vb])
            vop(lambda e: e.scalar_tensor_tensor(
                cols_sb[:, 6:7], cols_sb[:, 0:1], -1.0,
                cols_sb[:, 0:1], ALU.mult, ALU.mult),
                [sumb], [mkb("c_msq")])
            yield
            vop(lambda e: e.tensor_scalar(cols_sb[:, 1:2], cols_sb[:, 1:2],
                                          1.0 / D, EPS_LN, ALU.mult,
                                          ALU.add), [vb], [vb])
            vop(lambda e: e.tensor_tensor(cols_sb[:, 1:2], cols_sb[:, 1:2],
                                          cols_sb[:, 6:7], ALU.add),
                [vb, mkb("c_msq")], [vb])
            yield
            sop(lambda e: e.activation(cols_sb[:, 1:2], cols_sb[:, 1:2],
                                       AF.Ln), [vb], [vb])
            sop(lambda e: e.activation(cols_sb[:, 1:2], cols_sb[:, 1:2],
                                       AF.Exp, scale=-0.5), [vb], [vb])
            yield
            b2 = mkb("c_b2")
            vop(lambda e: e.tensor_tensor(cols_sb[:, 2:3], cols_sb[:, 0:1],
                                          cols_sb[:, 1:2], ALU.mult),
                [sumb, vb], [b2])
            xb = Ub(2)
            sop(lambda e: e.activation(xlnb, r1tok, AF.Identity,
                                       scale=cols_sb[:, 1:2],
                                       bias=cols_sb[:, 2:3]),
                [rb, vb, b2], [xb])
            dma("y", xln_spill[j * P: (j + 1) * P, :], xlnb, [xb],
                [mkb("xlnsp", j)], skey=("xlnsp", g % 2, jj))
            yield
            am = mkb("c_amax")
            vop(lambda e: e.reduce_max(cols_sb[:, 3:4], xlnb, axis=AX,
                                       apply_absolute_value=True),
                [xb], [am])
            vop(lambda e: e.tensor_scalar(cols_sb[:, 3:4], cols_sb[:, 3:4],
                                          1.0 / 127.0, EPS_Q, ALU.mult,
                                          ALU.add), [am], [am])
            vop(lambda e: e.reciprocal(cols_sb[:, 4:5], cols_sb[:, 3:4]),
                [am], [mkb("c_sinv")])
            yield
            vop(lambda e, j_=j: e.tensor_tensor(
                cols_sb[:, 8 + 2 * j_: 9 + 2 * j_], cols_sb[:, 3:4],
                ws_sb[:, 0:1], ALU.mult), [am, mkb("ws")], [mkb("sxg", j)])
            vop(lambda e, j_=j: e.tensor_tensor(
                cols_sb[:, 9 + 2 * j_: 10 + 2 * j_], cols_sb[:, 3:4],
                ws_sb[:, 1:2], ALU.mult), [am, mkb("ws")], [mkb("sxu", j)])
            sop(lambda e: e.activation(r1tok, xlnb, AF.Identity,
                                       scale=cols_sb[:, 4:5], bias=MAGIC),
                [xb, mkb("c_sinv")], [Ub(0)])
            qb = mkb("xqb")
            sop(lambda e: e.activation(xq_sb[:, 0:D], r1tok, AF.Identity,
                                       bias=-MAGIC), [Ub(0)], [qb])
            yield
            for grp in range((NDTF + 7) // 8):
                d0, d1 = grp * 8, min(NDTF, grp * 8 + 8)
                for dt in range(d0, d1):
                    top(lambda e, dt_=dt, g0=d0: e.transpose(
                        ps_tpb[:, (dt_ - g0) * P: (dt_ - g0 + 1) * P],
                        xq_sb[:, dt_ * P: (dt_ + 1) * P], idb_sb[:, :]),
                        [qb, CONST], [mkb("ps_tpb")],
                        inc=1 if dt == d1 - 1 else 0)
                vop(lambda e, g0=d0, g1=d1, jj_=jj: e.tensor_copy(
                    xqT_sb[:, (jj_ * NDTF + g0) * P:
                           (jj_ * NDTF + g1) * P],
                    ps_tpb[:, 0:(g1 - g0) * P]),
                    [mkb("ps_tpb")], [mkb("xqT", jj)])
                yield

        def emit_gate_up(ig, jj):
            j = g * TPG + jj
            for half, psum in ((0, ps_g), (1, ps_u)):
                sl = slab_sb[:, half * SLW: (half + 1) * SLW]
                slb = mkb("slab", half)
                pb = mkb("ps_gu", half)
                for dt in range(NDTF):
                    for q in range(NMM_IG):
                        c0, c1 = q * 512, min(IG, (q + 1) * 512)
                        top(lambda e, dt_=dt, c0_=c0, c1_=c1, s_=sl,
                            ps_=psum, jj_=jj: e.matmul(
                                ps_[:, c0_:c1_],
                                xqT_sb[:, (jj_ * NDTF + dt_) * P:
                                       (jj_ * NDTF + dt_ + 1) * P],
                                s_[:, dt_ * IG + c0_: dt_ * IG + c1_],
                                start=(dt_ == 0), stop=(dt_ == NDTF - 1)),
                            [mkb("xqT", jj), slb], [pb],
                            inc=1 if (dt == NDTF - 1 and q == NMM_IG - 1)
                            else 0)
                go = gslot(jj) if half == 0 else uslot(jj)
                gb = mkb("gu", half, jj % 2)
                scol = cols_sb[:, 8 + 2 * j + half: 9 + 2 * j + half]
                sdep = mkb("sxg", j) if half == 0 else mkb("sxu", j)
                fn = AF.Silu if half == 0 else AF.Identity
                sop(lambda e, go_=go, f=fn, c=scol, ps_=psum:
                    e.activation(go_, ps_[:, 0:IG], f, scale=c),
                    [pb, sdep], [gb])

        def emit_t(ig, jj):
            j = g * TPG + jj
            tb = mkb("gu", 0, jj % 2)
            vop(lambda e, jj_=jj: e.tensor_tensor(
                gslot(jj_), gslot(jj_), uslot(jj_), ALU.mult),
                [mkb("gu", 0, jj % 2), mkb("gu", 1, jj % 2)], [tb])
            mxb = mkb("c_tmax")
            vop(lambda e, jj_=jj: e.reduce_max(
                cols_sb[:, 5:6], gslot(jj_), axis=AX,
                apply_absolute_value=True), [tb], [mxb])
            if ig == 0:
                vop(lambda e, j_=j: e.tensor_copy(s2c(j_), cols_sb[:, 5:6]),
                    [mxb], [mkb("s2", j)])
            else:
                vop(lambda e, j_=j: e.tensor_tensor(
                    s2c(j_), s2c(j_), cols_sb[:, 5:6], ALU.max),
                    [mxb, mkb("s2", j)], [mkb("s2", j)])
            dma("y", t_spill[j * P: (j + 1) * P, ig * IG: (ig + 1) * IG],
                gslot(jj), [tb], [mkb("tsp", j)], skey=("tsp", g % 2, jj))

        prev = None
        for ig in range(NIG):
            for half, name in ((0, "g"), (1, "u")):
                sl = slab_sb[:, half * SLW: (half + 1) * SLW]
                dma("y", sl.rearrange("p (t f) -> p t f", t=NDTF),
                    wq_full[name].ap()[0:D,
                                       ig * IG:(ig + 1) * IG].rearrange(
                        "(t p) f -> p t f", p=P),
                    [mkb("wqf", name)], [mkb("slab", half)])
            for jj in range(TPG):
                emit_gate_up(ig, jj)
                if prev is not None:
                    emit_t(*prev)
                prev = (ig, jj)
                yield
        emit_t(*prev)
        for jj in range(TPG):
            j = g * TPG + jj
            vop(lambda e, j_=j: e.tensor_scalar(
                s2c(j_), s2c(j_), 1.0 / 127.0, EPS_Q, ALU.mult, ALU.add),
                [mkb("s2", j)], [mkb("s2", j)])
            vop(lambda e, j_=j: e.tensor_tensor(
                sxdc(j_), s2c(j_), ws_sb[:, 2:3], ALU.mult),
                [mkb("s2", j), mkb("ws")], [mkb("sxd", j)])
            vop(lambda e, j_=j: e.reciprocal(s2c(j_), s2c(j_)),
                [mkb("s2", j)], [mkb("s2", j)])
            yield

        def emit_tq_prep(igd, jj):
            j = g * TPG + jj
            trs = jj % 2
            trv = tr_sb[:, trs * IGD: (trs + 1) * IGD]
            trb = mkb("tr", trs)
            dma("p", trv, t_spill[j * P:(j + 1) * P,
                                  igd * IGD:(igd + 1) * IGD],
                [mkb("tsp", j)], [trb], skey=("trl", g % 2, jj % 2))
            sop(lambda e, t=trv, j_=j: e.activation(
                t, t, AF.Identity, scale=s2c(j_), bias=MAGIC),
                [trb, mkb("s2", j)], [trb])
            qb = mkb("xqb")
            sop(lambda e, t=trv: e.activation(
                tq_sb[:, 0:IGD], t, AF.Identity, bias=-MAGIC),
                [trb], [qb])
            yield
            tqv = tqT_sb[:, (jj % 2) * NIT * P: (jj % 2 + 1) * NIT * P]
            tqtags = [mkb("xqT", r)
                      for r in range(min(TPG, -(-2 * NIT) // NDTF))]
            for grp in range((NIT + 7) // 8):
                d0, d1 = grp * 8, min(NIT, grp * 8 + 8)
                for it in range(d0, d1):
                    top(lambda e, it_=it, g0=d0: e.transpose(
                        ps_tpb[:, (it_ - g0) * P: (it_ - g0 + 1) * P],
                        tq_sb[:, it_ * P: (it_ + 1) * P], idb_sb[:, :]),
                        [qb, CONST], [mkb("ps_tpb")],
                        inc=1 if it == d1 - 1 else 0)
                vop(lambda e, g0=d0, g1=d1, tv=tqv: e.tensor_copy(
                    tv[:, g0 * P: g1 * P], ps_tpb[:, 0:(g1 - g0) * P]),
                    [mkb("ps_tpb")], tqtags)
                yield

        def emit_dn_mms(igd, jj, hhx, pslot):
            tqv = tqT_sb[:, (jj % 2) * NIT * P: (jj % 2 + 1) * NIT * P]
            tqb = mkb("xqT", 0)
            tqtags = [mkb("xqT", r)
                      for r in range(min(TPG, -(-2 * NIT) // NDTF))]
            psum, pbk = ((ps_dn, mkb("ps_dn")) if pslot == 0
                         else (ps_g, mkb("ps_gu", 0)))
            for it in range(NIT):
                hseg = 0 if it < NH1 else 1
                t0 = 0 if hseg == 0 else NH1
                t1 = min(NH1, NIT) if hseg == 0 else NIT
                sl = slab_sb[:, t0 * D: t1 * D]
                for q in range(NMM_HH):
                    c0, c1 = q * 512, min(HH, (q + 1) * 512)
                    top(lambda e, it_=it, c0_=c0, c1_=c1, h_=hhx,
                        t0_=t0, s_=sl, ps_=psum, tv=tqv: e.matmul(
                            ps_[:, c0_:c1_],
                            tv[:, it_ * P: (it_ + 1) * P],
                            s_[:, (it_ - t0_) * D + h_ * HH + c0_:
                               (it_ - t0_) * D + h_ * HH + c1_],
                            start=(it_ == 0), stop=(it_ == NIT - 1)),
                        tqtags + [mkb("slab", hseg)], [pbk],
                        inc=1 if (it == NIT - 1 and q == NMM_HH - 1)
                        else 0)

        def emit_dn_add(igd, jj, hhx, pslot):
            psum, pbk = ((ps_dn, mkb("ps_dn")) if pslot == 0
                         else (ps_g, mkb("ps_gu", 0)))
            dst = unionA[:, jj * D + hhx * HH: jj * D + (hhx + 1) * HH]
            dnb = Ub(jj)
            if igd == 0:
                vop(lambda e, d=dst, ps_=psum: e.tensor_copy(
                    d, ps_[:, 0:HH]), [pbk], [dnb])
            else:
                vop(lambda e, d=dst, ps_=psum: e.tensor_tensor(
                    d, d, ps_[:, 0:HH], ALU.add), [pbk, dnb], [dnb])

        assert NHH == 1 or HH * NHH == D
        prevc = None
        for igd in range(NIGD):
            for hseg, (t0, t1) in enumerate(((0, min(NH1, NIT)),
                                             (NH1, NIT))):
                if t1 <= t0:
                    continue
                sl = slab_sb[:, t0 * D: t1 * D]
                dma("y", sl.rearrange("p (t f) -> p t f", t=t1 - t0),
                    wq_full["d"].ap()[igd * IGD + t0 * P:
                                      igd * IGD + t1 * P, :].rearrange(
                        "(t p) f -> p t f", p=P),
                    [mkb("wqf", "d")], [mkb("slab", hseg)])
            for jj in range(TPG):
                yield from emit_tq_prep(igd, jj)
                for hhx in range(NHH):
                    pslot = (igd * TPG * NHH + jj * NHH + hhx) % 2
                    emit_dn_mms(igd, jj, hhx, pslot)
                    if prevc is not None:
                        emit_dn_add(*prevc)
                    prevc = (igd, jj, hhx, pslot)
                    yield
        emit_dn_add(*prevc)
        yield

        GUB = [mkb("gu", 0, 0), mkb("gu", 0, 1), mkb("gu", 1, 0),
               mkb("gu", 1, 1)]
        for jj in range(TPG):
            j = g * TPG + jj
            xv = gu_sb[:, 0:D]
            xb2 = mkb("xln2")
            dma("p", xv, xln_spill[j * P: (j + 1) * P, :],
                [mkb("xlnsp", j)] + GUB, GUB + [xb2], skey=("xln2",))
            r2b = mkb("tr", 0)
            bd.op("v", lambda e, jj_=jj, j_=j: e.scalar_tensor_tensor(
                tr_sb[:, 0:D], unionA[:, jj_ * D: (jj_ + 1) * D],
                sxdc(j_), xv, ALU.mult, ALU.add),
                reads=[Ub(jj), mkb("sxd", j), xb2, mkb("tr", 1)],
                writes=[r2b, mkb("tr", 1)], sem="cV", inc=1)
            r2v = tr_sb[:, 0:D]
            sumb = mkb("c_sum")
            vop(lambda e: e.reduce_sum(cols_sb[:, 0:1], r2v, axis=AX),
                [r2b], [sumb])
            vop(lambda e: e.tensor_scalar_mul(
                cols_sb[:, 0:1], cols_sb[:, 0:1], -1.0 / D),
                [sumb], [sumb])
            yield
            vb = mkb("c_var")
            sop(lambda e, jj_=jj: e.activation(
                unionA[:, jj_ * D: (jj_ + 1) * D], r2v, AF.Square,
                accum_out=cols_sb[:, 1:2]),
                [r2b, Ub(jj)], [Ub(jj), vb])
            vop(lambda e: e.scalar_tensor_tensor(
                cols_sb[:, 6:7], cols_sb[:, 0:1], -1.0, cols_sb[:, 0:1],
                ALU.mult, ALU.mult), [sumb], [mkb("c_msq")])
            yield
            vop(lambda e: e.tensor_scalar(cols_sb[:, 1:2], cols_sb[:, 1:2],
                                          1.0 / D, EPS_LN, ALU.mult,
                                          ALU.add), [vb], [vb])
            vop(lambda e: e.tensor_tensor(cols_sb[:, 1:2], cols_sb[:, 1:2],
                                          cols_sb[:, 6:7], ALU.add),
                [vb, mkb("c_msq")], [vb])
            yield
            sop(lambda e: e.activation(cols_sb[:, 1:2], cols_sb[:, 1:2],
                                       AF.Ln), [vb], [vb])
            sop(lambda e: e.activation(cols_sb[:, 1:2], cols_sb[:, 1:2],
                                       AF.Exp, scale=-0.5), [vb], [vb])
            yield
            b2 = mkb("c_b2")
            vop(lambda e: e.tensor_tensor(cols_sb[:, 2:3], cols_sb[:, 0:1],
                                          cols_sb[:, 1:2], ALU.mult),
                [sumb, vb], [b2])
            ob = mkb("outp")
            bd.op("s", lambda e: e.activation(
                gu_sb[:, 0:D], tr_sb[:, 0:D], AF.Identity,
                scale=cols_sb[:, 1:2], bias=cols_sb[:, 2:3]),
                reads=[r2b, vb, b2, xb2] + GUB, writes=GUB + [ob],
                sem="cS", inc=1)
            dma("y", out_d[j * P: (j + 1) * P, :], gu_sb[:, 0:D],
                [ob] + GUB, [mkb("outd")])
            yield

    active = [weight_prep()]

    def pull():
        budget = 6 if len(active) > 1 else 4
        while active and budget > 0:
            try:
                next(active[0])
                budget -= 1
            except StopIteration:
                active.pop(0)

    for k in range(NCH):
        mixer_chunk(k, pull)
        if (k + 2) % (2 * NRC) == 0 and (k + 2) // (2 * NRC) <= NG:
            active.append(ffn_group((k + 2) // (2 * NRC) - 1))
    emit_pend_cc()
    while active:
        try:
            next(active[0])
        except StopIteration:
            active.pop(0)

    for name, tot in list(bd.semn.items()):
        if name.startswith("D_") and tot > 0:
            bd.raw_wait("y", name, tot)
    bd.finalize()
    bd.close()
    return nc


_CACHE = {}


def _get_program(key, cfg):
    if key not in _CACHE:
        _CACHE[key] = build_program(cfg)
    return _CACHE[key]


def kernel(L0_attn_x, L0_attn_W_x, L0_attn_W_dt, L0_attn_b_dt, L0_attn_A_log,
           L0_attn_D, L0_ffn_w_gate, L0_ffn_w_up, L0_ffn_w_down):
    x = np.ascontiguousarray(np.asarray(L0_attn_x, np.float32))
    W_x = np.asarray(L0_attn_W_x, np.float32)
    W_dt = np.asarray(L0_attn_W_dt, np.float32)
    b_dt = np.asarray(L0_attn_b_dt, np.float32)
    A_log = np.asarray(L0_attn_A_log, np.float32)
    Dv = np.asarray(L0_attn_D, np.float32)
    wg = np.asarray(L0_ffn_w_gate, np.float32)
    wu = np.asarray(L0_ffn_w_up, np.float32)
    wd = np.asarray(L0_ffn_w_down, np.float32)

    B, L, D = x.shape
    N = A_log.shape[1]
    R = W_dt.shape[1]
    I = wg.shape[0]
    nc = _get_program((L, D, N, R, I), dict(L=L, D=D, N=N, R=R, I=I))

    DH = D // 2
    NDT = DH // P
    WS_G = D // NCORES
    WS_D = I // NCORES
    LC = L // 8
    wgT = np.ascontiguousarray(wg.T)
    wuT = np.ascontiguousarray(wu.T)
    wdT = np.ascontiguousarray(wd.T)
    WxT = np.ascontiguousarray(W_x.T)
    WdtT = np.ascontiguousarray(W_dt.T)
    idf = np.eye(P, dtype=np.float32)
    idb = np.eye(P).astype(ml_dtypes.bfloat16)
    cvals = np.tile(np.array([0.0, 1.0, MAGIC, -MAGIC], np.float32), (P, 1))

    in_maps = []
    for c in range(NCORES):
        b, par = c // 2, c % 2
        my = slice(par * DH, (par + 1) * DH)
        perm = np.r_[np.arange(par * DH, (par + 1) * DH),
                     np.arange((1 - par) * DH, (2 - par) * DH)]
        col = lambda v: np.ascontiguousarray(v[my].reshape(NDT, P).T)
        in_maps.append(dict(
            xT=np.ascontiguousarray(x[b].T[perm]),
            wxT=np.ascontiguousarray(WxT[perm]),
            wdtT=np.ascontiguousarray(WdtT[:, my]),
            bdt=col(b_dt),
            alog=np.ascontiguousarray(
                A_log[my].reshape(NDT, P, N).transpose(1, 0, 2).reshape(
                    P, NDT * N)),
            dvec=col(Dv),
            wgTs=np.ascontiguousarray(wgT[c * WS_G:(c + 1) * WS_G]),
            wuTs=np.ascontiguousarray(wuT[c * WS_G:(c + 1) * WS_G]),
            wdTs=np.ascontiguousarray(wdT[c * WS_D:(c + 1) * WS_D]),
            idf=idf, idb=idb,
            cval=cvals,
            off=np.array([[par]], dtype=np.uint32),
        ))

    res = run_bass_kernel_spmd(nc, in_maps, core_ids=list(range(NCORES)))

    out = np.empty((B, L, D), np.float32)
    LH = L // 2
    for c in range(NCORES):
        b, par = c // 2, c % 2
        oc = np.asarray(res.results[c]["out"]).reshape(LH, D)
        for t in range(LH // LC):
            gl = (2 * t + par) * LC
            out[b, gl:gl + LC] = oc[t * LC:(t + 1) * LC]
    return out


# version 26
# speedup vs baseline: 1.0097x; 1.0097x over previous
import numpy as np
import ml_dtypes
import concourse.bass as bass
import concourse.mybir as mybir
from concourse.bass_utils import run_bass_kernel_spmd

F32 = mybir.dt.float32
BF16 = mybir.dt.bfloat16
U32 = mybir.dt.uint32
AF = mybir.ActivationFunctionType
ALU = mybir.AluOpType

P = 128
MAGIC = float(1.5 * 2 ** 23)
EPS_Q = 1e-05
EPS_LN = 1e-05
NCORES = 8
AX = mybir.AxisListType.X


class Buf:
    __slots__ = ("w", "readers", "key")

    def __init__(self, key=None):
        self.w = None
        self.readers = []
        self.key = key


class Builder:

    ENGS = ("t", "v", "s", "p", "y")
    SECT = {"t": "tensor", "v": "vector", "s": "scalar", "p": "gpsimd",
            "y": "sync"}

    def __init__(self, nc):
        self.nc = nc
        self.stack = []
        self.sem = {}
        self.semn = {}
        for name in ("cT", "cV", "cS", "cP", "dS", "dF", "dG", "cc"):
            self.sem[name] = self.enter(nc.semaphore(name))
            self.semn[name] = 0
        self.known = {e: {} for e in self.ENGS}
        self.phase_ops = {e: [] for e in self.ENGS}
        self.bufs = {}
        self.nblocks = 0
        self.pending_reads = {e: [] for e in self.ENGS}

    def enter(self, cm):
        v = cm.__enter__()
        self.stack.append(cm)
        return v

    def close(self):
        for cm in reversed(self.stack):
            cm.__exit__(None, None, None)
        self.stack = []

    def buf(self, *key):
        b = self.bufs.get(key)
        if b is None:
            b = self.bufs[key] = Buf(key)
        return b

    def dma_sem(self, queue, key):
        name = "D_" + queue + "_" + "_".join(str(k) for k in key)
        if name not in self.sem:
            self.sem[name] = self.enter(self.nc.semaphore(name))
            self.semn[name] = 0
        return name

    def op(self, e, emit, reads=(), writes=(), sem=None, inc=0):
        waits = {}

        def add(s, v):
            if s.startswith("D_"):
                v = self.semn[s]
            if waits.get(s, -1) < v:
                waits[s] = v

        for b in reads:
            if b.w is not None:
                add(*b.w)
        for b in writes:
            deps = list(b.readers)
            if b.w is not None:
                deps.append(b.w)
            for s, v in deps:
                add(s, v)
        known = self.known[e]
        fw = {}
        for s, v in waits.items():
            if known.get(s, 0) < v:
                fw[s] = v
                known[s] = v
        mile = None
        if sem is not None and inc:
            self.semn[sem] += inc
            mile = (sem, self.semn[sem])
        if mile is not None:
            for b in writes:
                b.w = mile
                b.readers = []
            for b in reads:
                b.readers.append(mile)
            for b in self.pending_reads[e]:
                b.readers.append(mile)
            self.pending_reads[e] = []
        else:
            self.pending_reads[e].extend(reads)
        self.phase_ops[e].append((fw, emit, sem if mile else None,
                                  inc if mile else 0))
        return mile

    def raw_wait(self, e, sem, val):
        known = self.known[e]
        if known.get(sem, 0) >= val:
            return
        known[sem] = val
        self.phase_ops[e].append(({sem: val}, None, None, 0))

    def flush(self):
        pass

    def finalize(self):
        nc = self.nc
        pending = {e: ops for e, ops in self.phase_ops.items() if ops}
        self.phase_ops = {e: [] for e in self.ENGS}
        if not pending:
            return
        block = nc.Block("main")
        block.__enter__()

        def runner(ops):
            def body(eng):
                for waits, emit, sem, inc in ops:
                    for s, v in waits.items():
                        eng.wait_ge(self.sem[s], v)
                    if emit is not None:
                        instr = emit(eng)
                        if sem is not None:
                            instr.then_inc(self.sem[sem], inc)
            return body

        for e, ops in pending.items():
            getattr(block, self.SECT[e])(runner(ops))
        block.__exit__(None, None, None)


def build_program(cfg):
    L, D, N, R, I = cfg["L"], cfg["D"], cfg["N"], cfg["R"], cfg["I"]
    NCH = 16
    LC = L // NCH
    DH = D // 2
    NDT = DH // P
    NDTF = D // P
    LTC = LC // P
    TOKT = (L // 2) // P
    NG = 4
    TPG = TOKT // NG
    IG = min(1024, I)
    NIG = I // IG
    IGD = min(1024, I)
    NIGD = I // IGD
    NIT = IGD // P
    HH = min(1024, D)
    NHH = D // HH
    RN = R + 2 * N
    BBLK = 0
    WS_G = D // NCORES
    WS_D = I // NCORES
    PW = min(2048, max(D, 512))
    BBLK = TPG * P
    NRC = BBLK // LC

    nc = bass.Bass("TRN2", target_bir_lowering=False)

    xT_d = nc.dram_tensor("xT", [D, L], F32, kind="ExternalInput")
    wxT_d = nc.dram_tensor("wxT", [D, RN], F32, kind="ExternalInput")
    wdtT_d = nc.dram_tensor("wdtT", [R, DH], F32, kind="ExternalInput")
    bdt_d = nc.dram_tensor("bdt", [P, NDT], F32, kind="ExternalInput")
    alog_d = nc.dram_tensor("alog", [P, NDT * N], F32, kind="ExternalInput")
    dvec_d = nc.dram_tensor("dvec", [P, NDT], F32, kind="ExternalInput")
    wgT_d = nc.dram_tensor("wgTs", [WS_G, I], F32, kind="ExternalInput")
    wuT_d = nc.dram_tensor("wuTs", [WS_G, I], F32, kind="ExternalInput")
    wdT_d = nc.dram_tensor("wdTs", [WS_D, D], F32, kind="ExternalInput")
    idf_d = nc.dram_tensor("idf", [P, P], F32, kind="ExternalInput")
    idb_d = nc.dram_tensor("idb", [P, P], BF16, kind="ExternalInput")
    off_d = nc.dram_tensor("off", [1, 1], U32, kind="ExternalInput")
    cval_d = nc.dram_tensor("cval", [P, 4], F32, kind="ExternalInput")
    out_d = nc.dram_tensor("out", [L // 2, D], F32, kind="ExternalOutput")

    xproj_dram = nc.dram_tensor("xproj_dram", [RN, L], F32)
    contrib = nc.dram_tensor("contrib", [L, DH], F32)
    gathered = nc.dram_tensor("gathered", [2 * L, DH], F32)
    wq_c = {"g": nc.dram_tensor("wqc_g", [WS_G, I], BF16),
            "u": nc.dram_tensor("wqc_u", [WS_G, I], BF16),
            "d": nc.dram_tensor("wqc_d", [WS_D, D], BF16)}
    wq_full = {"g": nc.dram_tensor("wqf_g", [D, I], BF16, addr_space="Shared"),
               "u": nc.dram_tensor("wqf_u", [D, I], BF16, addr_space="Shared"),
               "d": nc.dram_tensor("wqf_d", [I, D], BF16, addr_space="Shared")}
    red_in = nc.dram_tensor("red_in", [1, 4], F32)
    red_out = nc.dram_tensor("red_out", [1, 4], F32, addr_space="Shared")
    sc_dram = nc.dram_tensor("sc_dram", [1, 16], F32)
    myg = nc.dram_tensor("myg", [L, DH], F32)
    xln_spill = nc.dram_tensor("xln_spill", [L // 2, D], F32)
    t_spill = nc.dram_tensor("t_spill", [L // 2, I], F32)

    bd = Builder(nc)
    T = lambda *a: bd.enter(nc.sbuf_tensor(*a))
    PSUM = lambda *a: bd.enter(nc.psum_tensor(*a))

    wxT_sb = T("wxT_sb", [P, NDTF * RN], F32)
    wdtT_sb = T("wdtT_sb", [R, DH], F32)
    bdt_sb = T("bdt_sb", [P, NDT], F32)
    a_sb = T("a_sb", [P, NDT * N], F32)
    dp1_sb = T("dp1_sb", [P, NDT], F32)
    idf_sb = T("idf_sb", [P, P], F32)
    idb_sb = T("idb_sb", [P, P], BF16)
    off_sb = T("off_sb", [1, 1], U32)
    ws_sb = T("ws_sb", [P, 8], F32)
    cval_sb = T("cval_sb", [P, 4], F32)
    sc_sb = T("sc_sb", [1, 16], F32)
    cols_sb = T("cols_sb", [P, 88], F32)

    xt_sb = T("xt_sb", [P, 3 * LC], F32)
    dt_sb = T("dt_sb", [P, NDT * LC], F32)
    u_sb = T("u_sb", [P, NDT * LC], F32)
    acc_sb = T("acc_sb", [P, NDT * LC], F32)
    BD = min(4, NDT)
    NB = NDT // BD
    da_sb = T("da_sb", [P, 2 * BD * LC], F32)
    db_sb = T("db_sb", [P, BD * LC], F32)
    h_sb = T("h_sb", [P, BD * LC], F32)
    bb_sb = T("bb_sb", [P, 2 * LC], F32)
    cb_sb = T("cb_sb", [P, 2 * LC], F32)
    hst_sb = T("hst_sb", [P, NDT * N], F32)
    r1t_sb = T("r1t_sb", [P, DH], F32)
    xp_sb = T("xp_sb", [RN, 2 * LC], F32)
    spa_sb = T("spa_sb", [P, max(LC, 16)], F32)

    SLW = NDTF * IG
    slab_sb = T("slab_sb", [P, 2 * SLW], BF16)
    UD = 4 * D
    unionA = T("unionA", [P, UD], F32)
    xq_sb = T("xq_sb", [P, max(D, PW, IGD)], BF16)
    xqT_sb = T("xqT_sb", [P, max(TPG * NDTF * P, PW, 2 * NIT * P)], BF16)
    gu_sb = T("gu_sb", [P, max(4 * IG, D)], F32)
    tr_sb = T("tr_sb", [P, max(2 * IGD, D)], F32)
    tq_sb = xq_sb
    tqT_sb = xqT_sb

    ps_mx = PSUM("ps_mx", [P, max(LC, 512)], F32)
    ps_tpb = PSUM("ps_tpb", [P, 1024], BF16)
    ps_g = PSUM("ps_g", [P, IG], F32)
    ps_u = PSUM("ps_u", [P, IG], F32)
    ps_dn = PSUM("ps_dn", [P, HH], F32)
    ps_tp = ps_mx
    ps_tpF = ps_tpb[:, 0:1024].bitcast(F32)

    mkb = bd.buf
    CONST = mkb("const")
    regs = {}

    def dma(e, dst, src, reads, writes, skey=None):
        sem = bd.dma_sem(e, skey if skey is not None else writes[0].key)
        return bd.op(e, lambda eng: eng.dma_start(out=dst, in_=src),
                     reads=reads, writes=writes, sem=sem, inc=16)

    def vop(emit, reads, writes):
        return bd.op("v", emit, reads, writes, sem="cV", inc=1)

    def sop(emit, reads, writes, inc=1):
        return bd.op("s", emit, reads, writes, sem="cS", inc=inc)

    def top(emit, reads, writes, inc=1):
        return bd.op("t", emit, reads, writes, sem="cT", inc=inc)

    def pop_(emit, reads, writes):
        return bd.op("p", emit, reads, writes, sem="cP", inc=1)

    dma("y", wxT_sb[:, :].rearrange("p (t f) -> p t f", t=NDTF),
        wxT_d.ap().rearrange("(t p) f -> p t f", p=P), [], [CONST])
    dma("y", wdtT_sb[:, :], wdtT_d[:, :], [], [CONST])
    dma("y", bdt_sb[:, :], bdt_d[:, :], [], [CONST])
    dma("y", a_sb[:, :], alog_d[:, :], [], [mkb("A")])
    dma("y", dp1_sb[:, :], dvec_d[:, :], [], [mkb("dvec")])
    dma("y", idf_sb[:, :], idf_d[:, :], [], [CONST])
    dma("y", idb_sb[:, :], idb_d[:, :], [], [CONST])
    dma("y", off_sb[:, :], off_d[:, :], [], [mkb("off")])
    dma("y", cval_sb[:, :], cval_d[:, :], [], [CONST])
    for _cv, _ci in ((0.0, 0), (1.0, 1), (MAGIC, 2), (-MAGIC, 3)):
        nc.const_aps.register(F32, _cv, cval_sb[:, _ci:_ci + 1])
    _csem = bd.dma_sem("y", CONST.key)
    _cm = bd.semn[_csem]
    for _e in Builder.ENGS:
        bd.raw_wait(_e, _csem, _cm)
    sop(lambda e: e.activation(a_sb[:, :], a_sb[:, :], AF.Exp),
        [mkb("A")], [mkb("A")])
    vop(lambda e: e.tensor_scalar_mul(a_sb[:, :], a_sb[:, :], -1.0),
        [mkb("A")], [mkb("A")])
    vop(lambda e: e.tensor_scalar_add(dp1_sb[:, :], dp1_sb[:, :], 1.0),
        [mkb("dvec")], [mkb("dvec")])
    vop(lambda e: e.memset(hst_sb[:, :], 0.0), [],
        [mkb("hst", c) for c in range(NDT * N)])
    vop(lambda e: e.memset(cols_sb[:, 80:88], 0.0), [], [mkb("wps")])
    vop(lambda e: e.memset(sc_sb[0:1, :], 0.0), [], [mkb("sc")])
    vop(lambda e: e.memset(cols_sb[:, 80:81], 1.0), [mkb("wps")],
        [mkb("wps")])
    bd.flush()

    def weight_prep():
        shard_specs = [("g", wgT_d, WS_G, I), ("u", wuT_d, WS_G, I),
                       ("d", wdT_d, WS_D, D)]

        def prep_tiles(wd_, rows, cols):
            idx = 0
            for tr in range((rows + P - 1) // P):
                pr = min(P, rows - tr * P)
                for tc in range(0, cols, PW):
                    w = min(PW, cols - tc)
                    yield idx, tr, pr, tc, w
                    idx += 1

        for mi, (m, wd_, rows, cols) in enumerate(shard_specs):
            for idx, tr, pr, tc, w in prep_tiles(wd_, rows, cols):
                slot = idx % 2
                tile = unionA[:pr, slot * PW: slot * PW + w]
                tb = mkb("U", slot)
                dma("y", tile, wd_[tr * P: tr * P + pr, tc: tc + w], [], [tb])
                sop(lambda e, t=tile, sl=slot, pr_=pr: e.activation(
                    t, t, AF.Abs, accum_out=cols_sb[:pr_, 84 + sl: 85 + sl]),
                    [tb], [tb, mkb("wps")])
                vop(lambda e, sl=slot, pr_=pr, mi_=mi: e.tensor_tensor(
                    cols_sb[:pr_, 81 + mi_: 82 + mi_],
                    cols_sb[:pr_, 81 + mi_: 82 + mi_],
                    cols_sb[:pr_, 84 + sl: 85 + sl], ALU.add),
                    [mkb("wps")], [mkb("wps")])
            yield
            top(lambda e, mi_=mi: e.matmul(
                ps_mx[0:1, mi_: mi_ + 1], cols_sb[:, 80:81],
                cols_sb[:, 81 + mi_: 82 + mi_],
                start=True, stop=True),
                [mkb("wps")], [mkb("ps_mx")], inc=1)
            sop(lambda e, mi_=mi: e.copy(sc_sb[0:1, mi_: mi_ + 1],
                                         ps_mx[0:1, mi_: mi_ + 1]),
                [mkb("ps_mx")], [mkb("sc")])
            yield
        yield
        dma("y", red_in[0:1, 0:3], sc_sb[0:1, 0:3], [mkb("sc")], [mkb("red_in")])
        bd.op("p", lambda e: e.collective_compute(
            "AllReduce", ALU.add, replica_groups=[list(range(NCORES))],
            ins=[red_in[0:1, 0:3].opt()], outs=[red_out[0:1, 0:3].opt()]),
            reads=[mkb("red_in")], writes=[mkb("red_out")],
            sem=bd.dma_sem("cc", ("red",)), inc=1)
        dma("y", sc_sb[0:1, 4:7], red_out[0:1, 0:3], [mkb("red_out")],
            [mkb("sc")])
        for i, (m, _, rows_, cols_) in enumerate(shard_specs):
            numel = float(rows_ * NCORES * cols_)
            vop(lambda e, i_=i, s_=1.0 / numel: e.tensor_scalar(
                sc_sb[0:1, 8 + i_: 9 + i_], sc_sb[0:1, 4 + i_: 5 + i_],
                s_, EPS_Q, ALU.mult, ALU.add), [mkb("sc")], [mkb("sc")])
        vop(lambda e: e.reciprocal(sc_sb[0:1, 12:15], sc_sb[0:1, 8:11]),
            [mkb("sc")], [mkb("sc")])
        dma("y", sc_dram[0:1, :], sc_sb[0:1, :], [mkb("sc")], [mkb("scd")])
        dma("y", ws_sb[:, 0:3], sc_dram[0:1, 8:11].partition_broadcast(P),
            [mkb("scd")], [mkb("ws")])
        dma("y", ws_sb[:, 3:6], sc_dram[0:1, 12:15].partition_broadcast(P),
            [mkb("scd")], [mkb("ws")])
        yield
        bd.flush()

        for m, wd_, rows, cols in shard_specs:
            sidx = {"g": 3, "u": 4, "d": 5}[m]
            for idx, tr, pr, tc, w in prep_tiles(wd_, rows, cols):
                slot = idx % 2
                tile = unionA[:pr, slot * PW: slot * PW + w]
                tb = mkb("U", slot)
                dma("y", tile, wd_[tr * P: tr * P + pr, tc: tc + w], [], [tb])
                sop(lambda e, t=tile, pr_=pr, si=sidx: e.activation(
                    t, t, AF.Identity, scale=ws_sb[:pr_, si: si + 1], bias=MAGIC),
                    [tb, mkb("ws")], [tb])
                sop(lambda e, t=tile: e.activation(t, t, AF.Identity, bias=-MAGIC),
                    [tb], [tb])
                obf = (xq_sb if slot == 0 else xqT_sb)[:pr, 0:w]
                ob = mkb("wpo", slot)
                vop(lambda e, t=tile, o=obf: e.tensor_scalar(
                    o, t, 1.0, -1.0, ALU.min, ALU.max), [tb], [ob])
                dma("y", wq_c[m][tr * P: tr * P + pr, tc: tc + w], obf,
                    [ob], [mkb("wqc", m)])
        for m, _, _, _ in shard_specs:
            bd.op("p", lambda e, m_=m: e.collective_compute(
                "AllGather", ALU.bypass, replica_groups=[list(range(NCORES))],
                ins=[wq_c[m_].ap().opt()], outs=[wq_full[m_].ap().opt()]),
                reads=[mkb("wqc", m)], writes=[mkb("wqf", m)],
                sem=bd.dma_sem("cc", ("wq",)), inc=1)
            yield
        bd.flush()


    NMM_LC = (LC + 511) // 512
    for k in range(NCH):
        for dtf in range(NDTF):
            slot = dtf % 2
            tile = da_sb[:, slot * LC: (slot + 1) * LC]
            tb = mkb("da", slot)
            dma("y", tile, xT_d[dtf * P: (dtf + 1) * P,
                                k * LC: (k + 1) * LC], [], [tb])
            for q in range(NMM_LC):
                c0, c1 = q * 512, min(LC, (q + 1) * 512)
                top(lambda e, t=tile, dt_=dtf, c0_=c0, c1_=c1: e.matmul(
                    ps_mx[0:RN, c0_:c1_],
                    wxT_sb[:, dt_ * RN: (dt_ + 1) * RN], t[:, c0_:c1_],
                    start=(dt_ == 0), stop=(dt_ == NDTF - 1)),
                    [CONST, tb], [mkb("ps_mx")], inc=1)
        slot = k % 2
        xps = xp_sb[0:RN, slot * LC: (slot + 1) * LC]
        xb = mkb("xp", slot)
        sop(lambda e, d=xps: e.copy(d, ps_mx[0:RN, 0:LC]),
            [mkb("ps_mx")], [xb])
        dma("y", xproj_dram[0:RN, k * LC: (k + 1) * LC], xps, [xb],
            [mkb("xpd", k)], skey=("xpd",))
    bd.flush()


    pend_cc = []

    def emit_pend_cc():
        while pend_cc:
            pend_cc.pop(0)()

    def mixer_chunk(k, pull):
        slot = k % 2
        xps = xp_sb[0:RN, slot * LC: (slot + 1) * LC]
        xb = mkb("xp", slot)
        dma("p", xps, xproj_dram[0:RN, k * LC: (k + 1) * LC],
            [mkb("xpd", k)], [xb])
        for dt in range(NDT):
            dst = dt_sb[:, dt * LC: (dt + 1) * LC]
            dtb = mkb("dt", dt)
            for q in range(NMM_LC):
                c0, c1 = q * 512, min(LC, (q + 1) * 512)
                top(lambda e, dt_=dt, c0_=c0, c1_=c1, x_=xps: e.matmul(
                    ps_mx[:, c0_:c1_], wdtT_sb[:, dt_ * P: (dt_ + 1) * P],
                    x_[0:R, c0_:c1_], start=True, stop=True),
                    [CONST, xb], [mkb("ps_mx")],
                    inc=1 if q == NMM_LC - 1 else 0)
            sop(lambda e, d=dst, dt_=dt: e.activation(
                d, ps_mx[:, 0:LC], AF.Identity,
                bias=bdt_sb[:, dt_: dt_ + 1]),
                [mkb("ps_mx"), CONST], [dtb])
            sab = mkb("spl")
            sop(lambda e, d=dst: e.activation(
                spa_sb[:, 0:LC], d, AF.Abs), [dtb, mkb("spa")], [sab])
            sop(lambda e: e.activation(spa_sb[:, 0:LC], spa_sb[:, 0:LC],
                                       AF.Exp, scale=-1.0), [sab], [sab])
            sop(lambda e: e.activation(spa_sb[:, 0:LC], spa_sb[:, 0:LC],
                                       AF.Ln, bias=1.0), [sab], [sab])
            vop(lambda e, d=dst: e.tensor_scalar_max(d, d, 0.0),
                [dtb], [dtb])
            vop(lambda e, d=dst: e.tensor_tensor(
                d, d, spa_sb[:, 0:LC], ALU.add), [dtb, sab], [dtb])
            xts = dt % 3
            dma("p", xt_sb[:, xts * LC: (xts + 1) * LC],
                xT_d[dt * P: (dt + 1) * P, k * LC: (k + 1) * LC],
                [], [mkb("xt", xts)])
            vop(lambda e, dt_=dt, d=dst, s_=xts: e.tensor_tensor(
                u_sb[:, dt_ * LC: (dt_ + 1) * LC], d,
                xt_sb[:, s_ * LC: (s_ + 1) * LC], ALU.mult),
                [dtb, mkb("xt", xts)], [mkb("u", dt)])
        for n in range(N):
            if n == 2:
                emit_pend_cc()
            bslot = n % 2
            bbuf = bb_sb[:, bslot * LC: (bslot + 1) * LC]
            cbuf = cb_sb[:, bslot * LC: (bslot + 1) * LC]
            bB = mkb("bb", bslot)
            bC = mkb("cb", bslot)
            dma("p", bbuf,
                xproj_dram[R + n: R + n + 1,
                           k * LC: (k + 1) * LC].partition_broadcast(P),
                [mkb("xpd", k)], [bB])
            dma("p", cbuf,
                xproj_dram[R + N + n: R + N + n + 1,
                           k * LC: (k + 1) * LC].partition_broadcast(P),
                [mkb("xpd", k)], [bC])
            bbrep = bbuf.unsqueeze(1).broadcast_to([P, BD, LC])
            cbrep = cbuf.unsqueeze(1).broadcast_to([P, BD, LC])
            for hb in range(NB):
                s = (n * NB + hb) % 2
                dav = da_sb[:, s * BD * LC: (s + 1) * BD * LC]
                bda = mkb("da", s)
                dts = list(range(hb * BD, (hb + 1) * BD))
                cols = [dt * N + n for dt in dts]
                for j, dt in enumerate(dts):
                    sop(lambda e, j=j, dt=dt, n=n, dv=dav: e.activation(
                        dv[:, j * LC: (j + 1) * LC],
                        dt_sb[:, dt * LC: (dt + 1) * LC], AF.Exp,
                        scale=a_sb[:, dt * N + n: dt * N + n + 1]),
                        [mkb("dt", dt), mkb("A")], [bda],
                        inc=1 if j == BD - 1 else 0)
                dbb = mkb("db")
                uv = u_sb[:, hb * BD * LC: (hb + 1) * BD * LC]
                vop(lambda e, uv=uv, bp=bbrep: e.tensor_tensor(
                    db_sb[:, 0:BD * LC].rearrange(
                        "p (r f) -> p r f", r=BD),
                    uv.rearrange("p (r f) -> p r f", r=BD),
                    bp, ALU.mult),
                    [mkb("u", dt) for dt in dts] + [bB], [dbb])
                hcols = [mkb("hst", c) for c in cols]
                scr1 = cols_sb[:, 72:72 + BD]
                scr2 = cols_sb[:, 76:76 + BD]
                b1 = mkb("scr1")
                b2_ = mkb("scr2")
                vop(lambda e, dv=dav, s1=scr1: e.tensor_copy(
                    s1, dv[:, 0:BD * LC:LC]), [bda], [b1])
                vop(lambda e, dv=dav: e.tensor_scalar_mul(
                    dv[:, 0:BD * LC:LC], dv[:, 0:BD * LC:LC], 0.0),
                    [bda, b1], [bda])
                hstv = hst_sb[:, hb * BD * N + n:
                              hb * BD * N + n + (BD - 1) * N + 1: N]
                vop(lambda e, hv_=hstv, s1=scr1, s2=scr2: e.tensor_tensor(
                    s2, s1, hv_, ALU.mult), [b1] + hcols, [b2_])
                vop(lambda e, s2=scr2: e.tensor_tensor(
                    db_sb[:, 0:BD * LC:LC], db_sb[:, 0:BD * LC:LC],
                    s2, ALU.add), [b2_, dbb], [dbb])
                hbuf = mkb("h")
                vop(lambda e, dv=dav: e.tensor_tensor_scan(
                    h_sb[:, 0:BD * LC], dv, db_sb[:, 0:BD * LC],
                    cval_sb[:, 0:1], ALU.mult, ALU.add),
                    [bda, dbb, CONST], [hbuf])
                vop(lambda e, hv_=hstv: e.tensor_copy(
                    hv_, h_sb[:, LC - 1:BD * LC:LC]),
                    [hbuf] + hcols, hcols)
                accv = acc_sb[:, hb * BD * LC: (hb + 1) * BD * LC]
                atags = [mkb("acc", dt) for dt in dts]
                if n == 0:
                    vop(lambda e, av=accv, cp=cbrep: e.tensor_tensor(
                        av.rearrange("p (r f) -> p r f", r=BD),
                        h_sb[:, 0:BD * LC].rearrange(
                            "p (r f) -> p r f", r=BD),
                        cp, ALU.mult), [hbuf, bC], atags)
                else:
                    vop(lambda e, cp=cbrep: e.tensor_tensor(
                        db_sb[:, 0:BD * LC].rearrange(
                            "p (r f) -> p r f", r=BD),
                        h_sb[:, 0:BD * LC].rearrange(
                            "p (r f) -> p r f", r=BD),
                        cp, ALU.mult), [hbuf, bC, dbb], [dbb])
                    vop(lambda e, av=accv: e.tensor_tensor(
                        av, av, db_sb[:, 0:BD * LC], ALU.add),
                        [dbb] + atags, atags)
            if n >= 2:
                pull()
        for dt in range(NDT):
            xts = dt % 3
            dma("p", xt_sb[:, xts * LC: (xts + 1) * LC],
                xT_d[dt * P: (dt + 1) * P, k * LC: (k + 1) * LC],
                [], [mkb("xt", xts)])
            vop(lambda e, dt_=dt, s_=xts: e.scalar_tensor_tensor(
                u_sb[:, dt_ * LC: (dt_ + 1) * LC],
                xt_sb[:, s_ * LC: (s_ + 1) * LC],
                dp1_sb[:, dt_: dt_ + 1],
                acc_sb[:, dt_ * LC: (dt_ + 1) * LC], ALU.mult, ALU.add),
                [mkb("xt", xts), mkb("dvec"), mkb("acc", dt)],
                [mkb("u", dt)])
        for lt in range(LTC):
            rview = r1t_sb[:, 0:DH]
            rtb = mkb("r1t", 0)
            for grp in range((NDT + 3) // 4):
                d0, d1 = grp * 4, min(NDT, grp * 4 + 4)
                for dt in range(d0, d1):
                    top(lambda e, dt_=dt, lt_=lt, g0=d0: e.transpose(
                        ps_tpF[:, (dt_ - g0) * P: (dt_ - g0 + 1) * P],
                        u_sb[:, dt_ * LC + lt_ * P:
                             dt_ * LC + (lt_ + 1) * P], idf_sb[:, :]),
                        [mkb("u", dt), CONST], [mkb("ps_tpb")],
                        inc=1 if dt == d1 - 1 else 0)
                sop(lambda e, g0=d0, g1=d1, rv=rview: e.copy(
                    rv[:, g0 * P: g1 * P], ps_tpF[:, 0:(g1 - g0) * P]),
                    [mkb("ps_tpb")], [rtb])
            gate_reads = [rtb] + ([mkb("gath", k - 2)] if k >= 2 else [])
            dma("y", contrib[k * LC + lt * P: k * LC + (lt + 1) * P, :],
                rview, gate_reads, [mkb("contrib", k)],
                skey=("contrib", k % 2))
            pull()

        def emit_cc(k_=k):
            bd.op("p", lambda e: e.collective_compute(
                "AllGather", ALU.bypass,
                replica_groups=[[2 * i, 2 * i + 1]
                                for i in range(NCORES // 2)],
                ins=[contrib[k_ * LC: (k_ + 1) * LC, :].opt()],
                outs=[gathered[2 * k_ * LC: 2 * (k_ + 1) * LC, :].opt()]),
                reads=[mkb("contrib", k_)], writes=[mkb("gath", k_)],
                sem=bd.dma_sem("cc", ("g", k_ // (2 * NRC))), inc=1)
        pend_cc.append(emit_cc)

    def Ub(i):
        return mkb("U", i)

    r1tok = unionA[:, 0:D]
    xlnb = unionA[:, 2 * D:3 * D]
    s2c = lambda j: cols_sb[:, 40 + j: 41 + j]
    sxdc = lambda j: cols_sb[:, 56 + j: 57 + j]

    NMM_IG = (IG + 511) // 512
    NMM_HH = (HH + 511) // 512
    NH1 = max(NIT // 2, 1)

    def gslot(jj):
        return gu_sb[:, (jj % 2) * IG: (jj % 2 + 1) * IG]

    def uslot(jj):
        return gu_sb[:, (2 + jj % 2) * IG: (3 + jj % 2) * IG]

    def ffn_group(g):
        gv = gathered[g * 4 * BBLK: (g + 1) * 4 * BBLK, :].rearrange(
            "(a rc h rr) d -> a rc h rr d", a=2, rc=NRC, h=2)
        dstv = myg[g * 2 * BBLK: (g + 1) * 2 * BBLK, :].rearrange(
            "(rc h rr) d -> rc h rr d", rc=NRC, h=2)

        pieces_done = set()

        def ensure_piece(rc_):
            if rc_ in pieces_done:
                return
            pieces_done.add(rc_)
            base = 2 * g * NRC
            needed = [mkb("gath", base + rc_ + 2 * a_) for a_ in range(2)]
            while any(b.w is None for b in needed):
                yield

            def em(e, gv_=gv, dstv_=dstv, rc__=rc_):
                ro = regs.get("y")
                if ro is None:
                    ro = regs["y"] = e.alloc_register("off_reg")
                e.reg_load(ro, off_sb[0:1, 0:1])
                ofs = e.snap(ro)
                return e.dma_start(out=dstv_[rc__],
                                   in_=gv_[bass.ds(ofs, 1), rc__])
            mgb = mkb("myg", g, rc_)
            bd.op("y", em,
                  reads=[mkb("off")] + needed,
                  writes=[mgb], sem=bd.dma_sem("y", mgb.key), inc=16)
            for _ in range(8):
                yield
        def a_tile(jj):
            j = g * TPG + jj
            r = jj * P
            rc, rr = r // LC, r % LC
            yield from ensure_piece(rc)
            rb = Ub(0)
            for hh in range(2):
                st = g * 2 * BBLK + rc * 2 * LC + hh * LC + rr
                dma("p", r1tok[:, hh * DH: (hh + 1) * DH],
                    myg[st: st + P, :], [mkb("myg", g, rc)], [rb])
            sumb = mkb("c_sum")
            vop(lambda e: e.reduce_sum(cols_sb[:, 0:1], r1tok, axis=AX),
                [rb], [sumb])
            vop(lambda e: e.tensor_scalar_mul(
                cols_sb[:, 0:1], cols_sb[:, 0:1], -1.0 / D),
                [sumb], [sumb])
            yield
            vb = mkb("c_var")
            sop(lambda e: e.activation(xlnb, r1tok, AF.Square,
                                       accum_out=cols_sb[:, 1:2]),
                [rb], [Ub(2), vb])
            vop(lambda e: e.scalar_tensor_tensor(
                cols_sb[:, 6:7], cols_sb[:, 0:1], -1.0,
                cols_sb[:, 0:1], ALU.mult, ALU.mult),
                [sumb], [mkb("c_msq")])
            yield
            vop(lambda e: e.tensor_scalar(cols_sb[:, 1:2], cols_sb[:, 1:2],
                                          1.0 / D, EPS_LN, ALU.mult,
                                          ALU.add), [vb], [vb])
            vop(lambda e: e.tensor_tensor(cols_sb[:, 1:2], cols_sb[:, 1:2],
                                          cols_sb[:, 6:7], ALU.add),
                [vb, mkb("c_msq")], [vb])
            yield
            sop(lambda e: e.activation(cols_sb[:, 1:2], cols_sb[:, 1:2],
                                       AF.Ln), [vb], [vb])
            sop(lambda e: e.activation(cols_sb[:, 1:2], cols_sb[:, 1:2],
                                       AF.Exp, scale=-0.5), [vb], [vb])
            yield
            b2 = mkb("c_b2")
            vop(lambda e: e.tensor_tensor(cols_sb[:, 2:3], cols_sb[:, 0:1],
                                          cols_sb[:, 1:2], ALU.mult),
                [sumb, vb], [b2])
            xb = Ub(2)
            sop(lambda e: e.activation(xlnb, r1tok, AF.Identity,
                                       scale=cols_sb[:, 1:2],
                                       bias=cols_sb[:, 2:3]),
                [rb, vb, b2], [xb])
            dma("y", xln_spill[j * P: (j + 1) * P, :], xlnb, [xb],
                [mkb("xlnsp", j)], skey=("xlnsp", g % 2, jj))
            yield
            am = mkb("c_amax")
            vop(lambda e: e.reduce_max(cols_sb[:, 3:4], xlnb, axis=AX,
                                       apply_absolute_value=True),
                [xb], [am])
            vop(lambda e: e.tensor_scalar(cols_sb[:, 3:4], cols_sb[:, 3:4],
                                          1.0 / 127.0, EPS_Q, ALU.mult,
                                          ALU.add), [am], [am])
            vop(lambda e: e.reciprocal(cols_sb[:, 4:5], cols_sb[:, 3:4]),
                [am], [mkb("c_sinv")])
            yield
            vop(lambda e, j_=j: e.tensor_tensor(
                cols_sb[:, 8 + 2 * j_: 9 + 2 * j_], cols_sb[:, 3:4],
                ws_sb[:, 0:1], ALU.mult), [am, mkb("ws")], [mkb("sxg", j)])
            vop(lambda e, j_=j: e.tensor_tensor(
                cols_sb[:, 9 + 2 * j_: 10 + 2 * j_], cols_sb[:, 3:4],
                ws_sb[:, 1:2], ALU.mult), [am, mkb("ws")], [mkb("sxu", j)])
            sop(lambda e: e.activation(r1tok, xlnb, AF.Identity,
                                       scale=cols_sb[:, 4:5], bias=MAGIC),
                [xb, mkb("c_sinv")], [Ub(0)])
            qb = mkb("xqb")
            sop(lambda e: e.activation(xq_sb[:, 0:D], r1tok, AF.Identity,
                                       bias=-MAGIC), [Ub(0)], [qb])
            yield
            for grp in range((NDTF + 7) // 8):
                d0, d1 = grp * 8, min(NDTF, grp * 8 + 8)
                for dt in range(d0, d1):
                    top(lambda e, dt_=dt, g0=d0: e.transpose(
                        ps_tpb[:, (dt_ - g0) * P: (dt_ - g0 + 1) * P],
                        xq_sb[:, dt_ * P: (dt_ + 1) * P], idb_sb[:, :]),
                        [qb, CONST], [mkb("ps_tpb")],
                        inc=1 if dt == d1 - 1 else 0)
                vop(lambda e, g0=d0, g1=d1, jj_=jj: e.tensor_copy(
                    xqT_sb[:, (jj_ * NDTF + g0) * P:
                           (jj_ * NDTF + g1) * P],
                    ps_tpb[:, 0:(g1 - g0) * P]),
                    [mkb("ps_tpb")], [mkb("xqT", jj)])
                yield

        def emit_gate_up(ig, jj):
            j = g * TPG + jj
            for half, psum in ((0, ps_g), (1, ps_u)):
                sl = slab_sb[:, half * SLW: (half + 1) * SLW]
                slb = mkb("slab", half)
                pb = mkb("ps_gu", half)
                for dt in range(NDTF):
                    for q in range(NMM_IG):
                        c0, c1 = q * 512, min(IG, (q + 1) * 512)
                        top(lambda e, dt_=dt, c0_=c0, c1_=c1, s_=sl,
                            ps_=psum, jj_=jj: e.matmul(
                                ps_[:, c0_:c1_],
                                xqT_sb[:, (jj_ * NDTF + dt_) * P:
                                       (jj_ * NDTF + dt_ + 1) * P],
                                s_[:, dt_ * IG + c0_: dt_ * IG + c1_],
                                start=(dt_ == 0), stop=(dt_ == NDTF - 1)),
                            [mkb("xqT", jj), slb], [pb],
                            inc=1 if (dt == NDTF - 1 and q == NMM_IG - 1)
                            else 0)
                go = gslot(jj) if half == 0 else uslot(jj)
                gb = mkb("gu", half, jj % 2)
                scol = cols_sb[:, 8 + 2 * j + half: 9 + 2 * j + half]
                sdep = mkb("sxg", j) if half == 0 else mkb("sxu", j)
                fn = AF.Silu if half == 0 else AF.Identity
                sop(lambda e, go_=go, f=fn, c=scol, ps_=psum:
                    e.activation(go_, ps_[:, 0:IG], f, scale=c),
                    [pb, sdep], [gb])

        def emit_t(ig, jj):
            j = g * TPG + jj
            tb = mkb("gu", 0, jj % 2)
            vop(lambda e, jj_=jj: e.tensor_tensor(
                gslot(jj_), gslot(jj_), uslot(jj_), ALU.mult),
                [mkb("gu", 0, jj % 2), mkb("gu", 1, jj % 2)], [tb])
            mxb = mkb("c_tmax")
            vop(lambda e, jj_=jj: e.reduce_max(
                cols_sb[:, 5:6], gslot(jj_), axis=AX,
                apply_absolute_value=True), [tb], [mxb])
            if ig == 0:
                vop(lambda e, j_=j: e.tensor_copy(s2c(j_), cols_sb[:, 5:6]),
                    [mxb], [mkb("s2", j)])
            else:
                vop(lambda e, j_=j: e.tensor_tensor(
                    s2c(j_), s2c(j_), cols_sb[:, 5:6], ALU.max),
                    [mxb, mkb("s2", j)], [mkb("s2", j)])
            dma("y", t_spill[j * P: (j + 1) * P, ig * IG: (ig + 1) * IG],
                gslot(jj), [tb], [mkb("tsp", j)], skey=("tsp", g % 2, jj))

        def b_pass(jset):
            prev = None
            for ig in range(NIG):
                for half, name in ((0, "g"), (1, "u")):
                    sl = slab_sb[:, half * SLW: (half + 1) * SLW]
                    dma("y", sl.rearrange("p (t f) -> p t f", t=NDTF),
                        wq_full[name].ap()[0:D,
                                           ig * IG:(ig + 1) * IG].rearrange(
                            "(t p) f -> p t f", p=P),
                        [mkb("wqf", name)], [mkb("slab", half)])
                for jj in jset:
                    emit_gate_up(ig, jj)
                    if prev is not None:
                        emit_t(*prev)
                    prev = (ig, jj)
                    yield
            emit_t(*prev)

        jsets = ([[0, 1], [2, 3]] if (g == NG - 1 and TPG == 4)
                 else [list(range(TPG))])
        for jset in jsets:
            for jj in jset:
                yield from a_tile(jj)
            yield from b_pass(jset)
        for jj in range(TPG):
            j = g * TPG + jj
            vop(lambda e, j_=j: e.tensor_scalar(
                s2c(j_), s2c(j_), 1.0 / 127.0, EPS_Q, ALU.mult, ALU.add),
                [mkb("s2", j)], [mkb("s2", j)])
            vop(lambda e, j_=j: e.tensor_tensor(
                sxdc(j_), s2c(j_), ws_sb[:, 2:3], ALU.mult),
                [mkb("s2", j), mkb("ws")], [mkb("sxd", j)])
            vop(lambda e, j_=j: e.reciprocal(s2c(j_), s2c(j_)),
                [mkb("s2", j)], [mkb("s2", j)])
            yield

        def emit_tq_prep(igd, jj):
            j = g * TPG + jj
            trs = jj % 2
            trv = tr_sb[:, trs * IGD: (trs + 1) * IGD]
            trb = mkb("tr", trs)
            dma("p", trv, t_spill[j * P:(j + 1) * P,
                                  igd * IGD:(igd + 1) * IGD],
                [mkb("tsp", j)], [trb], skey=("trl", g % 2, jj % 2))
            sop(lambda e, t=trv, j_=j: e.activation(
                t, t, AF.Identity, scale=s2c(j_), bias=MAGIC),
                [trb, mkb("s2", j)], [trb])
            qb = mkb("xqb")
            sop(lambda e, t=trv: e.activation(
                tq_sb[:, 0:IGD], t, AF.Identity, bias=-MAGIC),
                [trb], [qb])
            yield
            tqv = tqT_sb[:, (jj % 2) * NIT * P: (jj % 2 + 1) * NIT * P]
            tqtags = [mkb("xqT", r)
                      for r in range(min(TPG, -(-2 * NIT) // NDTF))]
            for grp in range((NIT + 7) // 8):
                d0, d1 = grp * 8, min(NIT, grp * 8 + 8)
                for it in range(d0, d1):
                    top(lambda e, it_=it, g0=d0: e.transpose(
                        ps_tpb[:, (it_ - g0) * P: (it_ - g0 + 1) * P],
                        tq_sb[:, it_ * P: (it_ + 1) * P], idb_sb[:, :]),
                        [qb, CONST], [mkb("ps_tpb")],
                        inc=1 if it == d1 - 1 else 0)
                vop(lambda e, g0=d0, g1=d1, tv=tqv: e.tensor_copy(
                    tv[:, g0 * P: g1 * P], ps_tpb[:, 0:(g1 - g0) * P]),
                    [mkb("ps_tpb")], tqtags)
                yield

        def emit_dn_mms(igd, jj, hhx, pslot):
            tqv = tqT_sb[:, (jj % 2) * NIT * P: (jj % 2 + 1) * NIT * P]
            tqb = mkb("xqT", 0)
            tqtags = [mkb("xqT", r)
                      for r in range(min(TPG, -(-2 * NIT) // NDTF))]
            psum, pbk = ((ps_dn, mkb("ps_dn")) if pslot == 0
                         else (ps_g, mkb("ps_gu", 0)))
            for it in range(NIT):
                hseg = 0 if it < NH1 else 1
                t0 = 0 if hseg == 0 else NH1
                t1 = min(NH1, NIT) if hseg == 0 else NIT
                sl = slab_sb[:, t0 * D: t1 * D]
                for q in range(NMM_HH):
                    c0, c1 = q * 512, min(HH, (q + 1) * 512)
                    top(lambda e, it_=it, c0_=c0, c1_=c1, h_=hhx,
                        t0_=t0, s_=sl, ps_=psum, tv=tqv: e.matmul(
                            ps_[:, c0_:c1_],
                            tv[:, it_ * P: (it_ + 1) * P],
                            s_[:, (it_ - t0_) * D + h_ * HH + c0_:
                               (it_ - t0_) * D + h_ * HH + c1_],
                            start=(it_ == 0), stop=(it_ == NIT - 1)),
                        tqtags + [mkb("slab", hseg)], [pbk],
                        inc=1 if (it == NIT - 1 and q == NMM_HH - 1)
                        else 0)

        def emit_dn_add(igd, jj, hhx, pslot):
            psum, pbk = ((ps_dn, mkb("ps_dn")) if pslot == 0
                         else (ps_g, mkb("ps_gu", 0)))
            dst = unionA[:, jj * D + hhx * HH: jj * D + (hhx + 1) * HH]
            dnb = Ub(jj)
            if igd == 0:
                vop(lambda e, d=dst, ps_=psum: e.tensor_copy(
                    d, ps_[:, 0:HH]), [pbk], [dnb])
            else:
                vop(lambda e, d=dst, ps_=psum: e.tensor_tensor(
                    d, d, ps_[:, 0:HH], ALU.add), [pbk, dnb], [dnb])

        assert NHH == 1 or HH * NHH == D
        prevc = None
        for igd in range(NIGD):
            for hseg, (t0, t1) in enumerate(((0, min(NH1, NIT)),
                                             (NH1, NIT))):
                if t1 <= t0:
                    continue
                sl = slab_sb[:, t0 * D: t1 * D]
                dma("y", sl.rearrange("p (t f) -> p t f", t=t1 - t0),
                    wq_full["d"].ap()[igd * IGD + t0 * P:
                                      igd * IGD + t1 * P, :].rearrange(
                        "(t p) f -> p t f", p=P),
                    [mkb("wqf", "d")], [mkb("slab", hseg)])
            for jj in range(TPG):
                yield from emit_tq_prep(igd, jj)
                for hhx in range(NHH):
                    pslot = (igd * TPG * NHH + jj * NHH + hhx) % 2
                    emit_dn_mms(igd, jj, hhx, pslot)
                    if prevc is not None:
                        emit_dn_add(*prevc)
                    prevc = (igd, jj, hhx, pslot)
                    yield
        emit_dn_add(*prevc)
        yield

        GUB = [mkb("gu", 0, 0), mkb("gu", 0, 1), mkb("gu", 1, 0),
               mkb("gu", 1, 1)]
        for jj in range(TPG):
            j = g * TPG + jj
            xv = gu_sb[:, 0:D]
            xb2 = mkb("xln2")
            dma("p", xv, xln_spill[j * P: (j + 1) * P, :],
                [mkb("xlnsp", j)] + GUB, GUB + [xb2], skey=("xln2",))
            r2b = mkb("tr", 0)
            bd.op("v", lambda e, jj_=jj, j_=j: e.scalar_tensor_tensor(
                tr_sb[:, 0:D], unionA[:, jj_ * D: (jj_ + 1) * D],
                sxdc(j_), xv, ALU.mult, ALU.add),
                reads=[Ub(jj), mkb("sxd", j), xb2, mkb("tr", 1)],
                writes=[r2b, mkb("tr", 1)], sem="cV", inc=1)
            r2v = tr_sb[:, 0:D]
            sumb = mkb("c_sum")
            vop(lambda e: e.reduce_sum(cols_sb[:, 0:1], r2v, axis=AX),
                [r2b], [sumb])
            vop(lambda e: e.tensor_scalar_mul(
                cols_sb[:, 0:1], cols_sb[:, 0:1], -1.0 / D),
                [sumb], [sumb])
            yield
            vb = mkb("c_var")
            sop(lambda e, jj_=jj: e.activation(
                unionA[:, jj_ * D: (jj_ + 1) * D], r2v, AF.Square,
                accum_out=cols_sb[:, 1:2]),
                [r2b, Ub(jj)], [Ub(jj), vb])
            vop(lambda e: e.scalar_tensor_tensor(
                cols_sb[:, 6:7], cols_sb[:, 0:1], -1.0, cols_sb[:, 0:1],
                ALU.mult, ALU.mult), [sumb], [mkb("c_msq")])
            yield
            vop(lambda e: e.tensor_scalar(cols_sb[:, 1:2], cols_sb[:, 1:2],
                                          1.0 / D, EPS_LN, ALU.mult,
                                          ALU.add), [vb], [vb])
            vop(lambda e: e.tensor_tensor(cols_sb[:, 1:2], cols_sb[:, 1:2],
                                          cols_sb[:, 6:7], ALU.add),
                [vb, mkb("c_msq")], [vb])
            yield
            sop(lambda e: e.activation(cols_sb[:, 1:2], cols_sb[:, 1:2],
                                       AF.Ln), [vb], [vb])
            sop(lambda e: e.activation(cols_sb[:, 1:2], cols_sb[:, 1:2],
                                       AF.Exp, scale=-0.5), [vb], [vb])
            yield
            b2 = mkb("c_b2")
            vop(lambda e: e.tensor_tensor(cols_sb[:, 2:3], cols_sb[:, 0:1],
                                          cols_sb[:, 1:2], ALU.mult),
                [sumb, vb], [b2])
            ob = mkb("outp")
            bd.op("s", lambda e: e.activation(
                gu_sb[:, 0:D], tr_sb[:, 0:D], AF.Identity,
                scale=cols_sb[:, 1:2], bias=cols_sb[:, 2:3]),
                reads=[r2b, vb, b2, xb2] + GUB, writes=GUB + [ob],
                sem="cS", inc=1)
            dma("y", out_d[j * P: (j + 1) * P, :], gu_sb[:, 0:D],
                [ob] + GUB, [mkb("outd")])
            yield

    active = [weight_prep()]

    def pull():
        budget = 6 if len(active) > 1 else 4
        while active and budget > 0:
            try:
                next(active[0])
                budget -= 1
            except StopIteration:
                active.pop(0)

    for k in range(NCH):
        mixer_chunk(k, pull)
        if (k + 2) % (2 * NRC) == 0 and (k + 2) // (2 * NRC) <= NG:
            active.append(ffn_group((k + 2) // (2 * NRC) - 1))
    emit_pend_cc()
    while active:
        try:
            next(active[0])
        except StopIteration:
            active.pop(0)

    for name, tot in list(bd.semn.items()):
        if name.startswith("D_") and tot > 0:
            bd.raw_wait("y", name, tot)
    bd.finalize()
    bd.close()
    return nc


_CACHE = {}


def _get_program(key, cfg):
    if key not in _CACHE:
        _CACHE[key] = build_program(cfg)
    return _CACHE[key]


def kernel(L0_attn_x, L0_attn_W_x, L0_attn_W_dt, L0_attn_b_dt, L0_attn_A_log,
           L0_attn_D, L0_ffn_w_gate, L0_ffn_w_up, L0_ffn_w_down):
    x = np.ascontiguousarray(np.asarray(L0_attn_x, np.float32))
    W_x = np.asarray(L0_attn_W_x, np.float32)
    W_dt = np.asarray(L0_attn_W_dt, np.float32)
    b_dt = np.asarray(L0_attn_b_dt, np.float32)
    A_log = np.asarray(L0_attn_A_log, np.float32)
    Dv = np.asarray(L0_attn_D, np.float32)
    wg = np.asarray(L0_ffn_w_gate, np.float32)
    wu = np.asarray(L0_ffn_w_up, np.float32)
    wd = np.asarray(L0_ffn_w_down, np.float32)

    B, L, D = x.shape
    N = A_log.shape[1]
    R = W_dt.shape[1]
    I = wg.shape[0]
    nc = _get_program((L, D, N, R, I), dict(L=L, D=D, N=N, R=R, I=I))

    DH = D // 2
    NDT = DH // P
    WS_G = D // NCORES
    WS_D = I // NCORES
    LC = L // 8
    wgT = np.ascontiguousarray(wg.T)
    wuT = np.ascontiguousarray(wu.T)
    wdT = np.ascontiguousarray(wd.T)
    WxT = np.ascontiguousarray(W_x.T)
    WdtT = np.ascontiguousarray(W_dt.T)
    idf = np.eye(P, dtype=np.float32)
    idb = np.eye(P).astype(ml_dtypes.bfloat16)
    cvals = np.tile(np.array([0.0, 1.0, MAGIC, -MAGIC], np.float32), (P, 1))

    in_maps = []
    for c in range(NCORES):
        b, par = c // 2, c % 2
        my = slice(par * DH, (par + 1) * DH)
        perm = np.r_[np.arange(par * DH, (par + 1) * DH),
                     np.arange((1 - par) * DH, (2 - par) * DH)]
        col = lambda v: np.ascontiguousarray(v[my].reshape(NDT, P).T)
        in_maps.append(dict(
            xT=np.ascontiguousarray(x[b].T[perm]),
            wxT=np.ascontiguousarray(WxT[perm]),
            wdtT=np.ascontiguousarray(WdtT[:, my]),
            bdt=col(b_dt),
            alog=np.ascontiguousarray(
                A_log[my].reshape(NDT, P, N).transpose(1, 0, 2).reshape(
                    P, NDT * N)),
            dvec=col(Dv),
            wgTs=np.ascontiguousarray(wgT[c * WS_G:(c + 1) * WS_G]),
            wuTs=np.ascontiguousarray(wuT[c * WS_G:(c + 1) * WS_G]),
            wdTs=np.ascontiguousarray(wdT[c * WS_D:(c + 1) * WS_D]),
            idf=idf, idb=idb,
            cval=cvals,
            off=np.array([[par]], dtype=np.uint32),
        ))

    res = run_bass_kernel_spmd(nc, in_maps, core_ids=list(range(NCORES)))

    out = np.empty((B, L, D), np.float32)
    LH = L // 2
    for c in range(NCORES):
        b, par = c // 2, c % 2
        oc = np.asarray(res.results[c]["out"]).reshape(LH, D)
        for t in range(LH // LC):
            gl = (2 * t + par) * LC
            out[b, gl:gl + LC] = oc[t * LC:(t + 1) * LC]
    return out


# version 28
# speedup vs baseline: 1.0127x; 1.0030x over previous
import numpy as np
import ml_dtypes
import concourse.bass as bass
import concourse.mybir as mybir
from concourse.bass_utils import run_bass_kernel_spmd

F32 = mybir.dt.float32
BF16 = mybir.dt.bfloat16
U32 = mybir.dt.uint32
AF = mybir.ActivationFunctionType
ALU = mybir.AluOpType

P = 128
MAGIC = float(1.5 * 2 ** 23)
EPS_Q = 1e-05
EPS_LN = 1e-05
NCORES = 8
AX = mybir.AxisListType.X


class Buf:
    __slots__ = ("w", "readers", "key")

    def __init__(self, key=None):
        self.w = None
        self.readers = []
        self.key = key


class Builder:

    ENGS = ("t", "v", "s", "p", "y")
    SECT = {"t": "tensor", "v": "vector", "s": "scalar", "p": "gpsimd",
            "y": "sync"}

    def __init__(self, nc):
        self.nc = nc
        self.stack = []
        self.sem = {}
        self.semn = {}
        for name in ("cT", "cV", "cS", "cP", "dS", "dF", "dG", "cc"):
            self.sem[name] = self.enter(nc.semaphore(name))
            self.semn[name] = 0
        self.known = {e: {} for e in self.ENGS}
        self.phase_ops = {e: [] for e in self.ENGS}
        self.bufs = {}
        self.nblocks = 0
        self.pending_reads = {e: [] for e in self.ENGS}

    def enter(self, cm):
        v = cm.__enter__()
        self.stack.append(cm)
        return v

    def close(self):
        for cm in reversed(self.stack):
            cm.__exit__(None, None, None)
        self.stack = []

    def buf(self, *key):
        b = self.bufs.get(key)
        if b is None:
            b = self.bufs[key] = Buf(key)
        return b

    def dma_sem(self, queue, key):
        name = "D_" + queue + "_" + "_".join(str(k) for k in key)
        if name not in self.sem:
            self.sem[name] = self.enter(self.nc.semaphore(name))
            self.semn[name] = 0
        return name

    def op(self, e, emit, reads=(), writes=(), sem=None, inc=0):
        waits = {}

        def add(s, v):
            if s.startswith("D_"):
                v = self.semn[s]
            if waits.get(s, -1) < v:
                waits[s] = v

        for b in reads:
            if b.w is not None:
                add(*b.w)
        for b in writes:
            deps = list(b.readers)
            if b.w is not None:
                deps.append(b.w)
            for s, v in deps:
                add(s, v)
        known = self.known[e]
        fw = {}
        for s, v in waits.items():
            if known.get(s, 0) < v:
                fw[s] = v
                known[s] = v
        mile = None
        if sem is not None and inc:
            self.semn[sem] += inc
            mile = (sem, self.semn[sem])
        if mile is not None:
            for b in writes:
                b.w = mile
                b.readers = []
            for b in reads:
                b.readers.append(mile)
            for b in self.pending_reads[e]:
                b.readers.append(mile)
            self.pending_reads[e] = []
        else:
            self.pending_reads[e].extend(reads)
        self.phase_ops[e].append((fw, emit, sem if mile else None,
                                  inc if mile else 0))
        return mile

    def raw_wait(self, e, sem, val):
        known = self.known[e]
        if known.get(sem, 0) >= val:
            return
        known[sem] = val
        self.phase_ops[e].append(({sem: val}, None, None, 0))

    def flush(self):
        pass

    def finalize(self):
        nc = self.nc
        pending = {e: ops for e, ops in self.phase_ops.items() if ops}
        self.phase_ops = {e: [] for e in self.ENGS}
        if not pending:
            return
        block = nc.Block("main")
        block.__enter__()

        def runner(ops):
            def body(eng):
                for waits, emit, sem, inc in ops:
                    for s, v in waits.items():
                        eng.wait_ge(self.sem[s], v)
                    if emit is not None:
                        instr = emit(eng)
                        if sem is not None:
                            instr.then_inc(self.sem[sem], inc)
            return body

        for e, ops in pending.items():
            getattr(block, self.SECT[e])(runner(ops))
        block.__exit__(None, None, None)


def build_program(cfg):
    L, D, N, R, I = cfg["L"], cfg["D"], cfg["N"], cfg["R"], cfg["I"]
    NCH = 16
    LC = L // NCH
    DH = D // 2
    NDT = DH // P
    NDTF = D // P
    LTC = LC // P
    TOKT = (L // 2) // P
    NG = 4
    TPG = TOKT // NG
    IG = min(1024, I)
    NIG = I // IG
    IGD = min(1024, I)
    NIGD = I // IGD
    NIT = IGD // P
    HH = min(1024, D)
    NHH = D // HH
    RN = R + 2 * N
    BBLK = 0
    WS_G = D // NCORES
    WS_D = I // NCORES
    PW = min(2048, max(D, 512))
    BBLK = TPG * P
    NRC = BBLK // LC

    nc = bass.Bass("TRN2", target_bir_lowering=False)

    xT_d = nc.dram_tensor("xT", [D, L], F32, kind="ExternalInput")
    wxT_d = nc.dram_tensor("wxT", [D, RN], F32, kind="ExternalInput")
    wdtT_d = nc.dram_tensor("wdtT", [R, DH], F32, kind="ExternalInput")
    bdt_d = nc.dram_tensor("bdt", [P, NDT], F32, kind="ExternalInput")
    alog_d = nc.dram_tensor("alog", [P, NDT * N], F32, kind="ExternalInput")
    dvec_d = nc.dram_tensor("dvec", [P, NDT], F32, kind="ExternalInput")
    wgT_d = nc.dram_tensor("wgTs", [WS_G, I], F32, kind="ExternalInput")
    wuT_d = nc.dram_tensor("wuTs", [WS_G, I], F32, kind="ExternalInput")
    wdT_d = nc.dram_tensor("wdTs", [WS_D, D], F32, kind="ExternalInput")
    idf_d = nc.dram_tensor("idf", [P, P], F32, kind="ExternalInput")
    idb_d = nc.dram_tensor("idb", [P, P], BF16, kind="ExternalInput")
    off_d = nc.dram_tensor("off", [1, 1], U32, kind="ExternalInput")
    cval_d = nc.dram_tensor("cval", [P, 4], F32, kind="ExternalInput")
    out_d = nc.dram_tensor("out", [L // 2, D], F32, kind="ExternalOutput")

    xproj_dram = nc.dram_tensor("xproj_dram", [RN, L], F32)
    contrib = nc.dram_tensor("contrib", [L, DH], F32)
    gathered = nc.dram_tensor("gathered", [2 * L, DH], F32)
    wq_c = {"g": nc.dram_tensor("wqc_g", [WS_G, I], BF16),
            "u": nc.dram_tensor("wqc_u", [WS_G, I], BF16),
            "d": nc.dram_tensor("wqc_d", [WS_D, D], BF16)}
    wq_full = {"g": nc.dram_tensor("wqf_g", [D, I], BF16, addr_space="Shared"),
               "u": nc.dram_tensor("wqf_u", [D, I], BF16, addr_space="Shared"),
               "d": nc.dram_tensor("wqf_d", [I, D], BF16, addr_space="Shared")}
    red_in = nc.dram_tensor("red_in", [1, 4], F32)
    red_out = nc.dram_tensor("red_out", [1, 4], F32, addr_space="Shared")
    sc_dram = nc.dram_tensor("sc_dram", [1, 16], F32)
    myg = nc.dram_tensor("myg", [L, DH], F32)
    xln_spill = nc.dram_tensor("xln_spill", [L // 2, D], F32)
    t_spill = nc.dram_tensor("t_spill", [L // 2, I], F32)

    bd = Builder(nc)
    T = lambda *a: bd.enter(nc.sbuf_tensor(*a))
    PSUM = lambda *a: bd.enter(nc.psum_tensor(*a))

    wxT_sb = T("wxT_sb", [P, NDTF * RN], F32)
    wdtT_sb = T("wdtT_sb", [R, DH], F32)
    bdt_sb = T("bdt_sb", [P, NDT], F32)
    a_sb = T("a_sb", [P, NDT * N], F32)
    dp1_sb = T("dp1_sb", [P, NDT], F32)
    idf_sb = T("idf_sb", [P, P], F32)
    idb_sb = T("idb_sb", [P, P], BF16)
    off_sb = T("off_sb", [1, 1], U32)
    ws_sb = T("ws_sb", [P, 8], F32)
    cval_sb = T("cval_sb", [P, 4], F32)
    sc_sb = T("sc_sb", [1, 16], F32)
    cols_sb = T("cols_sb", [P, 88], F32)

    xt_sb = T("xt_sb", [P, 3 * LC], F32)
    dt_sb = T("dt_sb", [P, NDT * LC], F32)
    u_sb = T("u_sb", [P, NDT * LC], F32)
    acc_sb = T("acc_sb", [P, NDT * LC], F32)
    BD = min(4, NDT)
    NB = NDT // BD
    da_sb = T("da_sb", [P, 2 * BD * LC], F32)
    db_sb = T("db_sb", [P, BD * LC], F32)
    h_sb = T("h_sb", [P, BD * LC], F32)
    bb_sb = T("bb_sb", [P, 2 * LC], F32)
    cb_sb = T("cb_sb", [P, 2 * LC], F32)
    hst_sb = T("hst_sb", [P, NDT * N], F32)
    r1t_sb = T("r1t_sb", [P, DH], F32)
    xp_sb = T("xp_sb", [RN, 2 * LC], F32)
    spa_sb = T("spa_sb", [P, max(LC, 16)], F32)

    SLW = NDTF * IG
    slab_sb = T("slab_sb", [P, 2 * SLW], BF16)
    UD = 4 * D
    unionA = T("unionA", [P, UD], F32)
    xq_sb = T("xq_sb", [P, max(D, PW, IGD)], BF16)
    xqT_sb = T("xqT_sb", [P, max(TPG * NDTF * P, PW, 2 * NIT * P)], BF16)
    gu_sb = T("gu_sb", [P, max(4 * IG, D)], F32)
    tr_sb = T("tr_sb", [P, max(2 * IGD, D)], F32)
    tq_sb = xq_sb
    tqT_sb = xqT_sb

    ps_mx = PSUM("ps_mx", [P, max(LC, 512)], F32)
    ps_tpb = PSUM("ps_tpb", [P, 1024], BF16)
    ps_g = PSUM("ps_g", [P, IG], F32)
    ps_u = PSUM("ps_u", [P, IG], F32)
    ps_dn = PSUM("ps_dn", [P, HH], F32)
    ps_tp = ps_mx
    ps_tpF = ps_tpb[:, 0:1024].bitcast(F32)

    mkb = bd.buf
    CONST = mkb("const")
    regs = {}

    def dma(e, dst, src, reads, writes, skey=None):
        sem = bd.dma_sem(e, skey if skey is not None else writes[0].key)
        return bd.op(e, lambda eng: eng.dma_start(out=dst, in_=src),
                     reads=reads, writes=writes, sem=sem, inc=16)

    def vop(emit, reads, writes):
        return bd.op("v", emit, reads, writes, sem="cV", inc=1)

    def sop(emit, reads, writes, inc=1):
        return bd.op("s", emit, reads, writes, sem="cS", inc=inc)

    def top(emit, reads, writes, inc=1):
        return bd.op("t", emit, reads, writes, sem="cT", inc=inc)

    def pop_(emit, reads, writes):
        return bd.op("p", emit, reads, writes, sem="cP", inc=1)

    dma("y", wxT_sb[:, :].rearrange("p (t f) -> p t f", t=NDTF),
        wxT_d.ap().rearrange("(t p) f -> p t f", p=P), [], [CONST])
    dma("y", wdtT_sb[:, :], wdtT_d[:, :], [], [CONST])
    dma("y", bdt_sb[:, :], bdt_d[:, :], [], [CONST])
    dma("y", a_sb[:, :], alog_d[:, :], [], [mkb("A")])
    dma("y", dp1_sb[:, :], dvec_d[:, :], [], [mkb("dvec")])
    dma("y", idf_sb[:, :], idf_d[:, :], [], [CONST])
    dma("y", idb_sb[:, :], idb_d[:, :], [], [CONST])
    dma("y", off_sb[:, :], off_d[:, :], [], [mkb("off")])
    dma("y", cval_sb[:, :], cval_d[:, :], [], [CONST])
    for _cv, _ci in ((0.0, 0), (1.0, 1), (MAGIC, 2), (-MAGIC, 3)):
        nc.const_aps.register(F32, _cv, cval_sb[:, _ci:_ci + 1])
    _csem = bd.dma_sem("y", CONST.key)
    _cm = bd.semn[_csem]
    for _e in Builder.ENGS:
        bd.raw_wait(_e, _csem, _cm)
    sop(lambda e: e.activation(a_sb[:, :], a_sb[:, :], AF.Exp),
        [mkb("A")], [mkb("A")])
    vop(lambda e: e.tensor_scalar_mul(a_sb[:, :], a_sb[:, :], -1.0),
        [mkb("A")], [mkb("A")])
    vop(lambda e: e.tensor_scalar_add(dp1_sb[:, :], dp1_sb[:, :], 1.0),
        [mkb("dvec")], [mkb("dvec")])
    vop(lambda e: e.memset(hst_sb[:, :], 0.0), [],
        [mkb("hst", c) for c in range(NDT * N)])
    vop(lambda e: e.memset(cols_sb[:, 80:88], 0.0), [], [mkb("wps")])
    vop(lambda e: e.memset(sc_sb[0:1, :], 0.0), [], [mkb("sc")])
    vop(lambda e: e.memset(cols_sb[:, 80:81], 1.0), [mkb("wps")],
        [mkb("wps")])
    bd.flush()

    def weight_prep():
        for _ in range(12):
            yield
        shard_specs = [("g", wgT_d, WS_G, I), ("u", wuT_d, WS_G, I),
                       ("d", wdT_d, WS_D, D)]

        def prep_tiles(wd_, rows, cols):
            idx = 0
            for tr in range((rows + P - 1) // P):
                pr = min(P, rows - tr * P)
                for tc in range(0, cols, PW):
                    w = min(PW, cols - tc)
                    yield idx, tr, pr, tc, w
                    idx += 1

        for mi, (m, wd_, rows, cols) in enumerate(shard_specs):
            for idx, tr, pr, tc, w in prep_tiles(wd_, rows, cols):
                slot = idx % 4
                tile = unionA[:pr, slot * PW: slot * PW + w]
                tb = mkb("U", slot)
                dma("y", tile, wd_[tr * P: tr * P + pr, tc: tc + w], [], [tb])
                sop(lambda e, t=tile, sl=slot, pr_=pr: e.activation(
                    t, t, AF.Abs, accum_out=cols_sb[:pr_, 84 + sl: 85 + sl]),
                    [tb], [tb, mkb("wps")])
                vop(lambda e, sl=slot, pr_=pr, mi_=mi: e.tensor_tensor(
                    cols_sb[:pr_, 81 + mi_: 82 + mi_],
                    cols_sb[:pr_, 81 + mi_: 82 + mi_],
                    cols_sb[:pr_, 84 + sl: 85 + sl], ALU.add),
                    [mkb("wps")], [mkb("wps")])
            yield
            top(lambda e, mi_=mi: e.matmul(
                ps_mx[0:1, mi_: mi_ + 1], cols_sb[:, 80:81],
                cols_sb[:, 81 + mi_: 82 + mi_],
                start=True, stop=True),
                [mkb("wps")], [mkb("ps_mx")], inc=1)
            sop(lambda e, mi_=mi: e.copy(sc_sb[0:1, mi_: mi_ + 1],
                                         ps_mx[0:1, mi_: mi_ + 1]),
                [mkb("ps_mx")], [mkb("sc")])
            yield
        yield
        dma("y", red_in[0:1, 0:3], sc_sb[0:1, 0:3], [mkb("sc")], [mkb("red_in")])
        bd.op("p", lambda e: e.collective_compute(
            "AllReduce", ALU.add, replica_groups=[list(range(NCORES))],
            ins=[red_in[0:1, 0:3].opt()], outs=[red_out[0:1, 0:3].opt()]),
            reads=[mkb("red_in")], writes=[mkb("red_out")],
            sem=bd.dma_sem("cc", ("red",)), inc=1)
        dma("y", sc_sb[0:1, 4:7], red_out[0:1, 0:3], [mkb("red_out")],
            [mkb("sc")])
        for i, (m, _, rows_, cols_) in enumerate(shard_specs):
            numel = float(rows_ * NCORES * cols_)
            vop(lambda e, i_=i, s_=1.0 / numel: e.tensor_scalar(
                sc_sb[0:1, 8 + i_: 9 + i_], sc_sb[0:1, 4 + i_: 5 + i_],
                s_, EPS_Q, ALU.mult, ALU.add), [mkb("sc")], [mkb("sc")])
        vop(lambda e: e.reciprocal(sc_sb[0:1, 12:15], sc_sb[0:1, 8:11]),
            [mkb("sc")], [mkb("sc")])
        dma("y", sc_dram[0:1, :], sc_sb[0:1, :], [mkb("sc")], [mkb("scd")])
        dma("y", ws_sb[:, 0:3], sc_dram[0:1, 8:11].partition_broadcast(P),
            [mkb("scd")], [mkb("ws")])
        dma("y", ws_sb[:, 3:6], sc_dram[0:1, 12:15].partition_broadcast(P),
            [mkb("scd")], [mkb("ws")])
        yield
        bd.flush()

        for m, wd_, rows, cols in shard_specs:
            sidx = {"g": 3, "u": 4, "d": 5}[m]
            for idx, tr, pr, tc, w in prep_tiles(wd_, rows, cols):
                slot = idx % 4
                tile = unionA[:pr, slot * PW: slot * PW + w]
                tb = mkb("U", slot)
                dma("y", tile, wd_[tr * P: tr * P + pr, tc: tc + w], [], [tb])
                sop(lambda e, t=tile, pr_=pr, si=sidx: e.activation(
                    t, t, AF.Identity, scale=ws_sb[:pr_, si: si + 1], bias=MAGIC),
                    [tb, mkb("ws")], [tb])
                sop(lambda e, t=tile: e.activation(t, t, AF.Identity, bias=-MAGIC),
                    [tb], [tb])
                obf = (xq_sb if slot % 2 == 0 else xqT_sb)[:pr, 0:w]
                otag = mkb("xqb") if slot % 2 == 0 else mkb("xqT", 0)
                ob = mkb("wpo", slot % 2)
                vop(lambda e, t=tile, o=obf: e.tensor_scalar(
                    o, t, 1.0, -1.0, ALU.min, ALU.max), [tb], [ob, otag])
                dma("y", wq_c[m][tr * P: tr * P + pr, tc: tc + w], obf,
                    [ob, otag], [mkb("wqc", m)])
        for m, _, _, _ in shard_specs:
            bd.op("p", lambda e, m_=m: e.collective_compute(
                "AllGather", ALU.bypass, replica_groups=[list(range(NCORES))],
                ins=[wq_c[m_].ap().opt()], outs=[wq_full[m_].ap().opt()]),
                reads=[mkb("wqc", m)], writes=[mkb("wqf", m)],
                sem=bd.dma_sem("cc", ("wq",)), inc=1)
            yield
        bd.flush()


    NMM_LC = (LC + 511) // 512
    for k in range(NCH):
        for dtf in range(NDTF):
            slot = dtf % 2
            tile = da_sb[:, slot * LC: (slot + 1) * LC]
            tb = mkb("da", slot)
            dma("y", tile, xT_d[dtf * P: (dtf + 1) * P,
                                k * LC: (k + 1) * LC], [], [tb])
            for q in range(NMM_LC):
                c0, c1 = q * 512, min(LC, (q + 1) * 512)
                top(lambda e, t=tile, dt_=dtf, c0_=c0, c1_=c1: e.matmul(
                    ps_mx[0:RN, c0_:c1_],
                    wxT_sb[:, dt_ * RN: (dt_ + 1) * RN], t[:, c0_:c1_],
                    start=(dt_ == 0), stop=(dt_ == NDTF - 1)),
                    [CONST, tb], [mkb("ps_mx")], inc=1)
        slot = k % 2
        xps = xp_sb[0:RN, slot * LC: (slot + 1) * LC]
        xb = mkb("xp", slot)
        sop(lambda e, d=xps: e.copy(d, ps_mx[0:RN, 0:LC]),
            [mkb("ps_mx")], [xb])
        dma("y", xproj_dram[0:RN, k * LC: (k + 1) * LC], xps, [xb],
            [mkb("xpd", k)], skey=("xpd",))
    bd.flush()


    pend_cc = []

    def emit_pend_cc():
        while pend_cc:
            pend_cc.pop(0)()

    def mixer_chunk(k, pull):
        slot = k % 2
        xps = xp_sb[0:RN, slot * LC: (slot + 1) * LC]
        xb = mkb("xp", slot)
        dma("p", xps, xproj_dram[0:RN, k * LC: (k + 1) * LC],
            [mkb("xpd", k)], [xb])
        for dt in range(NDT):
            dst = dt_sb[:, dt * LC: (dt + 1) * LC]
            dtb = mkb("dt", dt)
            for q in range(NMM_LC):
                c0, c1 = q * 512, min(LC, (q + 1) * 512)
                top(lambda e, dt_=dt, c0_=c0, c1_=c1, x_=xps: e.matmul(
                    ps_mx[:, c0_:c1_], wdtT_sb[:, dt_ * P: (dt_ + 1) * P],
                    x_[0:R, c0_:c1_], start=True, stop=True),
                    [CONST, xb], [mkb("ps_mx")],
                    inc=1 if q == NMM_LC - 1 else 0)
            sop(lambda e, d=dst, dt_=dt: e.activation(
                d, ps_mx[:, 0:LC], AF.Identity,
                bias=bdt_sb[:, dt_: dt_ + 1]),
                [mkb("ps_mx"), CONST], [dtb])
            sab = mkb("spl")
            sop(lambda e, d=dst: e.activation(
                spa_sb[:, 0:LC], d, AF.Abs), [dtb, mkb("spa")], [sab])
            sop(lambda e: e.activation(spa_sb[:, 0:LC], spa_sb[:, 0:LC],
                                       AF.Exp, scale=-1.0), [sab], [sab])
            sop(lambda e: e.activation(spa_sb[:, 0:LC], spa_sb[:, 0:LC],
                                       AF.Ln, bias=1.0), [sab], [sab])
            vop(lambda e, d=dst: e.tensor_scalar_max(d, d, 0.0),
                [dtb], [dtb])
            vop(lambda e, d=dst: e.tensor_tensor(
                d, d, spa_sb[:, 0:LC], ALU.add), [dtb, sab], [dtb])
            xts = dt % 3
            dma("p", xt_sb[:, xts * LC: (xts + 1) * LC],
                xT_d[dt * P: (dt + 1) * P, k * LC: (k + 1) * LC],
                [], [mkb("xt", xts)])
            vop(lambda e, dt_=dt, d=dst, s_=xts: e.tensor_tensor(
                u_sb[:, dt_ * LC: (dt_ + 1) * LC], d,
                xt_sb[:, s_ * LC: (s_ + 1) * LC], ALU.mult),
                [dtb, mkb("xt", xts)], [mkb("u", dt)])
        for n in range(N):
            if n == 2:
                emit_pend_cc()
            bslot = n % 2
            bbuf = bb_sb[:, bslot * LC: (bslot + 1) * LC]
            cbuf = cb_sb[:, bslot * LC: (bslot + 1) * LC]
            bB = mkb("bb", bslot)
            bC = mkb("cb", bslot)
            dma("p", bbuf,
                xproj_dram[R + n: R + n + 1,
                           k * LC: (k + 1) * LC].partition_broadcast(P),
                [mkb("xpd", k)], [bB])
            dma("p", cbuf,
                xproj_dram[R + N + n: R + N + n + 1,
                           k * LC: (k + 1) * LC].partition_broadcast(P),
                [mkb("xpd", k)], [bC])
            bbrep = bbuf.unsqueeze(1).broadcast_to([P, BD, LC])
            cbrep = cbuf.unsqueeze(1).broadcast_to([P, BD, LC])
            for hb in range(NB):
                s = (n * NB + hb) % 2
                dav = da_sb[:, s * BD * LC: (s + 1) * BD * LC]
                bda = mkb("da", s)
                dts = list(range(hb * BD, (hb + 1) * BD))
                cols = [dt * N + n for dt in dts]
                for j, dt in enumerate(dts):
                    sop(lambda e, j=j, dt=dt, n=n, dv=dav: e.activation(
                        dv[:, j * LC: (j + 1) * LC],
                        dt_sb[:, dt * LC: (dt + 1) * LC], AF.Exp,
                        scale=a_sb[:, dt * N + n: dt * N + n + 1]),
                        [mkb("dt", dt), mkb("A")], [bda],
                        inc=1 if j == BD - 1 else 0)
                dbb = mkb("db")
                uv = u_sb[:, hb * BD * LC: (hb + 1) * BD * LC]
                vop(lambda e, uv=uv, bp=bbrep: e.tensor_tensor(
                    db_sb[:, 0:BD * LC].rearrange(
                        "p (r f) -> p r f", r=BD),
                    uv.rearrange("p (r f) -> p r f", r=BD),
                    bp, ALU.mult),
                    [mkb("u", dt) for dt in dts] + [bB], [dbb])
                hcols = [mkb("hst", c) for c in cols]
                scr1 = cols_sb[:, 72:72 + BD]
                scr2 = cols_sb[:, 76:76 + BD]
                b1 = mkb("scr1")
                b2_ = mkb("scr2")
                vop(lambda e, dv=dav, s1=scr1: e.tensor_copy(
                    s1, dv[:, 0:BD * LC:LC]), [bda], [b1])
                vop(lambda e, dv=dav: e.tensor_scalar_mul(
                    dv[:, 0:BD * LC:LC], dv[:, 0:BD * LC:LC], 0.0),
                    [bda, b1], [bda])
                hstv = hst_sb[:, hb * BD * N + n:
                              hb * BD * N + n + (BD - 1) * N + 1: N]
                vop(lambda e, hv_=hstv, s1=scr1, s2=scr2: e.tensor_tensor(
                    s2, s1, hv_, ALU.mult), [b1] + hcols, [b2_])
                vop(lambda e, s2=scr2: e.tensor_tensor(
                    db_sb[:, 0:BD * LC:LC], db_sb[:, 0:BD * LC:LC],
                    s2, ALU.add), [b2_, dbb], [dbb])
                hbuf = mkb("h")
                vop(lambda e, dv=dav: e.tensor_tensor_scan(
                    h_sb[:, 0:BD * LC], dv, db_sb[:, 0:BD * LC],
                    cval_sb[:, 0:1], ALU.mult, ALU.add),
                    [bda, dbb, CONST], [hbuf])
                vop(lambda e, hv_=hstv: e.tensor_copy(
                    hv_, h_sb[:, LC - 1:BD * LC:LC]),
                    [hbuf] + hcols, hcols)
                accv = acc_sb[:, hb * BD * LC: (hb + 1) * BD * LC]
                atags = [mkb("acc", dt) for dt in dts]
                if n == 0:
                    vop(lambda e, av=accv, cp=cbrep: e.tensor_tensor(
                        av.rearrange("p (r f) -> p r f", r=BD),
                        h_sb[:, 0:BD * LC].rearrange(
                            "p (r f) -> p r f", r=BD),
                        cp, ALU.mult), [hbuf, bC], atags)
                else:
                    vop(lambda e, cp=cbrep: e.tensor_tensor(
                        db_sb[:, 0:BD * LC].rearrange(
                            "p (r f) -> p r f", r=BD),
                        h_sb[:, 0:BD * LC].rearrange(
                            "p (r f) -> p r f", r=BD),
                        cp, ALU.mult), [hbuf, bC, dbb], [dbb])
                    vop(lambda e, av=accv: e.tensor_tensor(
                        av, av, db_sb[:, 0:BD * LC], ALU.add),
                        [dbb] + atags, atags)
            if n >= 2:
                pull()
        for dt in range(NDT):
            xts = dt % 3
            dma("p", xt_sb[:, xts * LC: (xts + 1) * LC],
                xT_d[dt * P: (dt + 1) * P, k * LC: (k + 1) * LC],
                [], [mkb("xt", xts)])
            vop(lambda e, dt_=dt, s_=xts: e.scalar_tensor_tensor(
                u_sb[:, dt_ * LC: (dt_ + 1) * LC],
                xt_sb[:, s_ * LC: (s_ + 1) * LC],
                dp1_sb[:, dt_: dt_ + 1],
                acc_sb[:, dt_ * LC: (dt_ + 1) * LC], ALU.mult, ALU.add),
                [mkb("xt", xts), mkb("dvec"), mkb("acc", dt)],
                [mkb("u", dt)])
        for lt in range(LTC):
            rview = r1t_sb[:, 0:DH]
            rtb = mkb("r1t", 0)
            for grp in range((NDT + 3) // 4):
                d0, d1 = grp * 4, min(NDT, grp * 4 + 4)
                for dt in range(d0, d1):
                    top(lambda e, dt_=dt, lt_=lt, g0=d0: e.transpose(
                        ps_tpF[:, (dt_ - g0) * P: (dt_ - g0 + 1) * P],
                        u_sb[:, dt_ * LC + lt_ * P:
                             dt_ * LC + (lt_ + 1) * P], idf_sb[:, :]),
                        [mkb("u", dt), CONST], [mkb("ps_tpb")],
                        inc=1 if dt == d1 - 1 else 0)
                sop(lambda e, g0=d0, g1=d1, rv=rview: e.copy(
                    rv[:, g0 * P: g1 * P], ps_tpF[:, 0:(g1 - g0) * P]),
                    [mkb("ps_tpb")], [rtb])
            gate_reads = [rtb] + ([mkb("gath", k - 2)] if k >= 2 else [])
            dma("y", contrib[k * LC + lt * P: k * LC + (lt + 1) * P, :],
                rview, gate_reads, [mkb("contrib", k)],
                skey=("contrib", k % 2))
            pull()

        def emit_cc(k_=k):
            bd.op("p", lambda e: e.collective_compute(
                "AllGather", ALU.bypass,
                replica_groups=[[2 * i, 2 * i + 1]
                                for i in range(NCORES // 2)],
                ins=[contrib[k_ * LC: (k_ + 1) * LC, :].opt()],
                outs=[gathered[2 * k_ * LC: 2 * (k_ + 1) * LC, :].opt()]),
                reads=[mkb("contrib", k_)], writes=[mkb("gath", k_)],
                sem=bd.dma_sem("cc", ("g", k_ // (2 * NRC))), inc=1)
        pend_cc.append(emit_cc)

    def Ub(i):
        return mkb("U", i)

    r1tok = unionA[:, 0:D]
    xlnb = unionA[:, 2 * D:3 * D]
    s2c = lambda j: cols_sb[:, 40 + j: 41 + j]
    sxdc = lambda j: cols_sb[:, 56 + j: 57 + j]

    NMM_IG = (IG + 511) // 512
    NMM_HH = (HH + 511) // 512
    NH1 = max(NIT // 2, 1)

    def gslot(jj):
        return gu_sb[:, (jj % 2) * IG: (jj % 2 + 1) * IG]

    def uslot(jj):
        return gu_sb[:, (2 + jj % 2) * IG: (3 + jj % 2) * IG]

    def ffn_group(g):
        gv = gathered[g * 4 * BBLK: (g + 1) * 4 * BBLK, :].rearrange(
            "(a rc h rr) d -> a rc h rr d", a=2, rc=NRC, h=2)
        dstv = myg[g * 2 * BBLK: (g + 1) * 2 * BBLK, :].rearrange(
            "(rc h rr) d -> rc h rr d", rc=NRC, h=2)

        pieces_done = set()

        def ensure_piece(rc_):
            if rc_ in pieces_done:
                return
            pieces_done.add(rc_)
            base = 2 * g * NRC
            needed = [mkb("gath", base + rc_ + 2 * a_) for a_ in range(2)]
            while any(b.w is None for b in needed):
                yield

            def em(e, gv_=gv, dstv_=dstv, rc__=rc_):
                ro = regs.get("y")
                if ro is None:
                    ro = regs["y"] = e.alloc_register("off_reg")
                e.reg_load(ro, off_sb[0:1, 0:1])
                ofs = e.snap(ro)
                return e.dma_start(out=dstv_[rc__],
                                   in_=gv_[bass.ds(ofs, 1), rc__])
            mgb = mkb("myg", g, rc_)
            bd.op("y", em,
                  reads=[mkb("off")] + needed,
                  writes=[mgb], sem=bd.dma_sem("y", mgb.key), inc=16)
            for _ in range(8):
                yield
        for jj in range(TPG):
            j = g * TPG + jj
            r = jj * P
            rc, rr = r // LC, r % LC
            yield from ensure_piece(rc)
            rb = Ub(0)
            for hh in range(2):
                st = g * 2 * BBLK + rc * 2 * LC + hh * LC + rr
                dma("p", r1tok[:, hh * DH: (hh + 1) * DH],
                    myg[st: st + P, :], [mkb("myg", g, rc)], [rb])
            sumb = mkb("c_sum")
            vop(lambda e: e.reduce_sum(cols_sb[:, 0:1], r1tok, axis=AX),
                [rb], [sumb])
            vop(lambda e: e.tensor_scalar_mul(
                cols_sb[:, 0:1], cols_sb[:, 0:1], -1.0 / D),
                [sumb], [sumb])
            yield
            vb = mkb("c_var")
            sop(lambda e: e.activation(xlnb, r1tok, AF.Square,
                                       accum_out=cols_sb[:, 1:2]),
                [rb], [Ub(2), vb])
            vop(lambda e: e.scalar_tensor_tensor(
                cols_sb[:, 6:7], cols_sb[:, 0:1], -1.0,
                cols_sb[:, 0:1], ALU.mult, ALU.mult),
                [sumb], [mkb("c_msq")])
            yield
            vop(lambda e: e.tensor_scalar(cols_sb[:, 1:2], cols_sb[:, 1:2],
                                          1.0 / D, EPS_LN, ALU.mult,
                                          ALU.add), [vb], [vb])
            vop(lambda e: e.tensor_tensor(cols_sb[:, 1:2], cols_sb[:, 1:2],
                                          cols_sb[:, 6:7], ALU.add),
                [vb, mkb("c_msq")], [vb])
            yield
            sop(lambda e: e.activation(cols_sb[:, 1:2], cols_sb[:, 1:2],
                                       AF.Ln), [vb], [vb])
            sop(lambda e: e.activation(cols_sb[:, 1:2], cols_sb[:, 1:2],
                                       AF.Exp, scale=-0.5), [vb], [vb])
            yield
            b2 = mkb("c_b2")
            vop(lambda e: e.tensor_tensor(cols_sb[:, 2:3], cols_sb[:, 0:1],
                                          cols_sb[:, 1:2], ALU.mult),
                [sumb, vb], [b2])
            xb = Ub(2)
            sop(lambda e: e.activation(xlnb, r1tok, AF.Identity,
                                       scale=cols_sb[:, 1:2],
                                       bias=cols_sb[:, 2:3]),
                [rb, vb, b2], [xb])
            dma("y", xln_spill[j * P: (j + 1) * P, :], xlnb, [xb],
                [mkb("xlnsp", j)], skey=("xlnsp", g % 2, jj))
            yield
            am = mkb("c_amax")
            vop(lambda e: e.reduce_max(cols_sb[:, 3:4], xlnb, axis=AX,
                                       apply_absolute_value=True),
                [xb], [am])
            vop(lambda e: e.tensor_scalar(cols_sb[:, 3:4], cols_sb[:, 3:4],
                                          1.0 / 127.0, EPS_Q, ALU.mult,
                                          ALU.add), [am], [am])
            vop(lambda e: e.reciprocal(cols_sb[:, 4:5], cols_sb[:, 3:4]),
                [am], [mkb("c_sinv")])
            yield
            vop(lambda e, j_=j: e.tensor_tensor(
                cols_sb[:, 8 + 2 * j_: 9 + 2 * j_], cols_sb[:, 3:4],
                ws_sb[:, 0:1], ALU.mult), [am, mkb("ws")], [mkb("sxg", j)])
            vop(lambda e, j_=j: e.tensor_tensor(
                cols_sb[:, 9 + 2 * j_: 10 + 2 * j_], cols_sb[:, 3:4],
                ws_sb[:, 1:2], ALU.mult), [am, mkb("ws")], [mkb("sxu", j)])
            sop(lambda e: e.activation(r1tok, xlnb, AF.Identity,
                                       scale=cols_sb[:, 4:5], bias=MAGIC),
                [xb, mkb("c_sinv")], [Ub(0)])
            qb = mkb("xqb")
            sop(lambda e: e.activation(xq_sb[:, 0:D], r1tok, AF.Identity,
                                       bias=-MAGIC), [Ub(0)], [qb])
            yield
            for grp in range((NDTF + 7) // 8):
                d0, d1 = grp * 8, min(NDTF, grp * 8 + 8)
                for dt in range(d0, d1):
                    top(lambda e, dt_=dt, g0=d0: e.transpose(
                        ps_tpb[:, (dt_ - g0) * P: (dt_ - g0 + 1) * P],
                        xq_sb[:, dt_ * P: (dt_ + 1) * P], idb_sb[:, :]),
                        [qb, CONST], [mkb("ps_tpb")],
                        inc=1 if dt == d1 - 1 else 0)
                vop(lambda e, g0=d0, g1=d1, jj_=jj: e.tensor_copy(
                    xqT_sb[:, (jj_ * NDTF + g0) * P:
                           (jj_ * NDTF + g1) * P],
                    ps_tpb[:, 0:(g1 - g0) * P]),
                    [mkb("ps_tpb")], [mkb("xqT", jj)])
                yield

        def emit_gate_up(ig, jj):
            j = g * TPG + jj
            for half, psum in ((0, ps_g), (1, ps_u)):
                sl = slab_sb[:, half * SLW: (half + 1) * SLW]
                slb = mkb("slab", half)
                pb = mkb("ps_gu", half)
                for dt in range(NDTF):
                    for q in range(NMM_IG):
                        c0, c1 = q * 512, min(IG, (q + 1) * 512)
                        top(lambda e, dt_=dt, c0_=c0, c1_=c1, s_=sl,
                            ps_=psum, jj_=jj: e.matmul(
                                ps_[:, c0_:c1_],
                                xqT_sb[:, (jj_ * NDTF + dt_) * P:
                                       (jj_ * NDTF + dt_ + 1) * P],
                                s_[:, dt_ * IG + c0_: dt_ * IG + c1_],
                                start=(dt_ == 0), stop=(dt_ == NDTF - 1)),
                            [mkb("xqT", jj), slb], [pb],
                            inc=1 if (dt == NDTF - 1 and q == NMM_IG - 1)
                            else 0)
                go = gslot(jj) if half == 0 else uslot(jj)
                gb = mkb("gu", half, jj % 2)
                scol = cols_sb[:, 8 + 2 * j + half: 9 + 2 * j + half]
                sdep = mkb("sxg", j) if half == 0 else mkb("sxu", j)
                fn = AF.Silu if half == 0 else AF.Identity
                sop(lambda e, go_=go, f=fn, c=scol, ps_=psum:
                    e.activation(go_, ps_[:, 0:IG], f, scale=c),
                    [pb, sdep], [gb])

        def emit_t(ig, jj):
            j = g * TPG + jj
            tb = mkb("gu", 0, jj % 2)
            vop(lambda e, jj_=jj: e.tensor_tensor(
                gslot(jj_), gslot(jj_), uslot(jj_), ALU.mult),
                [mkb("gu", 0, jj % 2), mkb("gu", 1, jj % 2)], [tb])
            mxb = mkb("c_tmax")
            vop(lambda e, jj_=jj: e.reduce_max(
                cols_sb[:, 5:6], gslot(jj_), axis=AX,
                apply_absolute_value=True), [tb], [mxb])
            if ig == 0:
                vop(lambda e, j_=j: e.tensor_copy(s2c(j_), cols_sb[:, 5:6]),
                    [mxb], [mkb("s2", j)])
            else:
                vop(lambda e, j_=j: e.tensor_tensor(
                    s2c(j_), s2c(j_), cols_sb[:, 5:6], ALU.max),
                    [mxb, mkb("s2", j)], [mkb("s2", j)])
            dma("y", t_spill[j * P: (j + 1) * P, ig * IG: (ig + 1) * IG],
                gslot(jj), [tb], [mkb("tsp", j)], skey=("tsp", g % 2, jj))

        prev = None
        for ig in range(NIG):
            for half, name in ((0, "g"), (1, "u")):
                sl = slab_sb[:, half * SLW: (half + 1) * SLW]
                dma("y", sl.rearrange("p (t f) -> p t f", t=NDTF),
                    wq_full[name].ap()[0:D,
                                       ig * IG:(ig + 1) * IG].rearrange(
                        "(t p) f -> p t f", p=P),
                    [mkb("wqf", name)], [mkb("slab", half)])
            for jj in range(TPG):
                emit_gate_up(ig, jj)
                if prev is not None:
                    emit_t(*prev)
                prev = (ig, jj)
                yield
        emit_t(*prev)
        for jj in range(TPG):
            j = g * TPG + jj
            vop(lambda e, j_=j: e.tensor_scalar(
                s2c(j_), s2c(j_), 1.0 / 127.0, EPS_Q, ALU.mult, ALU.add),
                [mkb("s2", j)], [mkb("s2", j)])
            vop(lambda e, j_=j: e.tensor_tensor(
                sxdc(j_), s2c(j_), ws_sb[:, 2:3], ALU.mult),
                [mkb("s2", j), mkb("ws")], [mkb("sxd", j)])
            vop(lambda e, j_=j: e.reciprocal(s2c(j_), s2c(j_)),
                [mkb("s2", j)], [mkb("s2", j)])
            yield

        def emit_tq_prep(igd, jj):
            j = g * TPG + jj
            trs = jj % 2
            trv = tr_sb[:, trs * IGD: (trs + 1) * IGD]
            trb = mkb("tr", trs)
            dma("p", trv, t_spill[j * P:(j + 1) * P,
                                  igd * IGD:(igd + 1) * IGD],
                [mkb("tsp", j)], [trb], skey=("trl", g % 2, jj % 2))
            sop(lambda e, t=trv, j_=j: e.activation(
                t, t, AF.Identity, scale=s2c(j_), bias=MAGIC),
                [trb, mkb("s2", j)], [trb])
            qb = mkb("xqb")
            sop(lambda e, t=trv: e.activation(
                tq_sb[:, 0:IGD], t, AF.Identity, bias=-MAGIC),
                [trb], [qb])
            yield
            tqv = tqT_sb[:, (jj % 2) * NIT * P: (jj % 2 + 1) * NIT * P]
            tqtags = [mkb("xqT", r)
                      for r in range(min(TPG, -(-2 * NIT) // NDTF))]
            for grp in range((NIT + 7) // 8):
                d0, d1 = grp * 8, min(NIT, grp * 8 + 8)
                for it in range(d0, d1):
                    top(lambda e, it_=it, g0=d0: e.transpose(
                        ps_tpb[:, (it_ - g0) * P: (it_ - g0 + 1) * P],
                        tq_sb[:, it_ * P: (it_ + 1) * P], idb_sb[:, :]),
                        [qb, CONST], [mkb("ps_tpb")],
                        inc=1 if it == d1 - 1 else 0)
                vop(lambda e, g0=d0, g1=d1, tv=tqv: e.tensor_copy(
                    tv[:, g0 * P: g1 * P], ps_tpb[:, 0:(g1 - g0) * P]),
                    [mkb("ps_tpb")], tqtags)
                yield

        def emit_dn_mms(igd, jj, hhx, pslot):
            tqv = tqT_sb[:, (jj % 2) * NIT * P: (jj % 2 + 1) * NIT * P]
            tqb = mkb("xqT", 0)
            tqtags = [mkb("xqT", r)
                      for r in range(min(TPG, -(-2 * NIT) // NDTF))]
            psum, pbk = ((ps_dn, mkb("ps_dn")) if pslot == 0
                         else (ps_g, mkb("ps_gu", 0)))
            for it in range(NIT):
                hseg = 0 if it < NH1 else 1
                t0 = 0 if hseg == 0 else NH1
                t1 = min(NH1, NIT) if hseg == 0 else NIT
                sl = slab_sb[:, t0 * D: t1 * D]
                for q in range(NMM_HH):
                    c0, c1 = q * 512, min(HH, (q + 1) * 512)
                    top(lambda e, it_=it, c0_=c0, c1_=c1, h_=hhx,
                        t0_=t0, s_=sl, ps_=psum, tv=tqv: e.matmul(
                            ps_[:, c0_:c1_],
                            tv[:, it_ * P: (it_ + 1) * P],
                            s_[:, (it_ - t0_) * D + h_ * HH + c0_:
                               (it_ - t0_) * D + h_ * HH + c1_],
                            start=(it_ == 0), stop=(it_ == NIT - 1)),
                        tqtags + [mkb("slab", hseg)], [pbk],
                        inc=1 if (it == NIT - 1 and q == NMM_HH - 1)
                        else 0)

        def emit_dn_add(igd, jj, hhx, pslot):
            psum, pbk = ((ps_dn, mkb("ps_dn")) if pslot == 0
                         else (ps_g, mkb("ps_gu", 0)))
            dst = unionA[:, jj * D + hhx * HH: jj * D + (hhx + 1) * HH]
            dnb = Ub(jj)
            if igd == 0:
                vop(lambda e, d=dst, ps_=psum: e.tensor_copy(
                    d, ps_[:, 0:HH]), [pbk], [dnb])
            else:
                vop(lambda e, d=dst, ps_=psum: e.tensor_tensor(
                    d, d, ps_[:, 0:HH], ALU.add), [pbk, dnb], [dnb])

        assert NHH == 1 or HH * NHH == D
        prevc = None
        for igd in range(NIGD):
            for hseg, (t0, t1) in enumerate(((0, min(NH1, NIT)),
                                             (NH1, NIT))):
                if t1 <= t0:
                    continue
                sl = slab_sb[:, t0 * D: t1 * D]
                dma("y", sl.rearrange("p (t f) -> p t f", t=t1 - t0),
                    wq_full["d"].ap()[igd * IGD + t0 * P:
                                      igd * IGD + t1 * P, :].rearrange(
                        "(t p) f -> p t f", p=P),
                    [mkb("wqf", "d")], [mkb("slab", hseg)])
            for jj in range(TPG):
                yield from emit_tq_prep(igd, jj)
                for hhx in range(NHH):
                    pslot = (igd * TPG * NHH + jj * NHH + hhx) % 2
                    emit_dn_mms(igd, jj, hhx, pslot)
                    if prevc is not None:
                        emit_dn_add(*prevc)
                    prevc = (igd, jj, hhx, pslot)
                    yield
        emit_dn_add(*prevc)
        yield

        GUB = [mkb("gu", 0, 0), mkb("gu", 0, 1), mkb("gu", 1, 0),
               mkb("gu", 1, 1)]
        for jj in range(TPG):
            j = g * TPG + jj
            xv = gu_sb[:, 0:D]
            xb2 = mkb("xln2")
            dma("p", xv, xln_spill[j * P: (j + 1) * P, :],
                [mkb("xlnsp", j)] + GUB, GUB + [xb2], skey=("xln2",))
            r2b = mkb("tr", 0)
            bd.op("v", lambda e, jj_=jj, j_=j: e.scalar_tensor_tensor(
                tr_sb[:, 0:D], unionA[:, jj_ * D: (jj_ + 1) * D],
                sxdc(j_), xv, ALU.mult, ALU.add),
                reads=[Ub(jj), mkb("sxd", j), xb2, mkb("tr", 1)],
                writes=[r2b, mkb("tr", 1)], sem="cV", inc=1)
            r2v = tr_sb[:, 0:D]
            sumb = mkb("c_sum")
            vop(lambda e: e.reduce_sum(cols_sb[:, 0:1], r2v, axis=AX),
                [r2b], [sumb])
            vop(lambda e: e.tensor_scalar_mul(
                cols_sb[:, 0:1], cols_sb[:, 0:1], -1.0 / D),
                [sumb], [sumb])
            yield
            vb = mkb("c_var")
            sop(lambda e, jj_=jj: e.activation(
                unionA[:, jj_ * D: (jj_ + 1) * D], r2v, AF.Square,
                accum_out=cols_sb[:, 1:2]),
                [r2b, Ub(jj)], [Ub(jj), vb])
            vop(lambda e: e.scalar_tensor_tensor(
                cols_sb[:, 6:7], cols_sb[:, 0:1], -1.0, cols_sb[:, 0:1],
                ALU.mult, ALU.mult), [sumb], [mkb("c_msq")])
            yield
            vop(lambda e: e.tensor_scalar(cols_sb[:, 1:2], cols_sb[:, 1:2],
                                          1.0 / D, EPS_LN, ALU.mult,
                                          ALU.add), [vb], [vb])
            vop(lambda e: e.tensor_tensor(cols_sb[:, 1:2], cols_sb[:, 1:2],
                                          cols_sb[:, 6:7], ALU.add),
                [vb, mkb("c_msq")], [vb])
            yield
            sop(lambda e: e.activation(cols_sb[:, 1:2], cols_sb[:, 1:2],
                                       AF.Ln), [vb], [vb])
            sop(lambda e: e.activation(cols_sb[:, 1:2], cols_sb[:, 1:2],
                                       AF.Exp, scale=-0.5), [vb], [vb])
            yield
            b2 = mkb("c_b2")
            vop(lambda e: e.tensor_tensor(cols_sb[:, 2:3], cols_sb[:, 0:1],
                                          cols_sb[:, 1:2], ALU.mult),
                [sumb, vb], [b2])
            ob = mkb("outp")
            bd.op("s", lambda e: e.activation(
                gu_sb[:, 0:D], tr_sb[:, 0:D], AF.Identity,
                scale=cols_sb[:, 1:2], bias=cols_sb[:, 2:3]),
                reads=[r2b, vb, b2, xb2] + GUB, writes=GUB + [ob],
                sem="cS", inc=1)
            dma("y", out_d[j * P: (j + 1) * P, :], gu_sb[:, 0:D],
                [ob] + GUB, [mkb("outd")])
            yield

    active = [weight_prep()]

    def pull():
        budget = 6 if len(active) > 1 else 4
        while active and budget > 0:
            try:
                next(active[0])
                budget -= 1
            except StopIteration:
                active.pop(0)

    for k in range(NCH):
        mixer_chunk(k, pull)
        if (k + 2) % (2 * NRC) == 0 and (k + 2) // (2 * NRC) <= NG:
            active.append(ffn_group((k + 2) // (2 * NRC) - 1))
    emit_pend_cc()
    while active:
        try:
            next(active[0])
        except StopIteration:
            active.pop(0)

    for name, tot in list(bd.semn.items()):
        if name.startswith("D_") and tot > 0:
            bd.raw_wait("y", name, tot)
    bd.finalize()
    bd.close()
    return nc


_CACHE = {}


def _get_program(key, cfg):
    if key not in _CACHE:
        _CACHE[key] = build_program(cfg)
    return _CACHE[key]


def kernel(L0_attn_x, L0_attn_W_x, L0_attn_W_dt, L0_attn_b_dt, L0_attn_A_log,
           L0_attn_D, L0_ffn_w_gate, L0_ffn_w_up, L0_ffn_w_down):
    x = np.ascontiguousarray(np.asarray(L0_attn_x, np.float32))
    W_x = np.asarray(L0_attn_W_x, np.float32)
    W_dt = np.asarray(L0_attn_W_dt, np.float32)
    b_dt = np.asarray(L0_attn_b_dt, np.float32)
    A_log = np.asarray(L0_attn_A_log, np.float32)
    Dv = np.asarray(L0_attn_D, np.float32)
    wg = np.asarray(L0_ffn_w_gate, np.float32)
    wu = np.asarray(L0_ffn_w_up, np.float32)
    wd = np.asarray(L0_ffn_w_down, np.float32)

    B, L, D = x.shape
    N = A_log.shape[1]
    R = W_dt.shape[1]
    I = wg.shape[0]
    nc = _get_program((L, D, N, R, I), dict(L=L, D=D, N=N, R=R, I=I))

    DH = D // 2
    NDT = DH // P
    WS_G = D // NCORES
    WS_D = I // NCORES
    LC = L // 8
    wgT = np.ascontiguousarray(wg.T)
    wuT = np.ascontiguousarray(wu.T)
    wdT = np.ascontiguousarray(wd.T)
    WxT = np.ascontiguousarray(W_x.T)
    WdtT = np.ascontiguousarray(W_dt.T)
    idf = np.eye(P, dtype=np.float32)
    idb = np.eye(P).astype(ml_dtypes.bfloat16)
    cvals = np.tile(np.array([0.0, 1.0, MAGIC, -MAGIC], np.float32), (P, 1))

    in_maps = []
    for c in range(NCORES):
        b, par = c // 2, c % 2
        my = slice(par * DH, (par + 1) * DH)
        perm = np.r_[np.arange(par * DH, (par + 1) * DH),
                     np.arange((1 - par) * DH, (2 - par) * DH)]
        col = lambda v: np.ascontiguousarray(v[my].reshape(NDT, P).T)
        in_maps.append(dict(
            xT=np.ascontiguousarray(x[b].T[perm]),
            wxT=np.ascontiguousarray(WxT[perm]),
            wdtT=np.ascontiguousarray(WdtT[:, my]),
            bdt=col(b_dt),
            alog=np.ascontiguousarray(
                A_log[my].reshape(NDT, P, N).transpose(1, 0, 2).reshape(
                    P, NDT * N)),
            dvec=col(Dv),
            wgTs=np.ascontiguousarray(wgT[c * WS_G:(c + 1) * WS_G]),
            wuTs=np.ascontiguousarray(wuT[c * WS_G:(c + 1) * WS_G]),
            wdTs=np.ascontiguousarray(wdT[c * WS_D:(c + 1) * WS_D]),
            idf=idf, idb=idb,
            cval=cvals,
            off=np.array([[par]], dtype=np.uint32),
        ))

    res = run_bass_kernel_spmd(nc, in_maps, core_ids=list(range(NCORES)))

    out = np.empty((B, L, D), np.float32)
    LH = L // 2
    for c in range(NCORES):
        b, par = c // 2, c % 2
        oc = np.asarray(res.results[c]["out"]).reshape(LH, D)
        for t in range(LH // LC):
            gl = (2 * t + par) * LC
            out[b, gl:gl + LC] = oc[t * LC:(t + 1) * LC]
    return out


# version 29 (final)
# speedup vs baseline: 1.0335x; 1.0205x over previous
import numpy as np
import ml_dtypes
import concourse.bass as bass
import concourse.mybir as mybir
from concourse.bass_utils import run_bass_kernel_spmd

F32 = mybir.dt.float32
BF16 = mybir.dt.bfloat16
U32 = mybir.dt.uint32
AF = mybir.ActivationFunctionType
ALU = mybir.AluOpType

P = 128
MAGIC = float(1.5 * 2 ** 23)
EPS_Q = 1e-05
EPS_LN = 1e-05
NCORES = 8
AX = mybir.AxisListType.X


class Buf:
    __slots__ = ("w", "readers", "key")

    def __init__(self, key=None):
        self.w = None
        self.readers = []
        self.key = key


class Builder:

    ENGS = ("t", "v", "s", "p", "y")
    SECT = {"t": "tensor", "v": "vector", "s": "scalar", "p": "gpsimd",
            "y": "sync"}

    def __init__(self, nc):
        self.nc = nc
        self.stack = []
        self.sem = {}
        self.semn = {}
        for name in ("cT", "cV", "cS", "cP", "dS", "dF", "dG", "cc"):
            self.sem[name] = self.enter(nc.semaphore(name))
            self.semn[name] = 0
        self.known = {e: {} for e in self.ENGS}
        self.phase_ops = {e: [] for e in self.ENGS}
        self.bufs = {}
        self.nblocks = 0
        self.pending_reads = {e: [] for e in self.ENGS}

    def enter(self, cm):
        v = cm.__enter__()
        self.stack.append(cm)
        return v

    def close(self):
        for cm in reversed(self.stack):
            cm.__exit__(None, None, None)
        self.stack = []

    def buf(self, *key):
        b = self.bufs.get(key)
        if b is None:
            b = self.bufs[key] = Buf(key)
        return b

    def dma_sem(self, queue, key):
        name = "D_" + queue + "_" + "_".join(str(k) for k in key)
        if name not in self.sem:
            self.sem[name] = self.enter(self.nc.semaphore(name))
            self.semn[name] = 0
        return name

    def op(self, e, emit, reads=(), writes=(), sem=None, inc=0):
        waits = {}

        def add(s, v):
            if s.startswith("D_"):
                v = self.semn[s]
            if waits.get(s, -1) < v:
                waits[s] = v

        for b in reads:
            if b.w is not None:
                add(*b.w)
        for b in writes:
            deps = list(b.readers)
            if b.w is not None:
                deps.append(b.w)
            for s, v in deps:
                add(s, v)
        known = self.known[e]
        fw = {}
        for s, v in waits.items():
            if known.get(s, 0) < v:
                fw[s] = v
                known[s] = v
        mile = None
        if sem is not None and inc:
            self.semn[sem] += inc
            mile = (sem, self.semn[sem])
        if mile is not None:
            for b in writes:
                b.w = mile
                b.readers = []
            for b in reads:
                b.readers.append(mile)
            for b in self.pending_reads[e]:
                b.readers.append(mile)
            self.pending_reads[e] = []
        else:
            self.pending_reads[e].extend(reads)
        self.phase_ops[e].append((fw, emit, sem if mile else None,
                                  inc if mile else 0))
        return mile

    def raw_wait(self, e, sem, val):
        known = self.known[e]
        if known.get(sem, 0) >= val:
            return
        known[sem] = val
        self.phase_ops[e].append(({sem: val}, None, None, 0))

    def flush(self):
        pass

    def finalize(self):
        nc = self.nc
        pending = {e: ops for e, ops in self.phase_ops.items() if ops}
        self.phase_ops = {e: [] for e in self.ENGS}
        if not pending:
            return
        block = nc.Block("main")
        block.__enter__()

        def runner(ops):
            def body(eng):
                for waits, emit, sem, inc in ops:
                    for s, v in waits.items():
                        eng.wait_ge(self.sem[s], v)
                    if emit is not None:
                        instr = emit(eng)
                        if sem is not None:
                            instr.then_inc(self.sem[sem], inc)
            return body

        for e, ops in pending.items():
            getattr(block, self.SECT[e])(runner(ops))
        block.__exit__(None, None, None)


def build_program(cfg):
    L, D, N, R, I = cfg["L"], cfg["D"], cfg["N"], cfg["R"], cfg["I"]
    NCH = 16
    LC = L // NCH
    DH = D // 2
    NDT = DH // P
    NDTF = D // P
    LTC = LC // P
    TOKT = (L // 2) // P
    NG = 4
    TPG = TOKT // NG
    IG = min(1024, I)
    NIG = I // IG
    IGD = min(1024, I)
    NIGD = I // IGD
    NIT = IGD // P
    HH = min(1024, D)
    NHH = D // HH
    RN = R + 2 * N
    BBLK = 0
    WS_G = D // NCORES
    WS_D = I // NCORES
    PW = min(2048, max(D, 512))
    BBLK = TPG * P
    NRC = BBLK // LC

    nc = bass.Bass("TRN2", target_bir_lowering=False)

    xT_d = nc.dram_tensor("xT", [D, L], F32, kind="ExternalInput")
    wxT_d = nc.dram_tensor("wxT", [D, RN], F32, kind="ExternalInput")
    wdtT_d = nc.dram_tensor("wdtT", [R, DH], F32, kind="ExternalInput")
    bdt_d = nc.dram_tensor("bdt", [P, NDT], F32, kind="ExternalInput")
    alog_d = nc.dram_tensor("alog", [P, NDT * N], F32, kind="ExternalInput")
    dvec_d = nc.dram_tensor("dvec", [P, NDT], F32, kind="ExternalInput")
    wgT_d = nc.dram_tensor("wgTs", [WS_G, I], F32, kind="ExternalInput")
    wuT_d = nc.dram_tensor("wuTs", [WS_G, I], F32, kind="ExternalInput")
    wdT_d = nc.dram_tensor("wdTs", [WS_D, D], F32, kind="ExternalInput")
    idf_d = nc.dram_tensor("idf", [P, P], F32, kind="ExternalInput")
    idb_d = nc.dram_tensor("idb", [P, P], BF16, kind="ExternalInput")
    off_d = nc.dram_tensor("off", [1, 1], U32, kind="ExternalInput")
    cval_d = nc.dram_tensor("cval", [P, 4], F32, kind="ExternalInput")
    out_d = nc.dram_tensor("out", [L // 2, D], F32, kind="ExternalOutput")

    xproj_dram = nc.dram_tensor("xproj_dram", [RN, L], F32)
    contrib = nc.dram_tensor("contrib", [L, DH], F32)
    gathered = nc.dram_tensor("gathered", [2 * L, DH], F32)
    wq_c = {"g": nc.dram_tensor("wqc_g", [WS_G, I], BF16),
            "u": nc.dram_tensor("wqc_u", [WS_G, I], BF16),
            "d": nc.dram_tensor("wqc_d", [WS_D, D], BF16)}
    wq_full = {"g": nc.dram_tensor("wqf_g", [D, I], BF16, addr_space="Shared"),
               "u": nc.dram_tensor("wqf_u", [D, I], BF16, addr_space="Shared"),
               "d": nc.dram_tensor("wqf_d", [I, D], BF16, addr_space="Shared")}
    red_in = nc.dram_tensor("red_in", [1, 4], F32)
    red_out = nc.dram_tensor("red_out", [1, 4], F32, addr_space="Shared")
    sc_dram = nc.dram_tensor("sc_dram", [1, 16], F32)
    myg = nc.dram_tensor("myg", [L, DH], F32)
    xln_spill = nc.dram_tensor("xln_spill", [L // 2, D], F32)
    t_spill = nc.dram_tensor("t_spill", [L // 2, I], F32)

    bd = Builder(nc)
    T = lambda *a: bd.enter(nc.sbuf_tensor(*a))
    PSUM = lambda *a: bd.enter(nc.psum_tensor(*a))

    wxT_sb = T("wxT_sb", [P, NDTF * RN], F32)
    wdtT_sb = T("wdtT_sb", [R, DH], F32)
    bdt_sb = T("bdt_sb", [P, NDT], F32)
    a_sb = T("a_sb", [P, NDT * N], F32)
    dp1_sb = T("dp1_sb", [P, NDT], F32)
    idf_sb = T("idf_sb", [P, P], F32)
    idb_sb = T("idb_sb", [P, P], BF16)
    off_sb = T("off_sb", [1, 1], U32)
    ws_sb = T("ws_sb", [P, 8], F32)
    cval_sb = T("cval_sb", [P, 4], F32)
    sc_sb = T("sc_sb", [1, 16], F32)
    cols_sb = T("cols_sb", [P, 88], F32)

    xt_sb = T("xt_sb", [P, 3 * LC], F32)
    dt_sb = T("dt_sb", [P, NDT * LC], F32)
    u_sb = T("u_sb", [P, NDT * LC], F32)
    acc_sb = T("acc_sb", [P, NDT * LC], F32)
    BD = min(4, NDT)
    NB = NDT // BD
    da_sb = T("da_sb", [P, 2 * BD * LC], F32)
    db_sb = T("db_sb", [P, BD * LC], F32)
    h_sb = T("h_sb", [P, BD * LC], F32)
    bb_sb = T("bb_sb", [P, 2 * LC], F32)
    cb_sb = T("cb_sb", [P, 2 * LC], F32)
    hst_sb = T("hst_sb", [P, NDT * N], F32)
    r1t_sb = T("r1t_sb", [P, DH], F32)
    xp_sb = T("xp_sb", [RN, 2 * LC], F32)
    spa_sb = T("spa_sb", [P, max(LC, 16)], F32)

    SLW = NDTF * IG
    slab_sb = T("slab_sb", [P, 2 * SLW], BF16)
    UD = 4 * D
    unionA = T("unionA", [P, UD], F32)
    xq_sb = T("xq_sb", [P, max(D, PW, IGD)], BF16)
    xqT_sb = T("xqT_sb", [P, max(TPG * NDTF * P, PW, 2 * NIT * P)], BF16)
    gu_sb = T("gu_sb", [P, max(4 * IG, D)], F32)
    tr_sb = T("tr_sb", [P, max(2 * IGD, D)], F32)
    tq_sb = xq_sb
    tqT_sb = xqT_sb

    ps_mx = PSUM("ps_mx", [P, max(LC, 512)], F32)
    ps_tpb = PSUM("ps_tpb", [P, 1024], BF16)
    ps_g = PSUM("ps_g", [P, IG], F32)
    ps_u = PSUM("ps_u", [P, IG], F32)
    ps_dn = PSUM("ps_dn", [P, HH], F32)
    ps_tp = ps_mx
    ps_tpF = ps_tpb[:, 0:1024].bitcast(F32)

    mkb = bd.buf
    CONST = mkb("const")
    regs = {}

    def dma(e, dst, src, reads, writes, skey=None):
        sem = bd.dma_sem(e, skey if skey is not None else writes[0].key)
        return bd.op(e, lambda eng: eng.dma_start(out=dst, in_=src),
                     reads=reads, writes=writes, sem=sem, inc=16)

    def vop(emit, reads, writes):
        return bd.op("v", emit, reads, writes, sem="cV", inc=1)

    def sop(emit, reads, writes, inc=1):
        return bd.op("s", emit, reads, writes, sem="cS", inc=inc)

    def top(emit, reads, writes, inc=1):
        return bd.op("t", emit, reads, writes, sem="cT", inc=inc)

    def pop_(emit, reads, writes):
        return bd.op("p", emit, reads, writes, sem="cP", inc=1)

    dma("y", wxT_sb[:, :].rearrange("p (t f) -> p t f", t=NDTF),
        wxT_d.ap().rearrange("(t p) f -> p t f", p=P), [], [CONST])
    dma("y", wdtT_sb[:, :], wdtT_d[:, :], [], [CONST])
    dma("y", bdt_sb[:, :], bdt_d[:, :], [], [CONST])
    dma("y", a_sb[:, :], alog_d[:, :], [], [mkb("A")])
    dma("y", dp1_sb[:, :], dvec_d[:, :], [], [mkb("dvec")])
    dma("y", idf_sb[:, :], idf_d[:, :], [], [CONST])
    dma("y", idb_sb[:, :], idb_d[:, :], [], [CONST])
    dma("y", off_sb[:, :], off_d[:, :], [], [mkb("off")])
    dma("y", cval_sb[:, :], cval_d[:, :], [], [CONST])
    for _cv, _ci in ((0.0, 0), (1.0, 1), (MAGIC, 2), (-MAGIC, 3)):
        nc.const_aps.register(F32, _cv, cval_sb[:, _ci:_ci + 1])
    _csem = bd.dma_sem("y", CONST.key)
    _cm = bd.semn[_csem]
    for _e in Builder.ENGS:
        bd.raw_wait(_e, _csem, _cm)
    sop(lambda e: e.activation(a_sb[:, :], a_sb[:, :], AF.Exp),
        [mkb("A")], [mkb("A")])
    vop(lambda e: e.tensor_scalar_mul(a_sb[:, :], a_sb[:, :], -1.0),
        [mkb("A")], [mkb("A")])
    vop(lambda e: e.tensor_scalar_add(dp1_sb[:, :], dp1_sb[:, :], 1.0),
        [mkb("dvec")], [mkb("dvec")])
    vop(lambda e: e.memset(hst_sb[:, :], 0.0), [],
        [mkb("hst", c) for c in range(NDT * N)])
    vop(lambda e: e.memset(cols_sb[:, 80:88], 0.0), [], [mkb("wps")])
    vop(lambda e: e.memset(sc_sb[0:1, :], 0.0), [], [mkb("sc")])
    vop(lambda e: e.memset(cols_sb[:, 80:81], 1.0), [mkb("wps")],
        [mkb("wps")])
    bd.flush()

    def weight_prep():
        for _ in range(12):
            yield
        shard_specs = [("g", wgT_d, WS_G, I), ("u", wuT_d, WS_G, I),
                       ("d", wdT_d, WS_D, D)]

        def prep_tiles(wd_, rows, cols):
            idx = 0
            for tr in range((rows + P - 1) // P):
                pr = min(P, rows - tr * P)
                for tc in range(0, cols, PW):
                    w = min(PW, cols - tc)
                    yield idx, tr, pr, tc, w
                    idx += 1

        for mi, (m, wd_, rows, cols) in enumerate(shard_specs):
            for idx, tr, pr, tc, w in prep_tiles(wd_, rows, cols):
                slot = idx % 4
                tile = unionA[:pr, slot * PW: slot * PW + w]
                tb = mkb("U", slot)
                dma("y", tile, wd_[tr * P: tr * P + pr, tc: tc + w], [], [tb])
                sop(lambda e, t=tile, sl=slot, pr_=pr: e.activation(
                    t, t, AF.Abs, accum_out=cols_sb[:pr_, 84 + sl: 85 + sl]),
                    [tb], [tb, mkb("wps")])
                vop(lambda e, sl=slot, pr_=pr, mi_=mi: e.tensor_tensor(
                    cols_sb[:pr_, 81 + mi_: 82 + mi_],
                    cols_sb[:pr_, 81 + mi_: 82 + mi_],
                    cols_sb[:pr_, 84 + sl: 85 + sl], ALU.add),
                    [mkb("wps")], [mkb("wps")])
            yield
            top(lambda e, mi_=mi: e.matmul(
                ps_mx[0:1, mi_: mi_ + 1], cols_sb[:, 80:81],
                cols_sb[:, 81 + mi_: 82 + mi_],
                start=True, stop=True),
                [mkb("wps")], [mkb("ps_mx")], inc=1)
            sop(lambda e, mi_=mi: e.copy(sc_sb[0:1, mi_: mi_ + 1],
                                         ps_mx[0:1, mi_: mi_ + 1]),
                [mkb("ps_mx")], [mkb("sc")])
            yield
        yield
        dma("y", red_in[0:1, 0:3], sc_sb[0:1, 0:3], [mkb("sc")], [mkb("red_in")])
        bd.op("p", lambda e: e.collective_compute(
            "AllReduce", ALU.add, replica_groups=[list(range(NCORES))],
            ins=[red_in[0:1, 0:3].opt()], outs=[red_out[0:1, 0:3].opt()]),
            reads=[mkb("red_in")], writes=[mkb("red_out")],
            sem=bd.dma_sem("cc", ("red",)), inc=1)
        dma("y", sc_sb[0:1, 4:7], red_out[0:1, 0:3], [mkb("red_out")],
            [mkb("sc")])
        for i, (m, _, rows_, cols_) in enumerate(shard_specs):
            numel = float(rows_ * NCORES * cols_)
            vop(lambda e, i_=i, s_=1.0 / numel: e.tensor_scalar(
                sc_sb[0:1, 8 + i_: 9 + i_], sc_sb[0:1, 4 + i_: 5 + i_],
                s_, EPS_Q, ALU.mult, ALU.add), [mkb("sc")], [mkb("sc")])
        vop(lambda e: e.reciprocal(sc_sb[0:1, 12:15], sc_sb[0:1, 8:11]),
            [mkb("sc")], [mkb("sc")])
        dma("y", sc_dram[0:1, :], sc_sb[0:1, :], [mkb("sc")], [mkb("scd")])
        dma("y", ws_sb[:, 0:3], sc_dram[0:1, 8:11].partition_broadcast(P),
            [mkb("scd")], [mkb("ws")])
        dma("y", ws_sb[:, 3:6], sc_dram[0:1, 12:15].partition_broadcast(P),
            [mkb("scd")], [mkb("ws")])
        yield
        bd.flush()

        for m, wd_, rows, cols in shard_specs:
            sidx = {"g": 3, "u": 4, "d": 5}[m]
            for idx, tr, pr, tc, w in prep_tiles(wd_, rows, cols):
                slot = idx % 4
                tile = unionA[:pr, slot * PW: slot * PW + w]
                tb = mkb("U", slot)
                dma("y", tile, wd_[tr * P: tr * P + pr, tc: tc + w], [], [tb])
                sop(lambda e, t=tile, pr_=pr, si=sidx: e.activation(
                    t, t, AF.Identity, scale=ws_sb[:pr_, si: si + 1], bias=MAGIC),
                    [tb, mkb("ws")], [tb])
                sop(lambda e, t=tile: e.activation(t, t, AF.Identity, bias=-MAGIC),
                    [tb], [tb])
                obf = (xq_sb if slot % 2 == 0 else xqT_sb)[:pr, 0:w]
                otag = mkb("xqb") if slot % 2 == 0 else mkb("xqT", 0)
                ob = mkb("wpo", slot % 2)
                vop(lambda e, t=tile, o=obf: e.tensor_scalar(
                    o, t, 1.0, -1.0, ALU.min, ALU.max), [tb], [ob, otag])
                dma("y", wq_c[m][tr * P: tr * P + pr, tc: tc + w], obf,
                    [ob, otag], [mkb("wqc", m)])
        for m, _, _, _ in shard_specs:
            bd.op("p", lambda e, m_=m: e.collective_compute(
                "AllGather", ALU.bypass, replica_groups=[list(range(NCORES))],
                ins=[wq_c[m_].ap().opt()], outs=[wq_full[m_].ap().opt()]),
                reads=[mkb("wqc", m)], writes=[mkb("wqf", m)],
                sem=bd.dma_sem("cc", ("wq",)), inc=1)
            yield
        bd.flush()


    NMM_LC = (LC + 511) // 512
    for k in range(NCH):
        for dtf in range(NDTF):
            slot = dtf % 2
            tile = da_sb[:, slot * LC: (slot + 1) * LC]
            tb = mkb("da", slot)
            dma("y", tile, xT_d[dtf * P: (dtf + 1) * P,
                                k * LC: (k + 1) * LC], [], [tb])
            for q in range(NMM_LC):
                c0, c1 = q * 512, min(LC, (q + 1) * 512)
                top(lambda e, t=tile, dt_=dtf, c0_=c0, c1_=c1: e.matmul(
                    ps_mx[0:RN, c0_:c1_],
                    wxT_sb[:, dt_ * RN: (dt_ + 1) * RN], t[:, c0_:c1_],
                    start=(dt_ == 0), stop=(dt_ == NDTF - 1)),
                    [CONST, tb], [mkb("ps_mx")], inc=1)
        slot = k % 2
        xps = xp_sb[0:RN, slot * LC: (slot + 1) * LC]
        xb = mkb("xp", slot)
        sop(lambda e, d=xps: e.copy(d, ps_mx[0:RN, 0:LC]),
            [mkb("ps_mx")], [xb])
        dma("y", xproj_dram[0:RN, k * LC: (k + 1) * LC], xps, [xb],
            [mkb("xpd", k)], skey=("xpd",))
    bd.flush()


    pend_cc = []

    def emit_pend_cc():
        while pend_cc:
            pend_cc.pop(0)()

    def mixer_chunk(k, pull):
        slot = k % 2
        xps = xp_sb[0:RN, slot * LC: (slot + 1) * LC]
        xb = mkb("xp", slot)
        dma("p", xps, xproj_dram[0:RN, k * LC: (k + 1) * LC],
            [mkb("xpd", k)], [xb])
        for dt in range(NDT):
            dst = dt_sb[:, dt * LC: (dt + 1) * LC]
            dtb = mkb("dt", dt)
            for q in range(NMM_LC):
                c0, c1 = q * 512, min(LC, (q + 1) * 512)
                top(lambda e, dt_=dt, c0_=c0, c1_=c1, x_=xps: e.matmul(
                    ps_mx[:, c0_:c1_], wdtT_sb[:, dt_ * P: (dt_ + 1) * P],
                    x_[0:R, c0_:c1_], start=True, stop=True),
                    [CONST, xb], [mkb("ps_mx")],
                    inc=1 if q == NMM_LC - 1 else 0)
            sop(lambda e, d=dst, dt_=dt: e.activation(
                d, ps_mx[:, 0:LC], AF.Identity,
                bias=bdt_sb[:, dt_: dt_ + 1]),
                [mkb("ps_mx"), CONST], [dtb])
            sab = mkb("spl")
            sop(lambda e, d=dst: e.activation(
                spa_sb[:, 0:LC], d, AF.Abs), [dtb, mkb("spa")], [sab])
            sop(lambda e: e.activation(spa_sb[:, 0:LC], spa_sb[:, 0:LC],
                                       AF.Exp, scale=-1.0), [sab], [sab])
            sop(lambda e: e.activation(spa_sb[:, 0:LC], spa_sb[:, 0:LC],
                                       AF.Ln, bias=1.0), [sab], [sab])
            vop(lambda e, d=dst: e.tensor_scalar_max(d, d, 0.0),
                [dtb], [dtb])
            vop(lambda e, d=dst: e.tensor_tensor(
                d, d, spa_sb[:, 0:LC], ALU.add), [dtb, sab], [dtb])
            xts = dt % 3
            dma("p", xt_sb[:, xts * LC: (xts + 1) * LC],
                xT_d[dt * P: (dt + 1) * P, k * LC: (k + 1) * LC],
                [], [mkb("xt", xts)])
            vop(lambda e, dt_=dt, d=dst, s_=xts: e.tensor_tensor(
                u_sb[:, dt_ * LC: (dt_ + 1) * LC], d,
                xt_sb[:, s_ * LC: (s_ + 1) * LC], ALU.mult),
                [dtb, mkb("xt", xts)], [mkb("u", dt)])
        for n in range(N):
            if n == 2:
                emit_pend_cc()
            bslot = n % 2
            bbuf = bb_sb[:, bslot * LC: (bslot + 1) * LC]
            cbuf = cb_sb[:, bslot * LC: (bslot + 1) * LC]
            bB = mkb("bb", bslot)
            bC = mkb("cb", bslot)
            dma("p", bbuf,
                xproj_dram[R + n: R + n + 1,
                           k * LC: (k + 1) * LC].partition_broadcast(P),
                [mkb("xpd", k)], [bB])
            dma("p", cbuf,
                xproj_dram[R + N + n: R + N + n + 1,
                           k * LC: (k + 1) * LC].partition_broadcast(P),
                [mkb("xpd", k)], [bC])
            bbrep = bbuf.unsqueeze(1).broadcast_to([P, BD, LC])
            cbrep = cbuf.unsqueeze(1).broadcast_to([P, BD, LC])
            for hb in range(NB):
                s = (n * NB + hb) % 2
                dav = da_sb[:, s * BD * LC: (s + 1) * BD * LC]
                bda = mkb("da", s)
                dts = list(range(hb * BD, (hb + 1) * BD))
                cols = [dt * N + n for dt in dts]
                for j, dt in enumerate(dts):
                    sop(lambda e, j=j, dt=dt, n=n, dv=dav: e.activation(
                        dv[:, j * LC: (j + 1) * LC],
                        dt_sb[:, dt * LC: (dt + 1) * LC], AF.Exp,
                        scale=a_sb[:, dt * N + n: dt * N + n + 1]),
                        [mkb("dt", dt), mkb("A")], [bda],
                        inc=1 if j == BD - 1 else 0)
                dbb = mkb("db")
                uv = u_sb[:, hb * BD * LC: (hb + 1) * BD * LC]
                vop(lambda e, uv=uv, bp=bbrep: e.tensor_tensor(
                    db_sb[:, 0:BD * LC].rearrange(
                        "p (r f) -> p r f", r=BD),
                    uv.rearrange("p (r f) -> p r f", r=BD),
                    bp, ALU.mult),
                    [mkb("u", dt) for dt in dts] + [bB], [dbb])
                hcols = [mkb("hst", c) for c in cols]
                scr1 = cols_sb[:, 72:72 + BD]
                scr2 = cols_sb[:, 76:76 + BD]
                b1 = mkb("scr1")
                b2_ = mkb("scr2")
                sop(lambda e, dv=dav, s1=scr1: e.activation(
                    s1, dv[:, 0:BD * LC:LC], AF.Identity), [bda], [b1])
                sop(lambda e, dv=dav: e.activation(
                    dv[:, 0:BD * LC:LC], dv[:, 0:BD * LC:LC], AF.Identity,
                    scale=0.0), [bda, b1], [bda])
                hstv = hst_sb[:, hb * BD * N + n:
                              hb * BD * N + n + (BD - 1) * N + 1: N]
                vop(lambda e, hv_=hstv, s1=scr1, s2=scr2: e.tensor_tensor(
                    s2, s1, hv_, ALU.mult), [b1] + hcols, [b2_])
                vop(lambda e, s2=scr2: e.tensor_tensor(
                    db_sb[:, 0:BD * LC:LC], db_sb[:, 0:BD * LC:LC],
                    s2, ALU.add), [b2_, dbb], [dbb])
                hbuf = mkb("h")
                vop(lambda e, dv=dav: e.tensor_tensor_scan(
                    h_sb[:, 0:BD * LC], dv, db_sb[:, 0:BD * LC],
                    cval_sb[:, 0:1], ALU.mult, ALU.add),
                    [bda, dbb, CONST], [hbuf])
                vop(lambda e, hv_=hstv: e.tensor_copy(
                    hv_, h_sb[:, LC - 1:BD * LC:LC]),
                    [hbuf] + hcols, hcols)
                accv = acc_sb[:, hb * BD * LC: (hb + 1) * BD * LC]
                atags = [mkb("acc", dt) for dt in dts]
                if n == 0:
                    vop(lambda e, av=accv, cp=cbrep: e.tensor_tensor(
                        av.rearrange("p (r f) -> p r f", r=BD),
                        h_sb[:, 0:BD * LC].rearrange(
                            "p (r f) -> p r f", r=BD),
                        cp, ALU.mult), [hbuf, bC], atags)
                else:
                    vop(lambda e, cp=cbrep: e.tensor_tensor(
                        db_sb[:, 0:BD * LC].rearrange(
                            "p (r f) -> p r f", r=BD),
                        h_sb[:, 0:BD * LC].rearrange(
                            "p (r f) -> p r f", r=BD),
                        cp, ALU.mult), [hbuf, bC, dbb], [dbb])
                    vop(lambda e, av=accv: e.tensor_tensor(
                        av, av, db_sb[:, 0:BD * LC], ALU.add),
                        [dbb] + atags, atags)
            if n >= 2:
                pull()
        for dt in range(NDT):
            xts = dt % 3
            dma("p", xt_sb[:, xts * LC: (xts + 1) * LC],
                xT_d[dt * P: (dt + 1) * P, k * LC: (k + 1) * LC],
                [], [mkb("xt", xts)])
            vop(lambda e, dt_=dt, s_=xts: e.scalar_tensor_tensor(
                u_sb[:, dt_ * LC: (dt_ + 1) * LC],
                xt_sb[:, s_ * LC: (s_ + 1) * LC],
                dp1_sb[:, dt_: dt_ + 1],
                acc_sb[:, dt_ * LC: (dt_ + 1) * LC], ALU.mult, ALU.add),
                [mkb("xt", xts), mkb("dvec"), mkb("acc", dt)],
                [mkb("u", dt)])
        for lt in range(LTC):
            rview = r1t_sb[:, 0:DH]
            rtb = mkb("r1t", 0)
            for grp in range((NDT + 3) // 4):
                d0, d1 = grp * 4, min(NDT, grp * 4 + 4)
                for dt in range(d0, d1):
                    top(lambda e, dt_=dt, lt_=lt, g0=d0: e.transpose(
                        ps_tpF[:, (dt_ - g0) * P: (dt_ - g0 + 1) * P],
                        u_sb[:, dt_ * LC + lt_ * P:
                             dt_ * LC + (lt_ + 1) * P], idf_sb[:, :]),
                        [mkb("u", dt), CONST], [mkb("ps_tpb")],
                        inc=1 if dt == d1 - 1 else 0)
                sop(lambda e, g0=d0, g1=d1, rv=rview: e.copy(
                    rv[:, g0 * P: g1 * P], ps_tpF[:, 0:(g1 - g0) * P]),
                    [mkb("ps_tpb")], [rtb])
            gate_reads = [rtb] + ([mkb("gath", k - 2)] if k >= 2 else [])
            dma("y", contrib[k * LC + lt * P: k * LC + (lt + 1) * P, :],
                rview, gate_reads, [mkb("contrib", k)],
                skey=("contrib", k % 2))
            pull()

        def emit_cc(k_=k):
            bd.op("p", lambda e: e.collective_compute(
                "AllGather", ALU.bypass,
                replica_groups=[[2 * i, 2 * i + 1]
                                for i in range(NCORES // 2)],
                ins=[contrib[k_ * LC: (k_ + 1) * LC, :].opt()],
                outs=[gathered[2 * k_ * LC: 2 * (k_ + 1) * LC, :].opt()]),
                reads=[mkb("contrib", k_)], writes=[mkb("gath", k_)],
                sem=bd.dma_sem("cc", ("g", k_ // (2 * NRC))), inc=1)
        pend_cc.append(emit_cc)

    def Ub(i):
        return mkb("U", i)

    r1tok = unionA[:, 0:D]
    xlnb = unionA[:, 2 * D:3 * D]
    s2c = lambda j: cols_sb[:, 40 + j: 41 + j]
    sxdc = lambda j: cols_sb[:, 56 + j: 57 + j]

    NMM_IG = (IG + 511) // 512
    NMM_HH = (HH + 511) // 512
    NH1 = max(NIT // 2, 1)

    def gslot(jj):
        return gu_sb[:, (jj % 2) * IG: (jj % 2 + 1) * IG]

    def uslot(jj):
        return gu_sb[:, (2 + jj % 2) * IG: (3 + jj % 2) * IG]

    def ffn_group(g):
        gv = gathered[g * 4 * BBLK: (g + 1) * 4 * BBLK, :].rearrange(
            "(a rc h rr) d -> a rc h rr d", a=2, rc=NRC, h=2)
        dstv = myg[g * 2 * BBLK: (g + 1) * 2 * BBLK, :].rearrange(
            "(rc h rr) d -> rc h rr d", rc=NRC, h=2)

        pieces_done = set()

        def ensure_piece(rc_):
            if rc_ in pieces_done:
                return
            pieces_done.add(rc_)
            base = 2 * g * NRC
            needed = [mkb("gath", base + rc_ + 2 * a_) for a_ in range(2)]
            while any(b.w is None for b in needed):
                yield

            def em(e, gv_=gv, dstv_=dstv, rc__=rc_):
                ro = regs.get("y")
                if ro is None:
                    ro = regs["y"] = e.alloc_register("off_reg")
                e.reg_load(ro, off_sb[0:1, 0:1])
                ofs = e.snap(ro)
                return e.dma_start(out=dstv_[rc__],
                                   in_=gv_[bass.ds(ofs, 1), rc__])
            mgb = mkb("myg", g, rc_)
            bd.op("y", em,
                  reads=[mkb("off")] + needed,
                  writes=[mgb], sem=bd.dma_sem("y", mgb.key), inc=16)
            for _ in range(8):
                yield
        for jj in range(TPG):
            j = g * TPG + jj
            r = jj * P
            rc, rr = r // LC, r % LC
            yield from ensure_piece(rc)
            rb = Ub(0)
            for hh in range(2):
                st = g * 2 * BBLK + rc * 2 * LC + hh * LC + rr
                dma("p", r1tok[:, hh * DH: (hh + 1) * DH],
                    myg[st: st + P, :], [mkb("myg", g, rc)], [rb])
            sumb = mkb("c_sum")
            vop(lambda e: e.reduce_sum(cols_sb[:, 0:1], r1tok, axis=AX),
                [rb], [sumb])
            vop(lambda e: e.tensor_scalar_mul(
                cols_sb[:, 0:1], cols_sb[:, 0:1], -1.0 / D),
                [sumb], [sumb])
            yield
            vb = mkb("c_var")
            sop(lambda e: e.activation(xlnb, r1tok, AF.Square,
                                       accum_out=cols_sb[:, 1:2]),
                [rb], [Ub(2), vb])
            vop(lambda e: e.scalar_tensor_tensor(
                cols_sb[:, 6:7], cols_sb[:, 0:1], -1.0,
                cols_sb[:, 0:1], ALU.mult, ALU.mult),
                [sumb], [mkb("c_msq")])
            yield
            vop(lambda e: e.tensor_scalar(cols_sb[:, 1:2], cols_sb[:, 1:2],
                                          1.0 / D, EPS_LN, ALU.mult,
                                          ALU.add), [vb], [vb])
            vop(lambda e: e.tensor_tensor(cols_sb[:, 1:2], cols_sb[:, 1:2],
                                          cols_sb[:, 6:7], ALU.add),
                [vb, mkb("c_msq")], [vb])
            yield
            sop(lambda e: e.activation(cols_sb[:, 1:2], cols_sb[:, 1:2],
                                       AF.Ln), [vb], [vb])
            sop(lambda e: e.activation(cols_sb[:, 1:2], cols_sb[:, 1:2],
                                       AF.Exp, scale=-0.5), [vb], [vb])
            yield
            b2 = mkb("c_b2")
            vop(lambda e: e.tensor_tensor(cols_sb[:, 2:3], cols_sb[:, 0:1],
                                          cols_sb[:, 1:2], ALU.mult),
                [sumb, vb], [b2])
            xb = Ub(2)
            sop(lambda e: e.activation(xlnb, r1tok, AF.Identity,
                                       scale=cols_sb[:, 1:2],
                                       bias=cols_sb[:, 2:3]),
                [rb, vb, b2], [xb])
            dma("y", xln_spill[j * P: (j + 1) * P, :], xlnb, [xb],
                [mkb("xlnsp", j)], skey=("xlnsp", g % 2, jj))
            yield
            am = mkb("c_amax")
            vop(lambda e: e.reduce_max(cols_sb[:, 3:4], xlnb, axis=AX,
                                       apply_absolute_value=True),
                [xb], [am])
            vop(lambda e: e.tensor_scalar(cols_sb[:, 3:4], cols_sb[:, 3:4],
                                          1.0 / 127.0, EPS_Q, ALU.mult,
                                          ALU.add), [am], [am])
            vop(lambda e: e.reciprocal(cols_sb[:, 4:5], cols_sb[:, 3:4]),
                [am], [mkb("c_sinv")])
            yield
            vop(lambda e, j_=j: e.tensor_tensor(
                cols_sb[:, 8 + 2 * j_: 9 + 2 * j_], cols_sb[:, 3:4],
                ws_sb[:, 0:1], ALU.mult), [am, mkb("ws")], [mkb("sxg", j)])
            vop(lambda e, j_=j: e.tensor_tensor(
                cols_sb[:, 9 + 2 * j_: 10 + 2 * j_], cols_sb[:, 3:4],
                ws_sb[:, 1:2], ALU.mult), [am, mkb("ws")], [mkb("sxu", j)])
            sop(lambda e: e.activation(r1tok, xlnb, AF.Identity,
                                       scale=cols_sb[:, 4:5], bias=MAGIC),
                [xb, mkb("c_sinv")], [Ub(0)])
            qb = mkb("xqb")
            sop(lambda e: e.activation(xq_sb[:, 0:D], r1tok, AF.Identity,
                                       bias=-MAGIC), [Ub(0)], [qb])
            yield
            for grp in range((NDTF + 7) // 8):
                d0, d1 = grp * 8, min(NDTF, grp * 8 + 8)
                for dt in range(d0, d1):
                    top(lambda e, dt_=dt, g0=d0: e.transpose(
                        ps_tpb[:, (dt_ - g0) * P: (dt_ - g0 + 1) * P],
                        xq_sb[:, dt_ * P: (dt_ + 1) * P], idb_sb[:, :]),
                        [qb, CONST], [mkb("ps_tpb")],
                        inc=1 if dt == d1 - 1 else 0)
                vop(lambda e, g0=d0, g1=d1, jj_=jj: e.tensor_copy(
                    xqT_sb[:, (jj_ * NDTF + g0) * P:
                           (jj_ * NDTF + g1) * P],
                    ps_tpb[:, 0:(g1 - g0) * P]),
                    [mkb("ps_tpb")], [mkb("xqT", jj)])
                yield

        def emit_gate_up(ig, jj):
            j = g * TPG + jj
            for half, psum in ((0, ps_g), (1, ps_u)):
                sl = slab_sb[:, half * SLW: (half + 1) * SLW]
                slb = mkb("slab", half)
                pb = mkb("ps_gu", half)
                for dt in range(NDTF):
                    for q in range(NMM_IG):
                        c0, c1 = q * 512, min(IG, (q + 1) * 512)
                        top(lambda e, dt_=dt, c0_=c0, c1_=c1, s_=sl,
                            ps_=psum, jj_=jj: e.matmul(
                                ps_[:, c0_:c1_],
                                xqT_sb[:, (jj_ * NDTF + dt_) * P:
                                       (jj_ * NDTF + dt_ + 1) * P],
                                s_[:, dt_ * IG + c0_: dt_ * IG + c1_],
                                start=(dt_ == 0), stop=(dt_ == NDTF - 1)),
                            [mkb("xqT", jj), slb], [pb],
                            inc=1 if (dt == NDTF - 1 and q == NMM_IG - 1)
                            else 0)
                go = gslot(jj) if half == 0 else uslot(jj)
                gb = mkb("gu", half, jj % 2)
                scol = cols_sb[:, 8 + 2 * j + half: 9 + 2 * j + half]
                sdep = mkb("sxg", j) if half == 0 else mkb("sxu", j)
                fn = AF.Silu if half == 0 else AF.Identity
                sop(lambda e, go_=go, f=fn, c=scol, ps_=psum:
                    e.activation(go_, ps_[:, 0:IG], f, scale=c),
                    [pb, sdep], [gb])

        def emit_t(ig, jj):
            j = g * TPG + jj
            tb = mkb("gu", 0, jj % 2)
            vop(lambda e, jj_=jj: e.tensor_tensor(
                gslot(jj_), gslot(jj_), uslot(jj_), ALU.mult),
                [mkb("gu", 0, jj % 2), mkb("gu", 1, jj % 2)], [tb])
            mxb = mkb("c_tmax")
            vop(lambda e, jj_=jj: e.reduce_max(
                cols_sb[:, 5:6], gslot(jj_), axis=AX,
                apply_absolute_value=True), [tb], [mxb])
            if ig == 0:
                vop(lambda e, j_=j: e.tensor_copy(s2c(j_), cols_sb[:, 5:6]),
                    [mxb], [mkb("s2", j)])
            else:
                vop(lambda e, j_=j: e.tensor_tensor(
                    s2c(j_), s2c(j_), cols_sb[:, 5:6], ALU.max),
                    [mxb, mkb("s2", j)], [mkb("s2", j)])
            dma("y", t_spill[j * P: (j + 1) * P, ig * IG: (ig + 1) * IG],
                gslot(jj), [tb], [mkb("tsp", j)], skey=("tsp", g % 2, jj))

        prev = None
        for ig in range(NIG):
            for half, name in ((0, "g"), (1, "u")):
                sl = slab_sb[:, half * SLW: (half + 1) * SLW]
                dma("y", sl.rearrange("p (t f) -> p t f", t=NDTF),
                    wq_full[name].ap()[0:D,
                                       ig * IG:(ig + 1) * IG].rearrange(
                        "(t p) f -> p t f", p=P),
                    [mkb("wqf", name)], [mkb("slab", half)])
            for jj in range(TPG):
                emit_gate_up(ig, jj)
                if prev is not None:
                    emit_t(*prev)
                prev = (ig, jj)
                yield
        emit_t(*prev)
        for jj in range(TPG):
            j = g * TPG + jj
            vop(lambda e, j_=j: e.tensor_scalar(
                s2c(j_), s2c(j_), 1.0 / 127.0, EPS_Q, ALU.mult, ALU.add),
                [mkb("s2", j)], [mkb("s2", j)])
            vop(lambda e, j_=j: e.tensor_tensor(
                sxdc(j_), s2c(j_), ws_sb[:, 2:3], ALU.mult),
                [mkb("s2", j), mkb("ws")], [mkb("sxd", j)])
            vop(lambda e, j_=j: e.reciprocal(s2c(j_), s2c(j_)),
                [mkb("s2", j)], [mkb("s2", j)])
            yield

        def emit_tq_prep(igd, jj):
            j = g * TPG + jj
            trs = jj % 2
            trv = tr_sb[:, trs * IGD: (trs + 1) * IGD]
            trb = mkb("tr", trs)
            dma("p", trv, t_spill[j * P:(j + 1) * P,
                                  igd * IGD:(igd + 1) * IGD],
                [mkb("tsp", j)], [trb], skey=("trl", g % 2, jj % 2))
            sop(lambda e, t=trv, j_=j: e.activation(
                t, t, AF.Identity, scale=s2c(j_), bias=MAGIC),
                [trb, mkb("s2", j)], [trb])
            qb = mkb("xqb")
            sop(lambda e, t=trv: e.activation(
                tq_sb[:, 0:IGD], t, AF.Identity, bias=-MAGIC),
                [trb], [qb])
            yield
            tqv = tqT_sb[:, (jj % 2) * NIT * P: (jj % 2 + 1) * NIT * P]
            tqtags = [mkb("xqT", r)
                      for r in range(min(TPG, -(-2 * NIT) // NDTF))]
            for grp in range((NIT + 7) // 8):
                d0, d1 = grp * 8, min(NIT, grp * 8 + 8)
                for it in range(d0, d1):
                    top(lambda e, it_=it, g0=d0: e.transpose(
                        ps_tpb[:, (it_ - g0) * P: (it_ - g0 + 1) * P],
                        tq_sb[:, it_ * P: (it_ + 1) * P], idb_sb[:, :]),
                        [qb, CONST], [mkb("ps_tpb")],
                        inc=1 if it == d1 - 1 else 0)
                vop(lambda e, g0=d0, g1=d1, tv=tqv: e.tensor_copy(
                    tv[:, g0 * P: g1 * P], ps_tpb[:, 0:(g1 - g0) * P]),
                    [mkb("ps_tpb")], tqtags)
                yield

        def emit_dn_mms(igd, jj, hhx, pslot):
            tqv = tqT_sb[:, (jj % 2) * NIT * P: (jj % 2 + 1) * NIT * P]
            tqb = mkb("xqT", 0)
            tqtags = [mkb("xqT", r)
                      for r in range(min(TPG, -(-2 * NIT) // NDTF))]
            psum, pbk = ((ps_dn, mkb("ps_dn")) if pslot == 0
                         else (ps_g, mkb("ps_gu", 0)))
            for it in range(NIT):
                hseg = 0 if it < NH1 else 1
                t0 = 0 if hseg == 0 else NH1
                t1 = min(NH1, NIT) if hseg == 0 else NIT
                sl = slab_sb[:, t0 * D: t1 * D]
                for q in range(NMM_HH):
                    c0, c1 = q * 512, min(HH, (q + 1) * 512)
                    top(lambda e, it_=it, c0_=c0, c1_=c1, h_=hhx,
                        t0_=t0, s_=sl, ps_=psum, tv=tqv: e.matmul(
                            ps_[:, c0_:c1_],
                            tv[:, it_ * P: (it_ + 1) * P],
                            s_[:, (it_ - t0_) * D + h_ * HH + c0_:
                               (it_ - t0_) * D + h_ * HH + c1_],
                            start=(it_ == 0), stop=(it_ == NIT - 1)),
                        tqtags + [mkb("slab", hseg)], [pbk],
                        inc=1 if (it == NIT - 1 and q == NMM_HH - 1)
                        else 0)

        def emit_dn_add(igd, jj, hhx, pslot):
            psum, pbk = ((ps_dn, mkb("ps_dn")) if pslot == 0
                         else (ps_g, mkb("ps_gu", 0)))
            dst = unionA[:, jj * D + hhx * HH: jj * D + (hhx + 1) * HH]
            dnb = Ub(jj)
            if igd == 0:
                vop(lambda e, d=dst, ps_=psum: e.tensor_copy(
                    d, ps_[:, 0:HH]), [pbk], [dnb])
            else:
                vop(lambda e, d=dst, ps_=psum: e.tensor_tensor(
                    d, d, ps_[:, 0:HH], ALU.add), [pbk, dnb], [dnb])

        assert NHH == 1 or HH * NHH == D
        prevc = None
        for igd in range(NIGD):
            for hseg, (t0, t1) in enumerate(((0, min(NH1, NIT)),
                                             (NH1, NIT))):
                if t1 <= t0:
                    continue
                sl = slab_sb[:, t0 * D: t1 * D]
                dma("y", sl.rearrange("p (t f) -> p t f", t=t1 - t0),
                    wq_full["d"].ap()[igd * IGD + t0 * P:
                                      igd * IGD + t1 * P, :].rearrange(
                        "(t p) f -> p t f", p=P),
                    [mkb("wqf", "d")], [mkb("slab", hseg)])
            for jj in range(TPG):
                yield from emit_tq_prep(igd, jj)
                for hhx in range(NHH):
                    pslot = (igd * TPG * NHH + jj * NHH + hhx) % 2
                    emit_dn_mms(igd, jj, hhx, pslot)
                    if prevc is not None:
                        emit_dn_add(*prevc)
                    prevc = (igd, jj, hhx, pslot)
                    yield
        emit_dn_add(*prevc)
        yield

        GUB = [mkb("gu", 0, 0), mkb("gu", 0, 1), mkb("gu", 1, 0),
               mkb("gu", 1, 1)]
        for jj in range(TPG):
            j = g * TPG + jj
            xv = gu_sb[:, 0:D]
            xb2 = mkb("xln2")
            dma("p", xv, xln_spill[j * P: (j + 1) * P, :],
                [mkb("xlnsp", j)] + GUB, GUB + [xb2], skey=("xln2",))
            r2b = mkb("tr", 0)
            bd.op("v", lambda e, jj_=jj, j_=j: e.scalar_tensor_tensor(
                tr_sb[:, 0:D], unionA[:, jj_ * D: (jj_ + 1) * D],
                sxdc(j_), xv, ALU.mult, ALU.add),
                reads=[Ub(jj), mkb("sxd", j), xb2, mkb("tr", 1)],
                writes=[r2b, mkb("tr", 1)], sem="cV", inc=1)
            r2v = tr_sb[:, 0:D]
            sumb = mkb("c_sum")
            vop(lambda e: e.reduce_sum(cols_sb[:, 0:1], r2v, axis=AX),
                [r2b], [sumb])
            vop(lambda e: e.tensor_scalar_mul(
                cols_sb[:, 0:1], cols_sb[:, 0:1], -1.0 / D),
                [sumb], [sumb])
            yield
            vb = mkb("c_var")
            sop(lambda e, jj_=jj: e.activation(
                unionA[:, jj_ * D: (jj_ + 1) * D], r2v, AF.Square,
                accum_out=cols_sb[:, 1:2]),
                [r2b, Ub(jj)], [Ub(jj), vb])
            vop(lambda e: e.scalar_tensor_tensor(
                cols_sb[:, 6:7], cols_sb[:, 0:1], -1.0, cols_sb[:, 0:1],
                ALU.mult, ALU.mult), [sumb], [mkb("c_msq")])
            yield
            vop(lambda e: e.tensor_scalar(cols_sb[:, 1:2], cols_sb[:, 1:2],
                                          1.0 / D, EPS_LN, ALU.mult,
                                          ALU.add), [vb], [vb])
            vop(lambda e: e.tensor_tensor(cols_sb[:, 1:2], cols_sb[:, 1:2],
                                          cols_sb[:, 6:7], ALU.add),
                [vb, mkb("c_msq")], [vb])
            yield
            sop(lambda e: e.activation(cols_sb[:, 1:2], cols_sb[:, 1:2],
                                       AF.Ln), [vb], [vb])
            sop(lambda e: e.activation(cols_sb[:, 1:2], cols_sb[:, 1:2],
                                       AF.Exp, scale=-0.5), [vb], [vb])
            yield
            b2 = mkb("c_b2")
            vop(lambda e: e.tensor_tensor(cols_sb[:, 2:3], cols_sb[:, 0:1],
                                          cols_sb[:, 1:2], ALU.mult),
                [sumb, vb], [b2])
            ob = mkb("outp")
            bd.op("s", lambda e: e.activation(
                gu_sb[:, 0:D], tr_sb[:, 0:D], AF.Identity,
                scale=cols_sb[:, 1:2], bias=cols_sb[:, 2:3]),
                reads=[r2b, vb, b2, xb2] + GUB, writes=GUB + [ob],
                sem="cS", inc=1)
            dma("y", out_d[j * P: (j + 1) * P, :], gu_sb[:, 0:D],
                [ob] + GUB, [mkb("outd")])
            yield

    active = [weight_prep()]

    def pull():
        budget = 6 if len(active) > 1 else 4
        while active and budget > 0:
            try:
                next(active[0])
                budget -= 1
            except StopIteration:
                active.pop(0)

    for k in range(NCH):
        mixer_chunk(k, pull)
        if (k + 2) % (2 * NRC) == 0 and (k + 2) // (2 * NRC) <= NG:
            active.append(ffn_group((k + 2) // (2 * NRC) - 1))
    emit_pend_cc()
    while active:
        try:
            next(active[0])
        except StopIteration:
            active.pop(0)

    for name, tot in list(bd.semn.items()):
        if name.startswith("D_") and tot > 0:
            bd.raw_wait("y", name, tot)
    bd.finalize()
    bd.close()
    return nc


_CACHE = {}


def _get_program(key, cfg):
    if key not in _CACHE:
        _CACHE[key] = build_program(cfg)
    return _CACHE[key]


def kernel(L0_attn_x, L0_attn_W_x, L0_attn_W_dt, L0_attn_b_dt, L0_attn_A_log,
           L0_attn_D, L0_ffn_w_gate, L0_ffn_w_up, L0_ffn_w_down):
    x = np.ascontiguousarray(np.asarray(L0_attn_x, np.float32))
    W_x = np.asarray(L0_attn_W_x, np.float32)
    W_dt = np.asarray(L0_attn_W_dt, np.float32)
    b_dt = np.asarray(L0_attn_b_dt, np.float32)
    A_log = np.asarray(L0_attn_A_log, np.float32)
    Dv = np.asarray(L0_attn_D, np.float32)
    wg = np.asarray(L0_ffn_w_gate, np.float32)
    wu = np.asarray(L0_ffn_w_up, np.float32)
    wd = np.asarray(L0_ffn_w_down, np.float32)

    B, L, D = x.shape
    N = A_log.shape[1]
    R = W_dt.shape[1]
    I = wg.shape[0]
    nc = _get_program((L, D, N, R, I), dict(L=L, D=D, N=N, R=R, I=I))

    DH = D // 2
    NDT = DH // P
    WS_G = D // NCORES
    WS_D = I // NCORES
    LC = L // 8
    wgT = np.ascontiguousarray(wg.T)
    wuT = np.ascontiguousarray(wu.T)
    wdT = np.ascontiguousarray(wd.T)
    WxT = np.ascontiguousarray(W_x.T)
    WdtT = np.ascontiguousarray(W_dt.T)
    idf = np.eye(P, dtype=np.float32)
    idb = np.eye(P).astype(ml_dtypes.bfloat16)
    cvals = np.tile(np.array([0.0, 1.0, MAGIC, -MAGIC], np.float32), (P, 1))

    in_maps = []
    for c in range(NCORES):
        b, par = c // 2, c % 2
        my = slice(par * DH, (par + 1) * DH)
        perm = np.r_[np.arange(par * DH, (par + 1) * DH),
                     np.arange((1 - par) * DH, (2 - par) * DH)]
        col = lambda v: np.ascontiguousarray(v[my].reshape(NDT, P).T)
        in_maps.append(dict(
            xT=np.ascontiguousarray(x[b].T[perm]),
            wxT=np.ascontiguousarray(WxT[perm]),
            wdtT=np.ascontiguousarray(WdtT[:, my]),
            bdt=col(b_dt),
            alog=np.ascontiguousarray(
                A_log[my].reshape(NDT, P, N).transpose(1, 0, 2).reshape(
                    P, NDT * N)),
            dvec=col(Dv),
            wgTs=np.ascontiguousarray(wgT[c * WS_G:(c + 1) * WS_G]),
            wuTs=np.ascontiguousarray(wuT[c * WS_G:(c + 1) * WS_G]),
            wdTs=np.ascontiguousarray(wdT[c * WS_D:(c + 1) * WS_D]),
            idf=idf, idb=idb,
            cval=cvals,
            off=np.array([[par]], dtype=np.uint32),
        ))

    res = run_bass_kernel_spmd(nc, in_maps, core_ids=list(range(NCORES)))

    out = np.empty((B, L, D), np.float32)
    LH = L // 2
    for c in range(NCORES):
        b, par = c // 2, c % 2
        oc = np.asarray(res.results[c]["out"]).reshape(LH, D)
        for t in range(LH // LC):
            gl = (2 * t + par) * LC
            out[b, gl:gl + LC] = oc[t * LC:(t + 1) * LC]
    return out
